# version 04
# baseline (speedup 1.0000x reference)
import contextlib
import numpy as np

import concourse.bass as bass
import concourse.mybir as mybir
from concourse.bass_utils import run_bass_kernel_spmd

F32 = mybir.dt.float32
BF16 = mybir.dt.bfloat16
AF = mybir.ActivationFunctionType
ALU = mybir.AluOpType
AX = mybir.AxisListType

P = 128
D = 1024


QUEUES = ("sync", "tensor", "vector", "scalar", "gpsimd")
MERGE_KNOW = False
STEP = {"cmp": 1, "dma": 16, "cc": 1}
NSLOT = {"dma": 8, "cc": 2}


class Op:
    __slots__ = ("queue", "kind", "fn", "deps", "qidx", "need_inc", "tag",
                 "know", "pe_group", "_waits", "slot", "cnt", "cmp_idx", "incs")

    def __init__(self, queue, kind, fn, tag, pe_group):
        self.queue = queue
        self.kind = kind
        self.fn = fn
        self.deps = set()
        self.qidx = -1
        self.need_inc = kind != "cmp"
        self.tag = tag
        self.know = None
        self.pe_group = pe_group
        self._waits = ()
        self.slot = None
        self.cnt = 0
        self.cmp_idx = -1
        self.incs = 1

    def semkey(self):
        if self.kind == "cmp":
            return ("cmp", self.queue)
        return (self.kind, self.queue, self.slot)


class Prog:
    def __init__(self):
        self.ops = []
        self.queues = {q: [] for q in QUEUES}
        self.last_writer = {}
        self.readers = {}
        self._slotrr = {}
        self._slotlast = {}
        self._cmpidx = {q: 0 for q in QUEUES}
        self.last_real = {}

    def op(self, queue, fn, reads=(), writes=(), kind="cmp", tag="",
           pe_group=None, incs=1, extra_deps=()):
        o = Op(queue, kind, fn, tag, pe_group)
        o.incs = incs
        o.qidx = len(self.queues[queue])
        deps = set()
        for r in reads:
            w = self.last_writer.get(r)
            if w is not None:
                deps.add(w)
        for w in writes:
            lw = self.last_writer.get(w)
            if lw is not None:
                deps.add(lw)
            for rd in self.readers.get(w, ()):
                deps.add(rd)
        for d in extra_deps:
            deps.add(d)
        deps.discard(o)
        if pe_group is not None:
            deps = {d for d in deps
                    if not (d.queue == "tensor" and d.pe_group == pe_group)}
        if kind == "cmp":
            o.cmp_idx = self._cmpidx[queue]
            self._cmpidx[queue] += 1
        else:
            rr = self._slotrr.get((queue, kind), 0)
            o.slot = rr % NSLOT[kind]
            self._slotrr[(queue, kind)] = rr + 1
            skey = (queue, kind, o.slot)
            prev = self._slotlast.get(skey)
            if prev is not None:
                deps.add(prev)
                o.cnt = prev.cnt + 1
            else:
                o.cnt = 1
            self._slotlast[skey] = o
        o.deps = deps
        for r in reads:
            self.readers.setdefault(r, []).append(o)
        for w in writes:
            self.last_writer[w] = o
            self.readers[w] = [o]
        self.ops.append(o)
        self.queues[queue].append(o)
        if not tag.startswith("bar"):
            self.last_real[queue] = o
        return o

    @staticmethod
    def _need(d):
        return d.cnt if d.kind != "cmp" else None

    def resolve(self):
        know = {q: {} for q in QUEUES}
        for o in self.ops:
            k = know[o.queue]
            waits = []
            for d in sorted(o.deps, key=lambda d: (d.queue, d.kind or "", -(d.qidx))):
                key = d.semkey()
                need = d.cnt if d.kind != "cmp" else d.cmp_idx + 1
                if k.get(key, 0) >= need:
                    continue
                waits.append(d)
                d.need_inc = True
                if MERGE_KNOW and d.know:
                    for sq, c in d.know.items():
                        if k.get(sq, 0) < c:
                            k[sq] = c
                if k.get(key, 0) < need:
                    k[key] = need
            o._waits = waits
            o.know = dict(k)
            okey = o.semkey()
            oneed = o.cnt if o.kind != "cmp" else o.cmp_idx + 1
            o.know[okey] = max(o.know.get(okey, 0), oneed)
        self.wait_val = {}
        for q in QUEUES:
            c = 0
            for o in self.queues[q]:
                if o.kind == "cmp" and o.need_inc:
                    c += o.incs
                    self.wait_val[o] = c

    def emit_queue(self, q, eng, sems):
        for o in self.queues[q]:
            merged = {}
            for d in o._waits:
                if d.kind == "cmp":
                    val = self.wait_val[d]
                else:
                    val = d.cnt * STEP[d.kind]
                key = d.semkey()
                merged[key] = max(merged.get(key, 0), val)
            for key, val in sorted(merged.items()):
                eng.wait_ge(sems[key], val)
            inst = o.fn(eng)
            if o.need_inc:
                inst.then_inc(sems[o.semkey()], STEP[o.kind])



LN_EPS = 1e-5


class Builder:
    def __init__(self, Tc=2048, n_cores=8,
                 do_a=True, do_f1=True, do_b=True, do_f2=True):
        self.Tc = Tc
        self.NT = Tc // P
        self.NC = n_cores
        self.do = dict(a=do_a, f1=do_f1, b=do_b, f2=do_f2)
        self.nc = bass.Bass("TRN2", target_bir_lowering=False)
        self.pr = Prog()
        self.ctx = contextlib.ExitStack()
        self._n = 0
        self._tps = 0
        self.hostc = {}
        self.inputs = []

    def nm(self, p):
        self._n += 1
        return f"{p}{self._n}"

    def sb(self, name, shape, dtype):
        return self.ctx.enter_context(
            self.nc.sbuf_tensor(name, list(shape), dtype, side="right"))

    def psum(self, shape, dtype=F32, name=None):
        if name is None:
            name = self.nm("ps")
        return self.ctx.enter_context(self.nc.psum_tensor(name, list(shape), dtype))

    def const(self, name, arr):
        arr = np.ascontiguousarray(arr, np.float32)
        self.hostc[name] = arr
        return self.nc.dram_tensor(name, list(arr.shape), F32, kind="ExternalInput").ap()

    def inp(self, name, shape):
        self.inputs.append(name)
        return self.nc.dram_tensor(name, list(shape), F32, kind="ExternalInput").ap()

    def begin_phase(self):
        if not hasattr(self, "phases"):
            self.phases = []
        self.phases.append(contextlib.ExitStack())

    def sbP(self, name, shape, dtype):
        return self.phases[-1].enter_context(
            self.nc.sbuf_tensor(name, list(shape), dtype))

    def sbx(self, name, shape, dtype):
        if getattr(self, "phases", None):
            return self.sbP(name, shape, dtype)
        return self.sb(name, shape, dtype)

    def barrier(self):
        targets = list(self.pr.last_real.values()) + \
            list(self.pr._slotlast.values())
        for q in QUEUES:
            self.pr.op(q, lambda e: e.nop(), reads=[], writes=[],
                       tag="bar", extra_deps=targets)

    def end_phase(self):
        self.barrier()
        self.phases.pop().close()
        if hasattr(self, "_qscr"):
            del self._qscr

    def fconst(self, val):
        if not hasattr(self, "_fc"):
            self._fc = {}
        if val not in self._fc:
            t = self.sb(self.nm("fc"), [P, 1], F32)
            self.ve(lambda e, t=t, v=val: e.memset(t[:], v),
                    reads=[], writes=[t.name], tag="fconst")
            self._fc[val] = t
        return self._fc[val]

    def dma(self, dst, src, reads, writes, q="sync", tag="dma"):
        self.pr.op(q, lambda e, d=dst, s=src: e.dma_start(out=d, in_=s),
                   reads=reads, writes=writes, kind="dma", tag=tag)

    def ve(self, fn, reads, writes, tag="ve"):
        self.pr.op("vector", fn, reads=reads, writes=writes, tag=tag)

    def se(self, fn, reads, writes, tag="se", incs=1):
        self.pr.op("scalar", fn, reads=reads, writes=writes, tag=tag, incs=incs)

    def pe(self, fn, reads, writes, tag="pe", group=None):
        self.pr.op("tensor", fn, reads=reads, writes=writes, tag=tag, pe_group=group)

    def gp(self, fn, reads, writes, kind="cmp", tag="gp"):
        self.pr.op("gpsimd", fn, reads=reads, writes=writes, kind=kind, tag=tag)

    def build(self):
        nc, NT, Tc = self.nc, self.NT, self.Tc

        self.x_hbm = self.inp("x_in", [Tc, D])
        self.xh_hbm = self.inp("x_halo", [3, D])
        self.prevmask_hbm = self.inp("prevmask", [self.NC, 1])
        self.prevsel_hbm = self.inp("prevsel", [self.NC, 1])

        wnames = {
            "aw_norm": [D], "aw_inproj": [4384, D], "aw_convw": [2304, 4],
            "aw_convb": [2304], "aw_Alog": [32], "aw_Dp": [32], "aw_dtb": [32],
            "aw_onorm": [2048], "aw_outproj": [D, 2048],
            "bw_q": [D, D], "bw_k": [D, D], "bw_v": [D, 2048], "bw_a": [D, 16],
            "bw_b": [D, 16], "bw_g": [D, 2048], "bw_out": [2048, D],
            "bw_Alog": [16], "bw_dtb": [16], "bw_convq": [D, 4], "bw_convk": [D, 4],
            "bw_convv": [2048, 4], "bw_norm": [128],
            "fw_r": [D, D], "fw_k": [D, D], "fw_v": [D, D],
            "fw_mur": [D], "fw_muk": [D],
        }
        self.W = {k: self.inp(k, v) for k, v in wnames.items()}
        self.out_hbm = nc.dram_tensor("x_out", [Tc, D], F32, kind="ExternalOutput").ap()

        self.c_id_hbm = self.const("c_id", np.eye(P, dtype=np.float32))
        self.c_triu_hbm = self.const("c_triu", np.triu(np.ones((P, P), np.float32)))

        self.xres = None
        self.id_bf = self.sb("id_bf", [P, P], BF16)
        self.id_f32 = self.sb("id_f32", [P, P], F32)
        self.triu_bf = self.sb("triu_bf", [P, P], BF16)
        self.tri_f32 = self.sb("tri_f32", [P, P], F32)
        self.mv = self.sb("mv_stats", [P, 2 * (NT + 1)], F32)
        self.rstd = self.sb("rstd_all", [P, NT + 1], F32)
        self.nmr = self.sb("nmr_all", [P, NT + 1], F32)
        self.prevmask = self.sb("prevmask_sb", [self.NC, 1], F32)
        self.prevsel = self.sb("prevsel_sb", [self.NC, 1], F32)
        self.prevsel_bf = self.sb("prevsel_bf", [self.NC, 1], BF16)

        self.tpf = self.psum([P, 512], F32)
        self.pb = [self.psum([P, 512], F32) for _ in range(4)]
        self.tpb4 = [self.psum([P, P], BF16) for _ in range(2)]
        self.tpf4 = [self.psum([P, P], F32) for _ in range(1)]
        self.scr_hbm = nc.dram_tensor("scr_bounce", [16, D], BF16)

        self.scrb_hbm = nc.dram_tensor("scrb_hbm", [8, 4096], BF16)
        self.c_ones_row_h = self.const("c_ones_row", np.ones((1, P), np.float32))
        self.c_sel_last_h = self.const("c_sel_last",
                                       np.eye(P, dtype=np.float32)[:, -1:])
        self.c_negrow_h = self.const("c_negrow", -np.ones((1, P), np.float32))
        self.c_ones3_h = self.const("c_ones3", np.ones((3, P), np.float32))
        self.ones_row = self.sb("ones_row", [1, P], F32)
        self.dma(self.ones_row[:], self.c_ones_row_h[:], reads=[],
                 writes=["ones_row"])
        self.sel_last = self.sb("sel_last", [P, 1], F32)
        self.dma(self.sel_last[:], self.c_sel_last_h[:], reads=[],
                 writes=["sel_last"])
        nrow_f = self.sb("negrow_f", [1, P], F32)
        nrow_b = self.sb("negrow_b", [1, P], BF16)
        self.dma(nrow_f[:], self.c_negrow_h[:], reads=[], writes=["negrow_f"])
        self.ve(lambda e: e.tensor_copy(nrow_b[:], nrow_f[:]),
                reads=["negrow_f"], writes=["negrow_b"], tag="nrcast")
        self.dma(self.scrb_hbm[3:4, 0:P], nrow_b[:], reads=["negrow_b"],
                 writes=["scrb_nr"], q="gpsimd")
        o3f = self.sb("ones3_f", [3, P], F32)
        o3b = self.sb("ones3_b", [3, P], BF16)
        self.dma(o3f[:], self.c_ones3_h[:], reads=[], writes=["ones3_f"])
        self.ve(lambda e: e.tensor_copy(o3b[:], o3f[:]),
                reads=["ones3_f"], writes=["ones3_b"], tag="o3cast")
        self.dma(self.scrb_hbm[0:3, 0:P], o3b[:], reads=["ones3_b"],
                 writes=["scrb3"], q="gpsimd")

        self.dma(self.id_f32[:], self.c_id_hbm[:], reads=[], writes=["id_f32"])
        self.ve(lambda e: e.tensor_copy(self.id_bf[:], self.id_f32[:]),
                reads=["id_f32"], writes=["id_bf"], tag="idcast")
        self.dma(self.tri_f32[:], self.c_triu_hbm[:], reads=[], writes=["tri_f32"])
        self.ve(lambda e: e.tensor_copy(self.triu_bf[:], self.tri_f32[:]),
                reads=["tri_f32"], writes=["triu_bf"], tag="tricast")
        self.dma(self.prevmask[:], self.prevmask_hbm[:], reads=[], writes=["prevmask"])
        self.dma(self.prevsel[:], self.prevsel_hbm[:], reads=[], writes=["prevsel"])
        self.ve(lambda e: e.tensor_copy(self.prevsel_bf[:], self.prevsel[:]),
                reads=["prevsel"], writes=["prevsel_bf"], tag="pselcast")

        self.phases = []
        if self.do["a"]:
            self.attn_a()
        self.ensure_xres(load=not self.do["a"])
        if self.do["f1"]:
            self.ffn(0)
        if self.do["b"]:
            self.attn_b()
        if self.do["f2"]:
            self.ffn(1)

        for t in range(NT):
            self.dma(self.out_hbm[t * P:(t + 1) * P, :], self.xres[t][:],
                     reads=[f"xres{t}"], writes=[f"OUT{t}"], q="gpsimd", tag="store")
        self.gp(lambda e: e.nop(), reads=[f"OUT{t}" for t in range(NT)],
                writes=[], tag="final")

        self.pr.resolve()
        semkeys = set()
        for o in self.pr.ops:
            semkeys.add(o.semkey())
        sems = {}
        for key in sorted(semkeys):
            name = "sem_" + "_".join(str(x) for x in key)
            sems[key] = self.ctx.enter_context(self.nc.semaphore(name))
        with self.nc.Block() as block:
            @block.sync
            def _(eng):
                self.pr.emit_queue("sync", eng, sems)

            @block.tensor
            def _(eng):
                self.pr.emit_queue("tensor", eng, sems)

            @block.vector
            def _(eng):
                self.pr.emit_queue("vector", eng, sems)

            @block.scalar
            def _(eng):
                self.pr.emit_queue("scalar", eng, sems)

            @block.gpsimd
            def _(eng):
                self.pr.emit_queue("gpsimd", eng, sems)
        self.ctx.close()
        return self.nc

    def ensure_xres(self, load):
        if self.xres is not None:
            return
        self.xres = [self.sb(f"xres{t}", [P, D], F32) for t in range(self.NT)]
        if load:
            for t in range(self.NT):
                self.dma(self.xres[t][:], self.x_hbm[t * P:(t + 1) * P, :],
                         reads=[], writes=[f"xres{t}"])

    def ln_stats(self, xtiles):
        NT = len(xtiles)
        for t, (xt, xkey) in enumerate(xtiles):
            F = xt.shape[1] if hasattr(xt, "shape") else D
            nchunk = (F + 511) // 512
            bt = self.sbx(self.nm("bnst"), [P, 6 * nchunk], F32)
            for ch in range(nchunk):
                lo, hi = ch * 512, min((ch + 1) * 512, F)
                self.ve(lambda e, b=bt, x=xt, ch=ch, lo=lo, hi=hi: e.bn_stats(
                    b[:, 6 * ch:6 * ch + 6], x[:, lo:hi]),
                    reads=[xkey], writes=[bt.name], tag="bnstat")
            self.ve(lambda e, b=bt, t=t: e.bn_aggr(self.mv[:, 2 * t:2 * t + 2], b[:]),
                    reads=[bt.name], writes=["mv"], tag="bnaggr")
        self.ln_stats_finish(NT)

    def ln_stats_finish(self, NT):
        mv3 = self.mv[:].rearrange("p (t two) -> p t two", two=2)
        varv = mv3[:, 0:NT, 1:2]
        meanv = mv3[:, 0:NT, 0:1]
        lnv = self.sbx(self.nm("lnv"), [P, NT], F32)
        eps = self.fconst(LN_EPS)
        self.se(lambda e: e.activation(lnv[:], varv, AF.Ln, bias=eps[:], scale=1.0),
                reads=["mv", eps.name], writes=[lnv.name], tag="ln_v")
        self.se(lambda e: e.activation(self.rstd[:, 0:NT], lnv[:], AF.Exp,
                                       bias=0.0, scale=-0.5),
                reads=[lnv.name], writes=["rstd"], tag="exp_v")
        self.ve(lambda e: e.tensor_tensor(self.nmr[:, 0:NT], self.rstd[:, 0:NT],
                                          meanv, ALU.mult),
                reads=["rstd", "mv"], writes=["nmr"], tag="nmr1")
        self.ve(lambda e: e.tensor_scalar(self.nmr[:, 0:NT], self.nmr[:, 0:NT],
                                          -1.0, None, ALU.mult),
                reads=["nmr"], writes=["nmr"], tag="nmr2")

    def ln_apply(self, xt, xkey, t, out_ap, outkey):
        self.ve(lambda e: e.tensor_scalar(out_ap, xt[:], self.rstd[:, t:t + 1],
                                          self.nmr[:, t:t + 1], ALU.mult, ALU.add),
                reads=[xkey, "rstd", "nmr"], writes=[outkey], tag="lnapply")

    def transpose_to(self, src_ap, srckey, dst_ap, dstkey, ncol=P):
        s = self._tps % 2
        self._tps += 1
        tk = f"tpb4_{s}"
        tp = self.tpb4[s][0:ncol, 0:P]
        self.pe(lambda e: e.transpose(tp, src_ap, self.id_bf[:]),
                reads=[srckey, "id_bf"], writes=[tk], tag="transp")
        self.ve(lambda e: e.tensor_copy(dst_ap, tp),
                reads=[tk], writes=[dstkey], tag="tevac")

    def feature_cols(self, hbm_ap, F, name):
        nk = F // P
        stage = self.sbx(self.nm(name + "_st"), [nk, P], F32)
        out = self.sbx(name, [P, nk], F32)
        self.dma(stage[:], hbm_ap.rearrange("(a b) -> a b", b=P),
                 reads=[], writes=[stage.name], tag="fcload")
        for c0 in range(0, nk, P):
            c1 = min(nk, c0 + P)
            sl = 0
            tp = self.tpf4[0][:]
            tk = "tpf4_0"
            self.pe(lambda e, tp=tp, c0=c0, c1=c1: e.transpose(
                tp[0:P, 0:c1 - c0], stage[c0:c1, :], self.id_f32[0:c1 - c0, 0:c1 - c0]),
                reads=[stage.name, "id_f32"], writes=[tk], tag="fcT")
            self.ve(lambda e, tp=tp, c0=c0, c1=c1: e.tensor_copy(
                out[:, c0:c1], tp[0:P, 0:c1 - c0]),
                reads=[tk], writes=[name], tag="fcTev")
        return out

    def ffn_prep_weights(self, pre):
        self.WTt = {w: [self.sbP(f"{pre}WT_{w}_{kk}", [P, D], BF16)
                        for kk in range(8)] for w in ("r", "k", "v")}
        wtmp = [self.sbP(f"{pre}w_tmp{i}", [P, D], F32) for i in range(2)]
        wtmp_bf = [self.sbP(f"{pre}w_tmpbf{i}", [P, D], BF16) for i in range(2)]
        i = 0
        for w in ("r", "k", "v"):
            hbm = self.W[f"fw_{w}"]
            for dd in range(8):
                b = i % 2
                i += 1
                self.dma(wtmp[b][:], hbm[dd * P:(dd + 1) * P, :],
                         reads=[f"{pre}w_tmpbf{b}"], writes=[f"{pre}w_tmp{b}"],
                         tag="fwload")
                self.ve(lambda e, b=b: e.tensor_copy(wtmp_bf[b][:], wtmp[b][:]),
                        reads=[f"{pre}w_tmp{b}"], writes=[f"{pre}w_tmpbf{b}"],
                        tag="fwcast")
                for kk in range(8):
                    dst = self.WTt[w][kk][:, dd * P:(dd + 1) * P]
                    self.transpose_to(wtmp_bf[b][:, kk * P:(kk + 1) * P],
                                      f"{pre}w_tmpbf{b}", dst, f"{pre}WT_{w}_{kk}")
        self.mu = {}
        for m in ("mur", "muk"):
            self.mu[m] = self.feature_cols(self.W[f"fw_{m}"], D, f"{pre}mu_{m}")

    def ffn(self, which):
        NT = self.NT
        pre = f"f{which}_"
        self.begin_phase()
        self.ffn_prep_weights(pre)
        self.pre = pre

        self.ln_stats([(self.xres[t], f"xres{t}") for t in range(NT)])

        xln = self.sbP(self.nm("f_xln"), [P, D], BF16)
        xlt = [[self.sbP(self.nm("f_xlt"), [P, P + 1], BF16) for kk in range(8)]
               for par in range(2)]
        xrT = self.sbP(self.nm("f_xrT"), [P, D], BF16)
        xkT = self.sbP(self.nm("f_xkT"), [P, D], BF16)
        tsub = self.sbP(self.nm("f_tsub"), [P, P], BF16)
        r_bf = self.sbP(self.nm("f_r"), [P, D], BF16)
        kr_bf = self.sbP(self.nm("f_kr"), [P, D], BF16)
        ksq = self.sbP(self.nm("f_ksq"), [P, D], BF16)
        ksqT = self.sbP(self.nm("f_ksqT"), [P, D], BF16)
        vv = self.sbP(self.nm("f_vv"), [P, D], F32)

        cc_in = self.nc.dram_tensor(self.nm("fcc_in"), [1, D], BF16)
        cc_out = self.nc.dram_tensor(self.nm("fcc_out"), [self.NC, D], BF16)
        ck = f"fcc{which}"
        halo_sb = self.sbP(self.nm("f_halo"), [self.NC, D], BF16)
        halo_row = self.sbP(self.nm("f_halor"), [1, D], BF16)
        halo8 = self.sbP(self.nm("f_halo8"), [8, P], BF16)
        halo_col = self.sbP(self.nm("f_halocol"), [P, 8], BF16)

        def transposes(t, par):
            self.ln_apply(self.xres[t], f"xres{t}", t, xln[:], xln.name)
            for kk in range(8):
                self.transpose_to(xln[:, kk * P:(kk + 1) * P], xln.name,
                                  xlt[par][kk][:, 1:P + 1], xlt[par][kk].name)

        t_last = NT - 1
        transposes(t_last, t_last % 2)
        self.dma(cc_in[0:1, :], xln[P - 1:P, :], reads=[xln.name], writes=[ck + "i"],
                 q="gpsimd", tag="ffn_ccw")
        self.gp(lambda e: e.collective_compute(
            "AllGather", ALU.bypass, replica_groups=[list(range(self.NC))],
            ins=[cc_in.ap().opt()], outs=[cc_out.ap().opt()]),
            reads=[ck + "i"], writes=[ck + "o"], kind="cc", tag="ffn_ag")
        self.dma(halo_sb[:], cc_out[:, :], reads=[ck + "o"], writes=[halo_sb.name],
                 q="gpsimd", tag="ffn_ccr")
        for hh in range(2):
            self.pe(lambda e, hh=hh: e.matmul(
                self.pb[hh][0:1, :], self.prevsel_bf[:],
                halo_sb[:, hh * 512:(hh + 1) * 512], start=True, stop=True),
                reads=[halo_sb.name, "prevsel_bf"], writes=[f"pb{hh}"],
                tag="halosel")
            self.ve(lambda e, hh=hh: e.tensor_copy(
                halo_row[:, hh * 512:(hh + 1) * 512], self.pb[hh][0:1, :]),
                reads=[f"pb{hh}"], writes=[halo_row.name], tag="haloev")
        self.dma(self.scr_hbm[0:1, :], halo_row[:], reads=[halo_row.name],
                 writes=[ck + "sc"], q="gpsimd", tag="haloscw")
        self.dma(halo8[:], self.scr_hbm[0:1, :].rearrange("a (k p) -> (a k) p", p=P),
                 reads=[ck + "sc"], writes=[halo8.name], q="gpsimd", tag="haloscr")
        self.pe(lambda e: e.transpose(self.tpb4[0][0:P, 0:8], halo8[0:8, :],
                                      self.id_bf[0:8, 0:8]),
                reads=[halo8.name, "id_bf"], writes=["tpb4_0"], tag="haloT")
        self.ve(lambda e: e.tensor_copy(halo_col[:], self.tpb4[0][0:P, 0:8]),
                reads=["tpb4_0"], writes=[halo_col.name], tag="haloTev")

        for t in list(range(NT - 1)) + [t_last]:
            par = t % 2
            transposes(t, par)
            for kk in range(8):
                if t == 0:
                    self.ve(lambda e, kk=kk, par=par: e.tensor_copy(
                        xlt[par][kk][:, 0:1], halo_col[:, kk:kk + 1]),
                        reads=[halo_col.name], writes=[xlt[par][kk].name], tag="bcol0")
                else:
                    pp = (t - 1) % 2
                    self.ve(lambda e, kk=kk, par=par, pp=pp: e.tensor_copy(
                        xlt[par][kk][:, 0:1], xlt[pp][kk][:, P:P + 1]),
                        reads=[xlt[pp][kk].name], writes=[xlt[par][kk].name], tag="bcol")
            self._ffn_tile(t, par, xlt, xrT, xkT, tsub, r_bf, kr_bf, ksq, ksqT, vv)
        self.end_phase()

    def _ffn_tile(self, t, par, xlt, xrT, xkT, tsub, r_bf, kr_bf, ksq, ksqT, vv):
        for kk in range(8):
            xc = xlt[par][kk][:, 1:P + 1]
            xp = xlt[par][kk][:, 0:P]
            xk_ = xlt[par][kk].name
            self.ve(lambda e, xc=xc, xp=xp: e.tensor_tensor(tsub[:], xc, xp, ALU.subtract),
                    reads=[xk_], writes=[tsub.name], tag="fmixsub")
            self.ve(lambda e, xp=xp, kk=kk, m=self.mu["mur"]: e.scalar_tensor_tensor(
                xrT[:, kk * P:(kk + 1) * P], tsub[:], m[:, kk:kk + 1], xp,
                ALU.mult, ALU.add),
                reads=[tsub.name, f"{self.pre}mu_mur", xk_], writes=[xrT.name], tag="fmixr")
            self.ve(lambda e, xp=xp, kk=kk, m=self.mu["muk"]: e.scalar_tensor_tensor(
                xkT[:, kk * P:(kk + 1) * P], tsub[:], m[:, kk:kk + 1], xp,
                ALU.mult, ALU.add),
                reads=[tsub.name, f"{self.pre}mu_muk", xk_], writes=[xkT.name], tag="fmixk")
        for nn in range(2):
            pbank, pk = self.pb[nn], f"pb{nn}"
            g = self.nm("gr")
            for kk in range(8):
                self.pe(lambda e, nn=nn, kk=kk, pbank=pbank, w=self.WTt["r"][kk]:
                        e.matmul(pbank[:], xrT[:, kk * P:(kk + 1) * P],
                                 w[:, nn * 512:(nn + 1) * 512],
                                 start=(kk == 0), stop=(kk == 7)),
                    reads=[xrT.name, f"{self.pre}WT_r_{kk}"], writes=[pk], tag="mm_r", group=g)
            self.se(lambda e, nn=nn, pbank=pbank: e.activation(
                r_bf[:, nn * 512:(nn + 1) * 512], pbank[:], AF.Sigmoid,
                bias=0.0, scale=1.0),
                reads=[pk], writes=[r_bf.name], tag="sig_r")
        for nn in range(2):
            pbank, pk = self.pb[2 + nn], f"pb{2+nn}"
            g = self.nm("gk")
            for kk in range(8):
                self.pe(lambda e, nn=nn, kk=kk, pbank=pbank, w=self.WTt["k"][kk]:
                        e.matmul(pbank[:], xkT[:, kk * P:(kk + 1) * P],
                                 w[:, nn * 512:(nn + 1) * 512],
                                 start=(kk == 0), stop=(kk == 7)),
                    reads=[xkT.name, f"{self.pre}WT_k_{kk}"], writes=[pk], tag="mm_k", group=g)
            self.se(lambda e, nn=nn, pbank=pbank: e.activation(
                kr_bf[:, nn * 512:(nn + 1) * 512], pbank[:], AF.Relu,
                bias=0.0, scale=1.0),
                reads=[pk], writes=[kr_bf.name], tag="relu_k")
        self.ve(lambda e: e.tensor_tensor(ksq[:], kr_bf[:], kr_bf[:], ALU.mult),
                reads=[kr_bf.name], writes=[ksq.name], tag="ksq")
        for kk in range(8):
            self.transpose_to(ksq[:, kk * P:(kk + 1) * P], ksq.name,
                              ksqT[:, kk * P:(kk + 1) * P], ksqT.name)
        for nn in range(2):
            pbank, pk = self.pb[nn], f"pb{nn}"
            g = self.nm("gv")
            for kk in range(8):
                self.pe(lambda e, nn=nn, kk=kk, pbank=pbank, w=self.WTt["v"][kk]:
                        e.matmul(pbank[:], ksqT[:, kk * P:(kk + 1) * P],
                                 w[:, nn * 512:(nn + 1) * 512],
                                 start=(kk == 0), stop=(kk == 7)),
                    reads=[ksqT.name, f"{self.pre}WT_v_{kk}"], writes=[pk], tag="mm_v", group=g)
            self.ve(lambda e, nn=nn, pbank=pbank: e.scalar_tensor_tensor(
                vv[:, nn * 512:(nn + 1) * 512], pbank[:], 0.0,
                r_bf[:, nn * 512:(nn + 1) * 512], ALU.bypass, ALU.mult),
                reads=[pk, r_bf.name], writes=[vv.name], tag="gate_v")
        self.ve(lambda e, t=t: e.tensor_tensor(self.xres[t][:], self.xres[t][:],
                                               vv[:], ALU.add),
                reads=[vv.name, f"xres{t}"], writes=[f"xres{t}"], tag="resadd")

    def attn_a(self):
        self.begin_phase()
        NT = self.NT
        nc = self.nc
        self.u_hbm = nc.dram_tensor("u_hbm", [self.Tc, 2048], BF16)
        self.zs_hbm = nc.dram_tensor("zs_hbm", [self.Tc, 2048], BF16)
        self.q9t_hbm = nc.dram_tensor("q9t_hbm", [16, P, self.Tc], BF16)
        self.aWx_hbm = nc.dram_tensor("aWx_hbm", [19, P, 1024], BF16)
        self.aWz_hbm = nc.dram_tensor("aWz_hbm", [8, P, 2048], BF16)
        self.oWT_hbm = nc.dram_tensor("oWT_hbm", [16, P, 1024], BF16)
        self.cca_in = nc.dram_tensor("cca_in", [P, 2080], F32)
        self.cca_out = nc.dram_tensor("cca_out", [self.NC * P, 2080], F32)

        self._aa_consts()
        NTL = self.NT
        self.B_f = self.sbP("B_f", [P, self.Tc], BF16)
        self.C_f = self.sbP("C_f", [P, self.Tc], BF16)
        self.B_tok = self.sbP("B_tok", [P, self.Tc], BF16)
        self.dt_all = self.sbP("dt_all", [P, (NTL + 1) * 32], F32)
        self.cs_all = self.sbP("cs_all", [P, NTL * 32], F32)
        self.CSE = self.sbP("CSE", [1, NTL * 32], F32)
        self.Ddec = self.sbP("Ddec", [P, NTL * 32], F32)
        self.Doff = self.sbP("Doff", [P, NTL * 32], F32)
        self.eA_rep = self.sbP("eA_rep", [P, NTL * 32], F32)
        self.S_run = self.sbP("S_run", [P, 2048], F32)

        self.begin_phase()
        NT1 = self.NT + 1
        self.q8T_all = self.sbP("q8T_all", [P, 8 * NT1 * P], BF16)
        self.TW = NT1 * P
        self.begin_phase()
        self._aa_wprep()
        self._aa_quant()
        self.end_phase()
        self.begin_phase()
        self._aa_inproj_conv()
        self._aa_dt()
        self._aa_states()
        self._aa_z()
        self.end_phase()
        self.end_phase()
        self.begin_phase()
        self.begin_phase()
        self._aa_sin()
        self.end_phase()
        self._aa_pass2()
        self.end_phase()
        self.ensure_xres(load=True)
        self.begin_phase()
        self._aa_outproj()
        self.end_phase()
        self.end_phase()

    def rowload(self, hbm_ap, n, name):
        t = self.sbP(self.nm(name), [1, n], F32)
        self.dma(t[:], hbm_ap.rearrange("(a b) -> a b", a=1), reads=[], writes=[t.name])
        return t

    def bcast_row(self, row_ap, rowkey, n, name, dtype=F32):
        out = self.sbP(name, [P, n], dtype)
        nb = (n + 511) // 512
        for b in range(nb):
            lo, hi = b * 512, min(n, (b + 1) * 512)
            s = self._tps % 4
            self._tps += 1
            ps = self.tpf[:, 128 * s:128 * s + 128] if hi - lo <= 128 else self.pb[b % 4]
            pk = "tpfB" if hi - lo <= 128 else f"pb{b%4}"
            self.pe(lambda e, ps=ps, lo=lo, hi=hi: e.matmul(
                ps[0:P, 0:hi - lo], self.ones_row[:], row_ap[0:1, lo:hi],
                start=True, stop=True),
                reads=[rowkey, "ones_row"], writes=[pk], tag="bcast")
            self.ve(lambda e, ps=ps, lo=lo, hi=hi: e.tensor_copy(
                out[:, lo:hi], ps[0:P, 0:hi - lo]),
                reads=[pk], writes=[name], tag="bcastev")
        return out

    def ln_single(self, x_ap, xkey, F, pref):
        nch = (F + 511) // 512
        bt = self.sbP(self.nm(pref + "bn"), [P, 6 * nch], F32)
        mvt = self.sbP(self.nm(pref + "mv"), [P, 2], F32)
        r = self.sbP(self.nm(pref + "r"), [P, 1], F32)
        nm = self.sbP(self.nm(pref + "nm"), [P, 1], F32)
        for ch in range(nch):
            lo, hi = ch * 512, min(F, (ch + 1) * 512)
            self.ve(lambda e, ch=ch, lo=lo, hi=hi: e.bn_stats(
                bt[:, 6 * ch:6 * ch + 6], x_ap[:, lo:hi]),
                reads=[xkey], writes=[bt.name], tag="lns_bn")
        self.ve(lambda e: e.bn_aggr(mvt[:], bt[:]),
                reads=[bt.name], writes=[mvt.name], tag="lns_ag")
        eps = self.fconst(LN_EPS)
        self.se(lambda e: e.activation(r[:], mvt[:, 1:2], AF.Ln, bias=eps[:], scale=1.0),
                reads=[mvt.name, eps.name], writes=[r.name], tag="lns_ln")
        self.se(lambda e: e.activation(r[:], r[:], AF.Exp, bias=0.0, scale=-0.5),
                reads=[r.name], writes=[r.name], tag="lns_ex")
        self.ve(lambda e: e.tensor_tensor(nm[:], r[:], mvt[:, 0:1], ALU.mult),
                reads=[r.name, mvt.name], writes=[nm.name], tag="lns_n1")
        self.ve(lambda e: e.tensor_scalar(nm[:], nm[:], -1.0, None, ALU.mult),
                reads=[nm.name], writes=[nm.name], tag="lns_n2")
        return r, nm

    def quant_chain(self, x_ap, xkey, F, out_ap, outkey, pref):
        r2, nm2 = self.ln_single(x_ap, xkey, F, pref)
        if not hasattr(self, "_qscr"):
            self._qscr = {}
        if F not in self._qscr:
            self._qscr[F] = (self.sbP(self.nm("qy2"), [P, F], F32),
                             self.sbP(self.nm("qtq"), [P, F], F32),
                             self.sbP(self.nm("qsg"), [P, F], F32),
                             self.sbP(self.nm("qi"), [P, F],
                                      mybir.dt.int32))
        y2, tq, sg, qi = self._qscr[F]
        self.ve(lambda e: e.tensor_scalar(y2[:], x_ap, r2[:], nm2[:], ALU.mult, ALU.add),
                reads=[xkey, r2.name, nm2.name], writes=[y2.name], tag="q_ln2")
        am = self.sbP(self.nm(pref + "am"), [P, 1], F32)
        sc = self.sbP(self.nm(pref + "sc"), [P, 1], F32)
        si = self.sbP(self.nm(pref + "si"), [P, 1], F32)
        self.ve(lambda e: e.tensor_reduce(am[:], y2[:], AX.X, ALU.max,
                                          apply_absolute_value=True),
                reads=[y2.name], writes=[am.name], tag="q_am")
        self.ve(lambda e: e.tensor_scalar(am[:], am[:], 1e-5, None, ALU.max),
                reads=[am.name], writes=[am.name], tag="q_amc")
        self.ve(lambda e: e.reciprocal(sc[:], am[:]),
                reads=[am.name], writes=[sc.name], tag="q_rc")
        self.ve(lambda e: e.tensor_scalar(sc[:], sc[:], 127.0, None, ALU.mult),
                reads=[sc.name], writes=[sc.name], tag="q_sc")
        self.ve(lambda e: e.tensor_scalar(si[:], am[:], 1.0 / 127.0, None, ALU.mult),
                reads=[am.name], writes=[si.name], tag="q_si")
        self.ve(lambda e: e.tensor_scalar(tq[:], y2[:], sc[:], None, ALU.mult),
                reads=[y2.name, sc.name], writes=[tq.name], tag="q_t")
        self.ve(lambda e: e.tensor_copy(qi[:], tq[:]),
                reads=[tq.name], writes=[qi.name], tag="q_i")
        self.ve(lambda e: e.tensor_scalar(sg[:], qi[:], 1.0, None, ALU.mult),
                reads=[qi.name], writes=[sg.name], tag="q_b")
        self.ve(lambda e: e.tensor_tensor(tq[:], tq[:], sg[:], ALU.subtract),
                reads=[tq.name, sg.name], writes=[tq.name], tag="q_r")
        self.ve(lambda e: e.tensor_scalar(y2[:], tq[:], 0.5, None, ALU.is_gt),
                reads=[tq.name], writes=[y2.name], tag="q_c1")
        self.ve(lambda e: e.tensor_scalar(tq[:], tq[:], -0.5, None, ALU.is_lt),
                reads=[tq.name], writes=[tq.name], tag="q_c2")
        self.ve(lambda e: e.tensor_tensor(sg[:], sg[:], y2[:], ALU.add),
                reads=[sg.name, y2.name], writes=[sg.name], tag="q_a1")
        self.ve(lambda e: e.tensor_tensor(sg[:], sg[:], tq[:], ALU.subtract),
                reads=[sg.name, tq.name], writes=[sg.name], tag="q_a2")
        self.ve(lambda e: e.tensor_scalar(out_ap, sg[:], si[:], None, ALU.mult),
                reads=[sg.name, si.name], writes=[outkey], tag="q_o")

    def _aa_consts(self):
        nwr = self.rowload(self.W["aw_norm"], D, "nw_row")
        self.normw_b = self.bcast_row(nwr[:], nwr.name, D, "normw_b")
        onr = self.rowload(self.W["aw_onorm"], 2048, "onw_row")
        self.onw_b = self.bcast_row(onr[:], onr.name, 2048, "onw_b")
        alog = self.rowload(self.W["aw_Alog"], 32, "alog_row")
        negA = self.sbP("negA_row", [1, 32], F32)
        self.se(lambda e: e.activation(negA[:], alog[:], AF.Exp, bias=0.0, scale=1.0),
                reads=[alog.name], writes=["negA_row"], tag="negA1")
        self.ve(lambda e: e.tensor_scalar(negA[:], negA[:], -1.0, None, ALU.mult),
                reads=["negA_row"], writes=["negA_row"], tag="negA2")
        self.negA_b = self.bcast_row(negA[:], "negA_row", 32, "negA_b")
        dtb = self.rowload(self.W["aw_dtb"], 32, "dtb_row")
        self.dtb_b = self.bcast_row(dtb[:], dtb.name, 32, "dtb_b")
        dpr = self.rowload(self.W["aw_Dp"], 32, "dp_row")
        self.Dp_b = self.bcast_row(dpr[:], dpr.name, 32, "Dp_b")
        self.cw = self.sbP("cw_all", [P, 18 * 4], F32)
        for dc in range(18):
            self.dma(self.cw[:, dc * 4:(dc + 1) * 4],
                     self.W["aw_convw"][dc * P:(dc + 1) * P, :],
                     reads=[], writes=["cw_all"])
        self.cb = self.feature_cols(self.W["aw_convb"], 2304, "cb_all")
        self.rhs99 = self.sbP("rhs99", [99, 4096], BF16)
        self.ve(lambda e: e.memset(self.rhs99[0:96, :], 0.0),
                reads=[], writes=["rhs99"], tag="rhsz")
        for rr in range(3):
            for h in range(32):
                self.dma(self.rhs99[32 * rr + h:32 * rr + h + 1,
                                    h * P:(h + 1) * P],
                         self.scrb_hbm[3:4, 0:P],
                         reads=["scrb_nr"], writes=["rhs99"], q="gpsimd",
                         tag="kron")
        self.lhsT99 = self.sbP("lhsT99", [99, P], BF16)
        self.dma(self.lhsT99[96:99, :], self.scrb_hbm[0:3, 0:P],
                 reads=["scrb3"], writes=["lhsT99"], q="gpsimd")

    def _aa_wprep(self):
        wld = [self.sbP(f"aw_ld{i}", [P, 2048], F32) for i in range(2)]
        wq = self.sbP("aw_q", [P, 2048], BF16)
        wt = self.sbP("aw_t", [P, 2048], F32)
        wm = self.sbP("aw_m", [P, 2048], F32)
        wi = self.sbP("aw_i", [P, 2048], mybir.dt.int32)
        wabs = self.sbP("aw_abs", [P, 2048], BF16)
        wstg = [self.sbP(f"aw_stg{i}", [P, P], BF16) for i in range(2)]

        zpad = self.sbP("aw_zpad", [P, 1024], F32)
        zpad_bf = self.sbP("aw_zpadb", [P, 1024], BF16)
        self.ve(lambda e: e.memset(zpad_bf[:], 0.0), reads=[], writes=["aw_zpadb"],
                tag="zpad")
        self.dma(self.aWx_hbm[18, :, :], zpad_bf[:], reads=["aw_zpadb"],
                 writes=["awsp_ip"], q="gpsimd", tag="zpadw")
        for wname, hbm, ntile, K, spills in (
                ("ip", self.W["aw_inproj"], 35, D, None),
                ("op", self.W["aw_outproj"], 8, 2048, None)):
            acc = self.sbP(f"aw_acc_{wname}", [P, ntile], F32)
            self.ve(lambda e, acc=acc: e.memset(acc[:], 0.0),
                    reads=[], writes=[acc.name], tag="accz")
            for dd in range(ntile):
                rows = 32 if (wname == "ip" and dd == 34) else P
                b = dd % 2
                self.dma(wld[b][0:rows, 0:K], hbm[dd * P:dd * P + rows, :],
                         reads=[f"aw_absdone{b}"], writes=[f"aw_ld{b}"], tag="awl")
                self.se(lambda e, b=b, dd=dd, rows=rows, K=K, acc=acc: e.activation(
                    wabs[0:rows, 0:K], wld[b][0:rows, 0:K], AF.Abs,
                    bias=0.0, scale=1.0, accum_out=acc[0:rows, dd:dd + 1]),
                    reads=[f"aw_ld{b}"], writes=[acc.name, "aw_abs", f"aw_absdone{b}"],
                    tag="awabs")
            cs1 = self.sbP(f"aw_cs_{wname}", [P, 1], F32)
            self.ve(lambda e, acc=acc, cs1=cs1: e.tensor_reduce(cs1[:], acc[:], AX.X, ALU.add),
                    reads=[acc.name], writes=[cs1.name], tag="awred")
            s = self._tps % 4
            self._tps += 1
            ps = self.tpf[:, 128 * s:128 * s + 128]
            pk = "tpfB"
            ones = self.fconst(1.0)
            self.pe(lambda e, ps=ps, cs1=cs1, ones=ones: e.matmul(ps[0:1, 0:1], cs1[:], ones[:],
                                              start=True, stop=True),
                    reads=[cs1.name, ones.name], writes=[pk], tag="awtot")
            tot = self.sbP(f"aw_tot_{wname}", [1, 1], F32)
            self.ve(lambda e, ps=ps, tot=tot: e.tensor_copy(tot[:], ps[0:1, 0:1]),
                    reads=[pk], writes=[tot.name], tag="awtev")
            self.ve(lambda e, tot=tot: e.reciprocal(tot[:], tot[:]),
                    reads=[tot.name], writes=[tot.name], tag="awrcp")
            NEL = float(ntile * P * K) if wname != "ip" else float(4384 * D)
            self.ve(lambda e, NEL=NEL, tot=tot: e.tensor_scalar(tot[:], tot[:], NEL, None,
                                                       ALU.mult),
                    reads=[tot.name], writes=[tot.name], tag="awsc")
            rs_col = self.sbP(f"aw_rs_{wname}", [P, 1], F32)
            s2 = self._tps % 4
            self._tps += 1
            ps2 = self.tpf[:, 128 * s2:128 * s2 + 128]
            pk2 = "tpfB"
            self.pe(lambda e, ps2=ps2, tot=tot: e.matmul(ps2[0:P, 0:1], self.ones_row[:],
                                                tot[:], start=True, stop=True),
                    reads=[tot.name, "ones_row"], writes=[pk2], tag="awbc")
            self.ve(lambda e, ps2=ps2, rs_col=rs_col: e.tensor_copy(rs_col[:], ps2[0:P, 0:1]),
                    reads=[pk2], writes=[rs_col.name], tag="awbcev")
            for dd in range(ntile):
                rows = 32 if (wname == "ip" and dd == 34) else P
                b = dd % 2
                self.dma(wld[b][0:rows, 0:K], hbm[dd * P:dd * P + rows, :],
                         reads=[f"aw_qdone{b}"], writes=[f"aw_ld{b}"], tag="awl2")
                self.ve(lambda e, b=b, rows=rows, K=K, rs_col=rs_col: e.tensor_scalar(
                    wt[0:rows, 0:K], wld[b][0:rows, 0:K], rs_col[0:rows, 0:1],
                    None, ALU.mult),
                    reads=[f"aw_ld{b}", rs_col.name],
                    writes=["aw_t", f"aw_qdone{b}"], tag="awt")
                self.ve(lambda e, rows=rows, K=K: e.tensor_copy(
                    wi[0:rows, 0:K], wt[0:rows, 0:K]),
                    reads=["aw_t"], writes=["aw_i"], tag="awi")
                self.ve(lambda e, rows=rows, K=K: e.tensor_scalar(
                    wm[0:rows, 0:K], wi[0:rows, 0:K], 1.0, None, ALU.mult),
                    reads=["aw_i"], writes=["aw_m"], tag="awb")
                self.ve(lambda e, rows=rows, K=K: e.tensor_tensor(
                    wt[0:rows, 0:K], wt[0:rows, 0:K], wm[0:rows, 0:K],
                    ALU.subtract),
                    reads=["aw_t", "aw_m"], writes=["aw_t"], tag="awr")
                self.ve(lambda e, rows=rows, K=K: e.tensor_scalar(
                    wq[0:rows, 0:K], wt[0:rows, 0:K], 0.5, None, ALU.is_gt),
                    reads=["aw_t"], writes=["aw_q"], tag="awc1")
                self.ve(lambda e, rows=rows, K=K: e.tensor_tensor(
                    wm[0:rows, 0:K], wm[0:rows, 0:K], wq[0:rows, 0:K], ALU.add),
                    reads=["aw_m", "aw_q"], writes=["aw_m"], tag="awa1")
                self.ve(lambda e, rows=rows, K=K: e.tensor_scalar(
                    wq[0:rows, 0:K], wt[0:rows, 0:K], -0.5, None, ALU.is_lt),
                    reads=["aw_t"], writes=["aw_q"], tag="awc2")
                self.ve(lambda e, rows=rows, K=K: e.tensor_tensor(
                    wm[0:rows, 0:K], wm[0:rows, 0:K], wq[0:rows, 0:K],
                    ALU.subtract),
                    reads=["aw_m", "aw_q"], writes=["aw_m"], tag="awa2")
                self.ve(lambda e, rows=rows, K=K: e.tensor_scalar(
                    wq[0:rows, 0:K], wm[0:rows, 0:K], 1.0, -1.0, ALU.min,
                    ALU.max),
                    reads=["aw_m"], writes=["aw_q"], tag="awclip")
                for kk in range(K // P):
                    s3 = self._tps % 2
                    self._tps += 1
                    tp = self.tpb4[s3][:]
                    tk = f"tpb4_{s3}"
                    wb = kk % 2
                    self.pe(lambda e, tp=tp, kk=kk, rows=rows: e.transpose(
                        tp[0:P, 0:rows], wq[0:rows, kk * P:(kk + 1) * P],
                        self.id_bf[0:rows, 0:rows]),
                        reads=["aw_q", "id_bf"], writes=[tk], tag="awT")
                    self.ve(lambda e, tp=tp, wb=wb, rows=rows: e.tensor_copy(
                        wstg[wb][0:P, 0:rows], tp[0:P, 0:rows]),
                        reads=[tk], writes=[f"aw_stg{wb}"], tag="awTev")
                    if wname == "ip":
                        if dd < 16:
                            dst = self.aWz_hbm[kk, :, dd * P:dd * P + rows]
                        else:
                            dst = self.aWx_hbm[dd - 16, :, kk * P:kk * P + rows]
                    else:
                        dst = self.oWT_hbm[kk, :, dd * P:dd * P + rows]
                    self.dma(dst, wstg[wb][0:P, 0:rows], reads=[f"aw_stg{wb}"],
                             writes=[f"awsp_{wname}", f"aw_stg{wb}"], q="gpsimd",
                             tag="awspill")

    def _aa_quant(self):
        NT1 = self.NT + 1
        xb = [self.sbP(f"aa_x{i}", [P, D], F32) for i in range(2)]

        def load_tile(t, b, rkey):
            if t == 0:
                self.ve(lambda e, b=b: e.memset(xb[b][:], 0.0),
                        reads=[rkey], writes=[f"aa_x{b}"], tag="xhz")
                self.dma(xb[b][125:128, :], self.xh_hbm[:, :],
                         reads=[f"aa_x{b}"], writes=[f"aa_x{b}"], tag="xhload")
            else:
                self.dma(xb[b][:], self.x_hbm[(t - 1) * P:t * P, :],
                         reads=[rkey], writes=[f"aa_x{b}"], tag="xload")

        for t in range(NT1):
            b = t % 2
            load_tile(t, b, f"aa_xs{b}")
            nchunk = 2
            bt = self.sbP(self.nm("bnst"), [P, 6 * nchunk], F32)
            for ch in range(nchunk):
                self.ve(lambda e, b=b, ch=ch, bt=bt: e.bn_stats(
                    bt[:, 6 * ch:6 * ch + 6], xb[b][:, ch * 512:(ch + 1) * 512]),
                    reads=[f"aa_x{b}"], writes=[bt.name], tag="bnstat")
            self.ve(lambda e, b=b, bt=bt, t=t: e.bn_aggr(
                self.mv[:, 2 * t:2 * t + 2], bt[:]),
                reads=[bt.name], writes=["mv", f"aa_xs{b}"], tag="bnaggr")
        self.ln_stats_finish(NT1)
        vtmp = self.sbP("aa_vt", [P, NT1], F32)
        rmsf = self.sbP("aa_rmsf", [P, NT1], F32)
        mv3 = self.mv[:].rearrange("p (t two) -> p t two", two=2)
        self.ve(lambda e: e.tensor_copy(vtmp[:].rearrange("p (t o) -> p t o", o=1),
                                        mv3[:, 0:NT1, 1:2]),
                reads=["mv"], writes=["aa_vt"], tag="aa_v")
        ve1 = self.sbP("aa_ve1", [P, NT1], F32)
        self.ve(lambda e: e.tensor_scalar(ve1[:], vtmp[:], LN_EPS, None, ALU.add),
                reads=["aa_vt"], writes=["aa_ve1"], tag="aa_ve")
        self.ve(lambda e: e.reciprocal(ve1[:], ve1[:]),
                reads=["aa_ve1"], writes=["aa_ve1"], tag="aa_rc")
        self.ve(lambda e: e.tensor_tensor(rmsf[:], vtmp[:], ve1[:], ALU.mult),
                reads=["aa_vt", "aa_ve1"], writes=["aa_rmsf"], tag="aa_ms")
        e6 = self.fconst(1e-6)
        self.se(lambda e: e.activation(rmsf[:], rmsf[:], AF.Ln, bias=e6[:], scale=1.0),
                reads=["aa_rmsf", e6.name], writes=["aa_rmsf"], tag="aa_ln")
        self.se(lambda e: e.activation(rmsf[:], rmsf[:], AF.Exp, bias=0.0, scale=-0.5),
                reads=["aa_rmsf"], writes=["aa_rmsf"], tag="aa_ex")

        y1 = self.sbP("aa_y1", [P, D], F32)
        hh = self.sbP("aa_h", [P, D], F32)
        q8s = self.sbP("aa_q8s", [P, D], BF16)
        for t in range(NT1):
            b = t % 2
            load_tile(t, b, f"aa_xa{b}")
            self.ln_apply(xb[b], f"aa_x{b}", t, y1[:], y1.name)
            self.ve(lambda e, t=t: e.scalar_tensor_tensor(
                hh[:], y1[:], rmsf[:, t:t + 1], self.normw_b[:], ALU.mult, ALU.mult),
                reads=[y1.name, "aa_rmsf", "normw_b"], writes=[hh.name], tag="aa_hh")
            self.quant_chain(hh[:], hh.name, D, q8s[:], q8s.name, f"aaq{t}")
            for kk in range(8):
                self.transpose_to(q8s[:, kk * P:(kk + 1) * P], q8s.name,
                                  self.q8T_all[:, kk * self.TW + t * P:
                                               kk * self.TW + (t + 1) * P],
                                  f"q8T_t{t}")

    def _aa_inproj_conv(self):
        NT1 = self.NT + 1
        TW = self.TW
        Tc = self.Tc
        grps = [(g * 512, min(TW, (g + 1) * 512)) for g in range((TW + 511) // 512)]
        aWd = [self.sbP(f"aWx_d{i}", [P, 1024], BF16) for i in range(2)]
        stage = self.sbP("xbc_stage", [P, TW], BF16)
        ct1 = self.sbP("aa_ct1", [P, Tc], BF16)
        ct2 = self.sbP("aa_ct2", [P, Tc], BF16)
        uf = self.sbP("aa_uf", [P, Tc], BF16)
        self.ustg = self.sbP("ustg", [P, 512], BF16)
        self.dtf = self.sbP("dtf", [32, TW], F32)

        q8keys = [f"q8T_t{t}" for t in range(NT1)]
        for dc in range(19):
            rows = 128 if dc < 18 else 32
            b = dc % 2
            self.dma(aWd[b][:], self.aWx_hbm[dc, :, :],
                     reads=[f"aWx_used{b}", "awsp_ip"],
                     writes=[f"aWx_d{b}"], tag="aWxl")
            for gi, (lo, hi) in enumerate(grps):
                pb = self.pb[gi % 2]
                pk = f"pb{gi % 2}"
                g = self.nm("gip")
                for kk in range(8):
                    self.pe(lambda e, b=b, kk=kk, lo=lo, hi=hi, pb=pb: e.matmul(
                        pb[0:P, 0:hi - lo], aWd[b][:, kk * P:(kk + 1) * P],
                        self.q8T_all[:, kk * TW + lo:kk * TW + hi],
                        start=(kk == 0), stop=(kk == 7)),
                        reads=[f"aWx_d{b}"] + q8keys, writes=[pk],
                        tag="mm_ip", group=g)
                last = gi == len(grps) - 1
                if dc < 18:
                    self.ve(lambda e, lo=lo, hi=hi, pb=pb: e.tensor_copy(
                        stage[:, lo:hi], pb[0:P, 0:hi - lo]),
                        reads=[pk], writes=["xbc_stage"] + ([f"aWx_used{b}"] if last else []),
                        tag="ipev")
                else:
                    self.ve(lambda e, lo=lo, hi=hi, pb=pb: e.tensor_copy(
                        self.dtf[0:32, lo:hi], pb[0:32, 0:hi - lo]),
                        reads=[pk], writes=["dtf"] + ([f"aWx_used{b}"] if last else []),
                        tag="ipevdt")
            if dc >= 18:
                continue
            cwc = [self.cw[:, dc * 4 + k:dc * 4 + k + 1] for k in range(4)]
            self.ve(lambda e, cwc=cwc: e.tensor_scalar(
                ct1[:], stage[:, 125:125 + Tc], cwc[0], None, ALU.mult),
                reads=["xbc_stage", "cw_all"], writes=["aa_ct1"], tag="cv0")
            self.ve(lambda e, cwc=cwc: e.scalar_tensor_tensor(
                ct2[:], stage[:, 126:126 + Tc], cwc[1], ct1[:], ALU.mult, ALU.add),
                reads=["xbc_stage", "cw_all", "aa_ct1"], writes=["aa_ct2"], tag="cv1")
            self.ve(lambda e, cwc=cwc: e.scalar_tensor_tensor(
                ct1[:], stage[:, 127:127 + Tc], cwc[2], ct2[:], ALU.mult, ALU.add),
                reads=["xbc_stage", "cw_all", "aa_ct2"], writes=["aa_ct1"], tag="cv2")
            self.ve(lambda e, cwc=cwc: e.scalar_tensor_tensor(
                ct2[:], stage[:, 128:128 + Tc], cwc[3], ct1[:], ALU.mult, ALU.add),
                reads=["xbc_stage", "cw_all", "aa_ct1"], writes=["aa_ct2"], tag="cv3")
            dst, dkey = ((uf, "aa_uf") if dc < 16 else
                         (self.B_f, "B_f") if dc == 16 else (self.C_f, "C_f"))
            self.se(lambda e, dst=dst, dc=dc: e.activation(
                dst[:], ct2[:], AF.Silu, bias=self.cb[:, dc:dc + 1], scale=1.0),
                reads=["aa_ct2", "cb_all"], writes=[dkey], tag="cvsilu")
            if dc < 16:
                for t0 in range(0, self.NT, 4):
                    nt4 = min(4, self.NT - t0)
                    for i in range(nt4):
                        tp = self.tpb4[i % 2][:]
                        self.pe(lambda e, tp=tp, t=t0 + i: e.transpose(
                            tp[:], uf[:, t * P:(t + 1) * P], self.id_bf[:]),
                            reads=["aa_uf", "id_bf"], writes=[f"tpb4_{i % 2}"], tag="uT")
                        self.ve(lambda e, tp=tp, i=i: e.tensor_copy(
                            self.ustg[:, i * P:(i + 1) * P], tp),
                            reads=[f"tpb4_{i % 2}"], writes=["ustg"], tag="uTev")
                    for i in range(nt4):
                        self.dma(self.u_hbm[(t0 + i) * P:(t0 + i + 1) * P,
                                            dc * P:(dc + 1) * P],
                                 self.ustg[:, i * P:(i + 1) * P],
                                 reads=["ustg"], writes=["u_sp"], q="gpsimd",
                                 tag="uspill")
            elif dc == 16:
                for t in range(self.NT):
                    self.transpose_to(self.B_f[:, t * P:(t + 1) * P], "B_f",
                                      self.B_tok[:, t * P:(t + 1) * P], "B_tok")

    def _aa_dt(self):
        NT1, NT = self.NT + 1, self.NT
        W = NT1 * 32
        for t in range(NT1):
            s3 = 0
            tp = self.tpf4[0][:]
            tk = "tpf4_0"
            self.pe(lambda e, tp=tp, t=t: e.transpose(
                tp[0:P, 0:32], self.dtf[0:32, t * P:(t + 1) * P],
                self.id_f32[0:32, 0:32]),
                reads=["dtf", "id_f32"], writes=[tk], tag="dtT")
            self.ve(lambda e, tp=tp, t=t: e.tensor_copy(
                self.dt_all[:, t * 32:(t + 1) * 32], tp[0:P, 0:32]),
                reads=[tk], writes=["dt_all"], tag="dtTev")
        dtbAP = self.dtb_b[:].unsqueeze(1).broadcast_to((P, NT1, 32))
        self.ve(lambda e: e.tensor_tensor(
            self.dt_all[:].rearrange("p (t h) -> p t h", h=32), self.dt_all[:].rearrange("p (t h) -> p t h", h=32), dtbAP, ALU.add),
            reads=["dt_all", "dtb_b"], writes=["dt_all"], tag="dt_b")
        ab = self.sbP("aa_ab", [P, W], F32)
        self.se(lambda e: e.activation(ab[:], self.dt_all[:], AF.Abs, bias=0.0,
                                       scale=1.0),
                reads=["dt_all"], writes=["aa_ab"], tag="dt_abs")
        self.se(lambda e: e.activation(ab[:], ab[:], AF.Exp, bias=0.0, scale=-1.0),
                reads=["aa_ab"], writes=["aa_ab"], tag="dt_exp")
        one = self.fconst(1.0)
        self.se(lambda e: e.activation(ab[:], ab[:], AF.Ln, bias=one[:], scale=1.0),
                reads=["aa_ab", one.name], writes=["aa_ab"], tag="dt_ln")
        rl = self.sbP("aa_rl", [P, W], F32)
        self.ve(lambda e: e.tensor_scalar(rl[:], self.dt_all[:], 0.0, None, ALU.max),
                reads=["dt_all"], writes=["aa_rl"], tag="dt_rl")
        self.ve(lambda e: e.tensor_tensor(self.dt_all[:], rl[:], ab[:], ALU.add),
                reads=["aa_rl", "aa_ab"], writes=["dt_all"], tag="dt_sp")
        self.a_all = self.sbP("a_all", [P, W], F32)
        negAAP = self.negA_b[:].unsqueeze(1).broadcast_to((P, NT1, 32))
        self.ve(lambda e: e.tensor_tensor(
            self.a_all[:].rearrange("p (t h) -> p t h", h=32),
            self.dt_all[:].rearrange("p (t h) -> p t h", h=32), negAAP, ALU.mult),
            reads=["dt_all", "negA_b"], writes=["a_all"], tag="a_mk")
        tri = self.tri_f32
        for ch in range(NT):
            t = ch + 1
            s3 = self._tps % 4
            self._tps += 1
            tp = self.tpf[:, 128 * s3:128 * s3 + 128]
            tk = "tpfB"
            self.pe(lambda e, tp=tp, t=t: e.matmul(
                tp[0:P, 0:32], tri[:], self.a_all[:, t * 32:(t + 1) * 32],
                start=True, stop=True),
                reads=["a_all", "tri_f32"], writes=[tk], tag="cs_mm")
            self.ve(lambda e, tp=tp, ch=ch: e.tensor_copy(
                self.cs_all[:, ch * 32:(ch + 1) * 32], tp[0:P, 0:32]),
                reads=[tk], writes=["cs_all"], tag="cs_ev")
            s4 = self._tps % 4
            self._tps += 1
            tp2 = self.tpf[:, 128 * s4:128 * s4 + 128]
            tk2 = "tpfB"
            self.pe(lambda e, tp2=tp2, ch=ch: e.matmul(
                tp2[0:1, 0:32], self.sel_last[:],
                self.cs_all[:, ch * 32:(ch + 1) * 32], start=True, stop=True),
                reads=["cs_all", "sel_last"], writes=[tk2], tag="cse_mm")
            self.ve(lambda e, tp2=tp2, ch=ch: e.tensor_copy(
                self.CSE[0:1, ch * 32:(ch + 1) * 32], tp2[0:1, 0:32]),
                reads=[tk2], writes=["CSE"], tag="cse_ev")
            s5 = self._tps % 4
            self._tps += 1
            tp3 = self.tpf[:, 128 * s5:128 * s5 + 128]
            tk3 = "tpfB"
            self.pe(lambda e, tp3=tp3, ch=ch: e.matmul(
                tp3[0:P, 0:32], self.ones_row[:],
                self.CSE[0:1, ch * 32:(ch + 1) * 32], start=True, stop=True),
                reads=["CSE", "ones_row"], writes=[tk3], tag="dd_bc")
            self.ve(lambda e, tp3=tp3, ch=ch: e.tensor_tensor(
                self.Ddec[:, ch * 32:(ch + 1) * 32], tp3[0:P, 0:32],
                self.cs_all[:, ch * 32:(ch + 1) * 32], ALU.subtract),
                reads=[tk3, "cs_all"], writes=["Ddec"], tag="dd_sub")
        self.se(lambda e: e.activation(self.Ddec[:], self.Ddec[:], AF.Exp,
                                       bias=0.0, scale=1.0),
                reads=["Ddec"], writes=["Ddec"], tag="dd_exp")
        self.se(lambda e: e.activation(self.Doff[:], self.cs_all[:], AF.Exp,
                                       bias=0.0, scale=1.0),
                reads=["cs_all"], writes=["Doff"], tag="doff")
        eC = self.sbP("eCSE", [1, NT * 32], F32)
        self.se(lambda e: e.activation(eC[:], self.CSE[:], AF.Exp, bias=0.0,
                                       scale=1.0),
                reads=["CSE"], writes=["eCSE"], tag="ecse")
        nb = (NT * 32 + 511) // 512
        for bnum in range(nb):
            lo, hi = bnum * 512, min(NT * 32, (bnum + 1) * 512)
            pbk = self.pb[bnum % 2]
            pk = f"pb{bnum % 2}"
            self.pe(lambda e, lo=lo, hi=hi, pbk=pbk: e.matmul(
                pbk[0:P, 0:hi - lo], self.ones_row[:], eC[0:1, lo:hi],
                start=True, stop=True),
                reads=["eCSE", "ones_row"], writes=[pk], tag="eA_bc")
            self.ve(lambda e, lo=lo, hi=hi, pbk=pbk: e.tensor_copy(
                self.eA_rep[:, lo:hi], pbk[0:P, 0:hi - lo]),
                reads=[pk], writes=["eA_rep"], tag="eA_ev")
        self.logLam = self.sbP("logLam", [1, 32], F32)
        self.ve(lambda e: e.tensor_reduce(
            self.logLam[:].unsqueeze(2), self.CSE[:].rearrange(
                "p (c h) -> p h c", h=32), AX.X, ALU.add),
            reads=["CSE"], writes=["logLam"], tag="loglam")

    def _aa_states(self):
        NT, Tc = self.NT, self.Tc
        self.ve(lambda e: e.memset(self.S_run[:], 0.0), reads=[], writes=["S_run"],
                tag="s0")
        u_ch = [self.sbP(f"u_ch{i}", [P, 2048], BF16) for i in range(2)]
        ud = self.sbP("aa_ud", [P, 2048], BF16)
        ddt = self.sbP("aa_ddt", [P, 32], F32)
        tA = self.sbP("aa_tA", [P, 2048], F32)
        for ch in range(NT):
            b = ch % 2
            self.dma(u_ch[b][:], self.u_hbm[ch * P:(ch + 1) * P, :],
                     reads=[f"u_used{b}", "u_sp"], writes=[f"u_ch{b}"], tag="uload")
            self.ve(lambda e, ch=ch: e.tensor_tensor(
                ddt[:], self.Ddec[:, ch * 32:(ch + 1) * 32],
                self.dt_all[:, (ch + 1) * 32:(ch + 2) * 32], ALU.mult),
                reads=["Ddec", "dt_all"], writes=["aa_ddt"], tag="ddt")
            self.ve(lambda e, b=b: e.tensor_tensor(
                ud[:].rearrange("p (h q) -> p h q", h=32), u_ch[b][:].rearrange(
                    "p (h q) -> p h q", h=32),
                ddt[:].unsqueeze(2).broadcast_to((P, 32, 64)), ALU.mult),
                reads=[f"u_ch{b}", "aa_ddt"], writes=["aa_ud", f"u_used{b}"],
                tag="udec")
            g = self.nm("gst")
            for q in range(4):
                self.pe(lambda e, ch=ch, q=q: e.matmul(
                    self.pb[q][:], self.B_tok[:, ch * P:(ch + 1) * P],
                    ud[:, q * 512:(q + 1) * 512], start=True, stop=True),
                    reads=["B_tok", "aa_ud"], writes=[f"pb{q}"], tag="mm_st", group=g)
            self.ve(lambda e, ch=ch: e.tensor_tensor(
                tA[:].rearrange("p (h q) -> p h q", h=32),
                self.S_run[:].rearrange("p (h q) -> p h q", h=32),
                self.eA_rep[:, ch * 32:(ch + 1) * 32].unsqueeze(2).broadcast_to(
                    (P, 32, 64)), ALU.mult),
                reads=["S_run", "eA_rep"], writes=["aa_tA"], tag="s_mul")
            for q in range(4):
                self.ve(lambda e, q=q: e.tensor_tensor(
                    self.S_run[:, q * 512:(q + 1) * 512],
                    tA[:, q * 512:(q + 1) * 512], self.pb[q][:], ALU.add),
                    reads=["aa_tA", f"pb{q}"], writes=["S_run"], tag="s_add")
        llb = self.bcast_row(self.logLam[:], "logLam", 32, "logLam_b")
        self.dma(self.cca_in[:, 0:2048], self.S_run[:], reads=["S_run"],
                 writes=["cca_i"], q="gpsimd", tag="e1w")
        self.dma(self.cca_in[:, 2048:2080], llb[:], reads=["logLam_b"],
                 writes=["cca_i2"], q="gpsimd", tag="e1w2")
        self.gp(lambda e: e.collective_compute(
            "AllGather", ALU.bypass, replica_groups=[list(range(self.NC))],
            ins=[self.cca_in.ap().opt()], outs=[self.cca_out.ap().opt()]),
            reads=["cca_i", "cca_i2"], writes=["cca_o"], kind="cc", tag="e1ag")

    def _aa_z(self):
        NT, TW = self.NT, self.TW
        zw = [self.sbP(f"aa_zw{i}", [P, 512], BF16) for i in range(2)]
        self._zsil = self.sbP("aa_zsil", [P, 512], BF16)
        ngw = 0
        for tg in range((NT + 3) // 4):
            tiles = [tg * 4 + i for i in range(4) if tg * 4 + i < NT]
            for zg in range(4):
                g = self.nm("gz")
                for kk in range(8):
                    b = ngw % 2
                    ngw += 1
                    self.dma(zw[b][:], self.aWz_hbm[kk, :, zg * 512:(zg + 1) * 512],
                             reads=[f"zw_used{b}", "awsp_ip"],
                             writes=[f"aa_zw{b}"], tag="zwl")
                    for i, t in enumerate(tiles):
                        self.pe(lambda e, b=b, kk=kk, t=t, i=i: e.matmul(
                            self.pb[i][:],
                            self.q8T_all[:, kk * TW + (t + 1) * P:
                                         kk * TW + (t + 2) * P],
                            zw[b][:], start=(kk == 0), stop=(kk == 7)),
                            reads=[f"aa_zw{b}", f"q8T_t{t+1}"],
                            writes=[f"pb{i}"] + ([f"zw_used{b}"] if i == len(tiles) - 1 else []),
                            tag="mm_z", group=g + str(i))
                for i, t in enumerate(tiles):
                    self.se(lambda e, i=i: e.activation(
                        self._zsil[:], self.pb[i][:],
                        AF.Silu, bias=0.0, scale=1.0),
                        reads=[f"pb{i}"], writes=["aa_zsil"], tag="zsilu")
                    self.dma(self.zs_hbm[t * P:(t + 1) * P, zg * 512:(zg + 1) * 512],
                             self._zsil[:], reads=["aa_zsil"], writes=["zs_sp"],
                             q="gpsimd", tag="zspill")

    def _aa_sin(self):
        NC = self.NC
        Lall = self.sbP("aa_Lall", [P, NC * 32], F32)
        for j in range(NC):
            self.dma(Lall[:, j * 32:(j + 1) * 32],
                     self.cca_out[j * P:(j + 1) * P, 2048:2080],
                     reads=["cca_o"], writes=["aa_Lall"], tag="e1rL")
        pmr = self.sbP("pm_row", [1, NC], F32)
        self.dma(pmr[:], self.prevmask_hbm.rearrange("a b -> b a"), reads=[],
                 writes=["pm_row"], tag="pmr")
        pmb = self.bcast_row(pmr[:], "pm_row", NC, "aa_pmb")
        Lm = self.sbP("aa_Lm", [P, NC * 32], F32)
        self.ve(lambda e: e.tensor_tensor(
            Lm[:].rearrange("p (j h) -> p j h", h=32),
            Lall[:].rearrange("p (j h) -> p j h", h=32),
            pmb[:].unsqueeze(2).broadcast_to((P, NC, 32)), ALU.mult),
            reads=["aa_Lall", "aa_pmb"], writes=["aa_Lm"], tag="lm")
        sfx = self.sbP("aa_sfx", [P, NC * 32], F32)
        self.ve(lambda e: e.memset(sfx[:, (NC - 1) * 32:NC * 32], 0.0),
                reads=[], writes=["aa_sfx"], tag="sfz")
        for j in range(NC - 2, -1, -1):
            self.ve(lambda e, j=j: e.tensor_tensor(
                sfx[:, j * 32:(j + 1) * 32], sfx[:, (j + 1) * 32:(j + 2) * 32],
                Lm[:, (j + 1) * 32:(j + 2) * 32], ALU.add),
                reads=["aa_sfx", "aa_Lm"], writes=["aa_sfx"], tag="sfx")
        wj = self.sbP("aa_wj", [P, NC * 32], F32)
        self.se(lambda e: e.activation(wj[:], sfx[:], AF.Exp, bias=0.0, scale=1.0),
                reads=["aa_sfx"], writes=["aa_wj"], tag="wj_e")
        self.ve(lambda e: e.tensor_tensor(
            wj[:].rearrange("p (j h) -> p j h", h=32),
            wj[:].rearrange("p (j h) -> p j h", h=32),
            pmb[:].unsqueeze(2).broadcast_to((P, NC, 32)), ALU.mult),
            reads=["aa_wj", "aa_pmb"], writes=["aa_wj"], tag="wj_m")
        self.ve(lambda e: e.memset(self.S_run[:], 0.0), reads=["S_run"],
                writes=["S_run"], tag="sin0")
        Sg = [self.sbP(f"aa_Sg{i}", [P, 2048], F32) for i in range(2)]
        tS = self.sbP("aa_tS", [P, 2048], F32)
        for j in range(NC):
            b = j % 2
            self.dma(Sg[b][:], self.cca_out[j * P:(j + 1) * P, 0:2048],
                     reads=[f"Sg_used{b}", "cca_o"], writes=[f"aa_Sg{b}"], tag="e1rS")
            self.ve(lambda e, b=b, j=j: e.tensor_tensor(
                tS[:].rearrange("p (h q) -> p h q", h=32),
                Sg[b][:].rearrange("p (h q) -> p h q", h=32),
                wj[:, j * 32:(j + 1) * 32].unsqueeze(2).broadcast_to((P, 32, 64)),
                ALU.mult),
                reads=[f"aa_Sg{b}", "aa_wj"], writes=["aa_tS", f"Sg_used{b}"],
                tag="sin_m")
            self.ve(lambda e: e.tensor_tensor(self.S_run[:], self.S_run[:], tS[:],
                                              ALU.add),
                    reads=["aa_tS", "S_run"], writes=["S_run"], tag="sin_a")

    def _aa_pass2(self):
        NT, Tc, TW = self.NT, self.Tc, self.TW
        u_ch = [self.sbP(f"p2_u{i}", [P, 2048], BF16) for i in range(2)]
        zs_ch = [self.sbP(f"p2_z{i}", [P, 2048], BF16) for i in range(2)]
        S_bf = self.sbP("p2_Sbf", [P, 2048], BF16)
        msc = self.sbP("p2_msc", [P, P], BF16)
        mt = [self.sbP(f"p2_mt{i}", [P, 2048], BF16) for i in range(2)]
        csp = self.sbP("p2_csp", [P, 32], F32)
        hi = self.sbP("p2_hi", [P, 32], BF16)
        mid = self.sbP("p2_mid", [P, 32], BF16)
        lo = self.sbP("p2_lo", [P, 32], BF16)
        yoff = self.sbP("p2_yoff", [P, 2048], F32)
        tdp = self.sbP("p2_tdp", [P, 2048], F32)
        udt = self.sbP("p2_udt", [P, 2048], BF16)
        yg = self.sbP("p2_yg", [P, 2048], F32)
        scr = self.sbP("p2_scr", [P, 2048], BF16)
        ssum = self.sbP("p2_ssum", [P, 1], F32)
        rmsr = self.sbP("p2_rmsr", [P, 1], F32)
        h2 = self.sbP("p2_h2", [P, 2048], F32)
        q9s = self.sbP("p2_q9s", [P, 2048], BF16)
        self.q9stg = self.sbP("q9stg", [P, 512], BF16)
        ud = self.sbP("p2_ud", [P, 2048], BF16)
        ddt = self.sbP("p2_ddt", [P, 32], F32)
        tA = self.sbP("p2_tA", [P, 2048], F32)
        sc2048 = self.fconst(1.0 / 2048.0)
        e6 = self.fconst(1e-6)

        for ch in range(NT):
            b = ch % 2
            self.dma(u_ch[b][:], self.u_hbm[ch * P:(ch + 1) * P, :],
                     reads=[f"p2u_used{b}", "u_sp"], writes=[f"p2_u{b}"], tag="p2ul")
            self.dma(zs_ch[b][:], self.zs_hbm[ch * P:(ch + 1) * P, :],
                     reads=[f"p2z_used{b}", "zs_sp"], writes=[f"p2_z{b}"], tag="p2zl")
            self.ve(lambda e: e.tensor_copy(S_bf[:], self.S_run[:]),
                    reads=["S_run"], writes=["p2_Sbf"], tag="p2sbf")
            s3 = self._tps % 4
            self._tps += 1
            tp = self.tpf[:, 128 * s3:128 * s3 + 128]
            tk = "tpfB"
            self.pe(lambda e, tp=tp, ch=ch: e.matmul(
                tp[:, 0:P], self.B_f[:, ch * P:(ch + 1) * P],
                self.C_f[:, ch * P:(ch + 1) * P], start=True, stop=True),
                reads=["B_f", "C_f"], writes=[tk], tag="p2sc")
            self.ve(lambda e, tp=tp: e.tensor_tensor(msc[:], tp[:, 0:P],
                                                     self.triu_bf[:], ALU.mult),
                    reads=[tk, "triu_bf"], writes=["p2_msc"], tag="p2scm")
            self.ve(lambda e, ch=ch: e.tensor_copy(
                csp[:], self.cs_all[:, ch * 32:(ch + 1) * 32]),
                reads=["cs_all"], writes=["p2_csp"], tag="p2cs")
            self.ve(lambda e: e.tensor_copy(hi[:], csp[:]),
                    reads=["p2_csp"], writes=["p2_hi"], tag="p2hi")
            self.ve(lambda e: e.tensor_tensor(csp[:], csp[:], hi[:], ALU.subtract),
                    reads=["p2_csp", "p2_hi"], writes=["p2_csp"], tag="p2r1")
            self.ve(lambda e: e.tensor_copy(mid[:], csp[:]),
                    reads=["p2_csp"], writes=["p2_mid"], tag="p2mid")
            self.ve(lambda e: e.tensor_tensor(csp[:], csp[:], mid[:], ALU.subtract),
                    reads=["p2_csp", "p2_mid"], writes=["p2_csp"], tag="p2r2")
            self.ve(lambda e: e.tensor_copy(lo[:], csp[:]),
                    reads=["p2_csp"], writes=["p2_lo"], tag="p2lo")
            for rr, (t_bf, t_key) in enumerate(
                    ((hi, "p2_hi"), (mid, "p2_mid"), (lo, "p2_lo"))):
                s4 = self._tps % 2
                self._tps += 1
                tpb = self.tpb4[s4][:]
                tbk = f"tpb4_{s4}"
                self.pe(lambda e, tpb=tpb, t_bf=t_bf: e.transpose(
                    tpb[0:32, 0:P], t_bf[:], self.id_bf[:]),
                    reads=[t_key, "id_bf"], writes=[tbk], tag="p2csT")
                self.ve(lambda e, tpb=tpb, rr=rr: e.tensor_copy(
                    self.lhsT99[32 * rr:32 * rr + 32, :], tpb[0:32, 0:P]),
                    reads=[tbk], writes=["lhsT99"], tag="p2csTe")
                self.dma(self.scrb_hbm[4 + rr:5 + rr, :].rearrange(
                    "a (h l) -> (a h) l", h=32),
                    self.lhsT99[32 * rr:32 * rr + 32, :],
                    reads=["lhsT99"], writes=[f"p2fl{rr}"], q="gpsimd", tag="p2flw")
                self.dma(self.rhs99[96 + rr:97 + rr, :],
                         self.scrb_hbm[4 + rr:5 + rr, 0:4096],
                         reads=[f"p2fl{rr}"], writes=["rhs99"], q="gpsimd",
                         tag="p2flr")
            for half in range(2):
                g = self.nm("gld")
                for q in range(4):
                    self.pe(lambda e, half=half, q=q: e.matmul(
                        self.pb[q][:], self.lhsT99[:],
                        self.rhs99[:, half * 2048 + q * 512:half * 2048 + (q + 1) * 512],
                        start=True, stop=True),
                        reads=["lhsT99", "rhs99"], writes=[f"pb{q}"],
                        tag="mm_ld", group=g)
                for q in range(4):
                    self.ve(lambda e, half=half, q=q: e.tensor_scalar(
                        mt[half][:, q * 512:(q + 1) * 512], self.pb[q][:],
                        0.0, None, ALU.min),
                        reads=[f"pb{q}"], writes=[f"p2_mt{half}"], tag="ld_ev")
                self.se(lambda e, half=half: e.activation(
                    mt[half][:], mt[half][:], AF.Exp, bias=0.0, scale=1.0),
                    reads=[f"p2_mt{half}"], writes=[f"p2_mt{half}"], tag="ld_exp")
                self.ve(lambda e, half=half: e.tensor_tensor(
                    mt[half][:].rearrange("p (h l) -> p h l", l=P),
                    mt[half][:].rearrange("p (h l) -> p h l", l=P),
                    msc[:].unsqueeze(1).broadcast_to((P, 16, P)), ALU.mult),
                    reads=[f"p2_mt{half}", "p2_msc"], writes=[f"p2_mt{half}"],
                    tag="ld_m")
            g = self.nm("gyo")
            for q in range(4):
                self.pe(lambda e, ch=ch, q=q: e.matmul(
                    self.pb[q][:], self.C_f[:, ch * P:(ch + 1) * P],
                    S_bf[:, q * 512:(q + 1) * 512], start=True, stop=True),
                    reads=["C_f", "p2_Sbf"], writes=[f"pb{q}"], tag="mm_yo", group=g)
            for q in range(4):
                self.ve(lambda e, ch=ch, q=q: e.tensor_tensor(
                    yoff[:, q * 512:(q + 1) * 512].rearrange(
                        "p (h z) -> p h z", z=64),
                    self.pb[q][:].rearrange("p (h z) -> p h z", z=64),
                    self.Doff[:, ch * 32 + q * 8:ch * 32 + (q + 1) * 8].unsqueeze(
                        2).broadcast_to((P, 8, 64)), ALU.mult),
                    reads=[f"pb{q}", "Doff"], writes=["p2_yoff"], tag="yo_ev")
            self.ve(lambda e, ch=ch, b=b: e.tensor_tensor(
                udt[:].rearrange("p (h z) -> p h z", h=32),
                u_ch[b][:].rearrange("p (h z) -> p h z", h=32),
                self.dt_all[:, (ch + 1) * 32:(ch + 2) * 32].unsqueeze(2).broadcast_to(
                    (P, 32, 64)), ALU.mult),
                reads=[f"p2_u{b}", "dt_all"], writes=["p2_udt"], tag="udt")
            for q in range(4):
                g = self.nm("gyd")
                for hh in range(8):
                    h = q * 8 + hh
                    self.pe(lambda e, h=h, hh=hh, q=q: e.matmul(
                        self.pb[q][:, hh * 64:(hh + 1) * 64],
                        mt[h // 16][:, (h % 16) * P:(h % 16 + 1) * P],
                        udt[:, h * 64:(h + 1) * 64],
                        start=True, stop=True),
                        reads=[f"p2_mt{h//16}", "p2_udt"], writes=[f"pb{q}"],
                        tag="mm_yd", group=g)
            self.ve(lambda e, b=b: e.tensor_tensor(
                tdp[:].rearrange("p (h z) -> p h z", h=32),
                u_ch[b][:].rearrange("p (h z) -> p h z", h=32),
                self.Dp_b[:].unsqueeze(2).broadcast_to((P, 32, 64)), ALU.mult),
                reads=[f"p2_u{b}", "Dp_b"], writes=["p2_tdp"], tag="tdp")
            self.ve(lambda e: e.tensor_tensor(yg[:], yoff[:], tdp[:], ALU.add),
                    reads=["p2_yoff", "p2_tdp"], writes=["p2_yg"], tag="yg1")
            for q in range(4):
                self.ve(lambda e, q=q: e.tensor_tensor(
                    yg[:, q * 512:(q + 1) * 512], yg[:, q * 512:(q + 1) * 512],
                    self.pb[q][:], ALU.add),
                    reads=["p2_yg", f"pb{q}"], writes=["p2_yg"], tag="yg2")
            self.ve(lambda e, b=b: e.tensor_tensor(yg[:], yg[:], zs_ch[b][:],
                                                   ALU.mult),
                    reads=["p2_yg", f"p2_z{b}"],
                    writes=["p2_yg", f"p2z_used{b}"], tag="yg3")
            self.se(lambda e: e.activation(scr[:], yg[:], AF.Square, bias=0.0,
                                           scale=1.0, accum_out=ssum[:]),
                    reads=["p2_yg"], writes=["p2_scr", "p2_ssum"], tag="rms_sq")
            self.se(lambda e: e.activation(rmsr[:], ssum[:], AF.Ln, bias=e6[:],
                                           scale=sc2048[:]),
                    reads=["p2_ssum", e6.name, sc2048.name], writes=["p2_rmsr"],
                    tag="rms_ln")
            self.se(lambda e: e.activation(rmsr[:], rmsr[:], AF.Exp, bias=0.0,
                                           scale=-0.5),
                    reads=["p2_rmsr"], writes=["p2_rmsr"], tag="rms_ex")
            self.ve(lambda e: e.scalar_tensor_tensor(
                h2[:], yg[:], rmsr[:], self.onw_b[:], ALU.mult, ALU.mult),
                reads=["p2_yg", "p2_rmsr", "onw_b"], writes=["p2_h2"], tag="rms_ap")
            self.quant_chain(h2[:], "p2_h2", 2048, q9s[:], "p2_q9s", f"p2q{ch}")
            for k0 in range(0, 16, 4):
                for i in range(4):
                    tpb = self.tpb4[i % 2][:]
                    self.pe(lambda e, tpb=tpb, kk=k0 + i: e.transpose(
                        tpb[:], q9s[:, kk * P:(kk + 1) * P], self.id_bf[:]),
                        reads=["p2_q9s", "id_bf"], writes=[f"tpb4_{i % 2}"], tag="q9T")
                    self.ve(lambda e, tpb=tpb, i=i: e.tensor_copy(
                        self.q9stg[:, i * P:(i + 1) * P], tpb),
                        reads=[f"tpb4_{i % 2}"], writes=["q9stg"], tag="q9Tev")
                for i in range(4):
                    self.dma(self.q9t_hbm[k0 + i, :, ch * P:(ch + 1) * P],
                             self.q9stg[:, i * P:(i + 1) * P],
                             reads=["q9stg"], writes=["q9_sp"], q="gpsimd",
                             tag="q9spill")
            self.ve(lambda e, ch=ch: e.tensor_tensor(
                ddt[:], self.Ddec[:, ch * 32:(ch + 1) * 32],
                self.dt_all[:, (ch + 1) * 32:(ch + 2) * 32], ALU.mult),
                reads=["Ddec", "dt_all"], writes=["p2_ddt"], tag="p2ddt")
            self.ve(lambda e, b=b: e.tensor_tensor(
                ud[:].rearrange("p (h z) -> p h z", h=32),
                u_ch[b][:].rearrange("p (h z) -> p h z", h=32),
                ddt[:].unsqueeze(2).broadcast_to((P, 32, 64)), ALU.mult),
                reads=[f"p2_u{b}", "p2_ddt"], writes=["p2_ud", f"p2u_used{b}"],
                tag="p2udec")
            g = self.nm("gst2")
            for q in range(4):
                self.pe(lambda e, ch=ch, q=q: e.matmul(
                    self.pb[q][:], self.B_tok[:, ch * P:(ch + 1) * P],
                    ud[:, q * 512:(q + 1) * 512], start=True, stop=True),
                    reads=["B_tok", "p2_ud"], writes=[f"pb{q}"], tag="mm_st2",
                    group=g)
            self.ve(lambda e, ch=ch: e.tensor_tensor(
                tA[:].rearrange("p (h z) -> p h z", h=32),
                self.S_run[:].rearrange("p (h z) -> p h z", h=32),
                self.eA_rep[:, ch * 32:(ch + 1) * 32].unsqueeze(2).broadcast_to(
                    (P, 32, 64)), ALU.mult),
                reads=["S_run", "eA_rep"], writes=["p2_tA"], tag="p2smul")
            for q in range(4):
                self.ve(lambda e, q=q: e.tensor_tensor(
                    self.S_run[:, q * 512:(q + 1) * 512],
                    tA[:, q * 512:(q + 1) * 512], self.pb[q][:], ALU.add),
                    reads=["p2_tA", f"pb{q}"], writes=["S_run"], tag="p2sadd")

    def _aa_outproj(self):
        NT, Tc = self.NT, self.Tc
        assert self.xres is not None
        oW = [self.sbP(f"op_w{i}", [P, 512], BF16) for i in range(2)]
        q9b = [self.sbP(f"op_q{i}", [P, 512], BF16) for i in range(16)]
        nw = 0
        for tg in range((NT + 3) // 4):
            tiles = [tg * 4 + i for i in range(4) if tg * 4 + i < NT]
            w = len(tiles) * P
            for kk in range(16):
                self.dma(q9b[kk][:, 0:w],
                         self.q9t_hbm[kk, :, tiles[0] * P:tiles[0] * P + w],
                         reads=[f"opq_used{kk}", "q9_sp"], writes=[f"op_q{kk}"],
                         tag="opql")
            for nn in range(2):
                for kk in range(16):
                    bw = nw % 2
                    nw += 1
                    self.dma(oW[bw][:], self.oWT_hbm[kk, :, nn * 512:(nn + 1) * 512],
                             reads=[f"opw_used{bw}", "awsp_op"],
                             writes=[f"op_w{bw}"], tag="opwl")
                    for i, t in enumerate(tiles):
                        self.pe(lambda e, bw=bw, kk=kk, i=i: e.matmul(
                            self.pb[i][:], q9b[kk][:, i * P:(i + 1) * P],
                            oW[bw][:], start=(kk == 0), stop=(kk == 15)),
                            reads=[f"op_w{bw}", f"op_q{kk}"],
                            writes=[f"pb{i}"] + ([f"opw_used{bw}"] if i == len(tiles) - 1 else []),
                            tag="mm_op", group=f"gop{tg}_{nn}_{i}")
                for i, t in enumerate(tiles):
                    self.ve(lambda e, i=i, t=t, nn=nn: e.tensor_tensor(
                        self.xres[t][:, nn * 512:(nn + 1) * 512],
                        self.xres[t][:, nn * 512:(nn + 1) * 512],
                        self.pb[i][:], ALU.add),
                        reads=[f"pb{i}", f"xres{t}"],
                        writes=[f"xres{t}"] + ([f"opq_used{kk2}" for kk2 in range(16)] if nn == 1 and i == len(tiles) - 1 else []),
                        tag="op_ev")

    def attn_b(self):
        nc, NT, Tc = self.nc, self.NT, self.Tc
        NT1 = NT + 1
        self.TW2 = NT1 * P
        self.qf_hbm = nc.dram_tensor("qf_hbm", [8, P, Tc], BF16)
        self.kf_hbm = nc.dram_tensor("kf_hbm", [8, P, Tc], BF16)
        self.kt_hbm = nc.dram_tensor("kt_hbm", [Tc, 1024], BF16)
        self.uw_hbm = nc.dram_tensor("uw_hbm", [Tc, 3072], BF16)
        self.suw_hbm = nc.dram_tensor("suw_hbm", [Tc, 3072], BF16)
        self.gg_hbm = nc.dram_tensor("gg_hbm", [Tc, 2048], BF16)
        self.yt_hbm = nc.dram_tensor("yt_hbm", [16, P, Tc], BF16)
        self.wout_hbm = nc.dram_tensor("wout_hbm", [16, P, 1024], BF16)
        self.ccb1_in = nc.dram_tensor("ccb1_in", [3, D], F32)
        self.ccb1_out = nc.dram_tensor("ccb1_out", [self.NC * 3, D], F32)
        self.ccb2_in = nc.dram_tensor("ccb2_in", [64, 3072], F32)
        self.ccb2_out = nc.dram_tensor("ccb2_out", [self.NC * 64, 3072], F32)

        self.begin_phase()
        self._ab_consts()
        self.ab_beta = self.sbP("ab_beta", [P, NT * 16], F32)
        self.ab_G = self.sbP("ab_G", [P, NT * 16], F32)
        self.ab_CSE = self.sbP("ab_CSE", [1, NT * 16], F32)
        self.ab_Dbdec = self.sbP("ab_Dbdec", [P, NT * 16], F32)
        self.ab_LamT = self.sbP("ab_LamT", [P, NT * 16], F32)
        self.ab_eG = self.sbP("ab_eG", [P, NT * 16], F32)
        self.ab_S = self.sbP("ab_S", [64, 2048], F32)
        self.rhs99b = self.sbP("rhs99b", [99, 2048], BF16)
        self.lhsT99b = self.sbP("lhsT99b", [99, P], BF16)
        self.s_csp = self.sbP("s_csp", [P, 16], F32)
        self.s_hi = self.sbP("s_hi", [P, 16], BF16)
        self.s_mid = self.sbP("s_mid", [P, 16], BF16)
        self.s_lo = self.sbP("s_lo", [P, 16], BF16)
        self.s_eAt = self.sbP("s_eAt", [P, 2048], BF16)
        self.s_eBt = self.sbP("s_eBt", [P, 2048], BF16)
        self._ab_kron16()

        self.begin_phase()
        self.x2lnT = self.sbP("x2lnT", [P, 8 * NT1 * P], BF16)
        self.begin_phase()
        self._ab_ln()
        self.end_phase()
        self.begin_phase()
        self._ab_proj()
        self.end_phase()
        self.begin_phase()
        self._ab_gg_wout()
        self.end_phase()
        self.end_phase()
        self.begin_phase()
        self._ab_solve()
        self.end_phase()
        self.begin_phase()
        self.begin_phase()
        self._ab_sin()
        self.end_phase()
        self._ab_pass2()
        self.end_phase()
        self.end_phase()

    def _ab_consts(self):
        half01 = np.zeros((P, 2), np.float32)
        half01[0:64, 0] = 1.0
        half01[64:128, 1] = 1.0
        self.c_half01_h = self.const("c_half01", half01)
        sel2 = np.zeros((2, P), np.float32)
        sel2[0, 0:64] = 1.0
        sel2[1, 64:128] = 1.0
        self.c_sel2_h = self.const("c_sel2", sel2)
        self.c_ntrius_h = self.const("c_ntrius",
                                     -np.triu(np.ones((P, P), np.float32), 1))
        self.c_ntril_h = self.const("c_ntril",
                                    -np.tril(np.ones((P, P), np.float32), -1))
        self.c_i64t_h = self.const("c_i64t",
                                   np.tile(np.eye(64, dtype=np.float32), (1, 16)))
        self.prevsel3_hbm = self.inp("prevsel3", [self.NC * 3, 3])

        h01f = self.sbP("half01f", [P, 2], F32)
        self.dma(h01f[:], self.c_half01_h[:], reads=[], writes=["half01f"])
        self.half01 = self.sbP("half01", [P, 2], BF16)
        self.ve(lambda e: e.tensor_copy(self.half01[:], h01f[:]),
                reads=["half01f"], writes=["half01"], tag="h01c")
        self.sel2f = self.sbP("sel2f", [2, P], F32)
        self.dma(self.sel2f[:], self.c_sel2_h[:], reads=[], writes=["sel2f"])
        self.ntrius = self.sbP("ntrius", [P, P], BF16)
        t1 = self.sbP("ntrius_f", [P, P], F32)
        self.dma(t1[:], self.c_ntrius_h[:], reads=[], writes=["ntrius_f"])
        self.ve(lambda e: e.tensor_copy(self.ntrius[:], t1[:]),
                reads=["ntrius_f"], writes=["ntrius"], tag="ntuc")
        self.ntril = self.sbP("ntril", [P, P], BF16)
        t2 = self.sbP("ntril_f", [P, P], F32)
        self.dma(t2[:], self.c_ntril_h[:], reads=[], writes=["ntril_f"])
        self.ve(lambda e: e.tensor_copy(self.ntril[:], t2[:]),
                reads=["ntril_f"], writes=["ntril"], tag="ntlc")
        self.i64t = self.sbP("i64t", [64, 1024], F32)
        self.dma(self.i64t[:], self.c_i64t_h[:], reads=[], writes=["i64t"])
        al = self.rowload(self.W["bw_Alog"], 16, "bal_row")
        nA = self.sbP("bnegA_row", [1, 16], F32)
        self.se(lambda e: e.activation(nA[:], al[:], AF.Exp, bias=0.0, scale=1.0),
                reads=[al.name], writes=["bnegA_row"], tag="bnA1")
        self.ve(lambda e: e.tensor_scalar(nA[:], nA[:], -1.0, None, ALU.mult),
                reads=["bnegA_row"], writes=["bnegA_row"], tag="bnA2")
        self.bnegA = self.bcast_row(nA[:], "bnegA_row", 16, "bnegA_b")
        dtb = self.rowload(self.W["bw_dtb"], 16, "bdtb_row")
        self.bdtb = self.bcast_row(dtb[:], dtb.name, 16, "bdtb_b")
        nr = self.rowload(self.W["bw_norm"], 128, "bnorm_row")
        self.bnorm = self.bcast_row(nr[:], nr.name, 128, "bnorm_b")
        self.bcw = self.sbP("bcw", [P, 32 * 4], F32)
        for i in range(8):
            self.dma(self.bcw[:, i * 4:(i + 1) * 4],
                     self.W["bw_convq"][i * P:(i + 1) * P, :], reads=[],
                     writes=["bcw"], tag="bcwl")
        for i in range(8):
            self.dma(self.bcw[:, 32 + i * 4:32 + (i + 1) * 4],
                     self.W["bw_convk"][i * P:(i + 1) * P, :], reads=[],
                     writes=["bcw"], tag="bcwl")
        for i in range(16):
            self.dma(self.bcw[:, 64 + i * 4:64 + (i + 1) * 4],
                     self.W["bw_convv"][i * P:(i + 1) * P, :], reads=[],
                     writes=["bcw"], tag="bcwl")

    def _ab_kron16(self):
        self.ve(lambda e: e.memset(self.rhs99b[0:96, :], 0.0),
                reads=[], writes=["rhs99b"], tag="rhsbz")
        self.ve(lambda e: e.memset(self.lhsT99b[0:96, :], 0.0),
                reads=[], writes=["lhsT99b"], tag="lhsbz")
        for rr in range(3):
            for h in range(16):
                self.dma(self.rhs99b[32 * rr + h:32 * rr + h + 1,
                                     h * P:(h + 1) * P],
                         self.scrb_hbm[3:4, 0:P],
                         reads=["scrb_nr"], writes=["rhs99b"], q="gpsimd",
                         tag="kron16")
        self.dma(self.lhsT99b[96:99, :], self.scrb_hbm[0:3, 0:P],
                 reads=["scrb3"], writes=["lhsT99b"], q="gpsimd")

    def _ab_ln(self):
        NT, NT1 = self.NT, self.NT + 1
        self.dma(self.ccb1_in[:, :], self.xres[NT - 1][125:128, :],
                 reads=[f"xres{NT-1}"], writes=["ccb1i"], q="gpsimd", tag="b_ccw")
        self.gp(lambda e: e.collective_compute(
            "AllGather", ALU.bypass, replica_groups=[list(range(self.NC))],
            ins=[self.ccb1_in.ap().opt()], outs=[self.ccb1_out.ap().opt()]),
            reads=["ccb1i"], writes=["ccb1o"], kind="cc", tag="b_ag")
        gath = self.sbP("b_gath", [self.NC * 3, D], F32)
        self.dma(gath[:], self.ccb1_out[:, :], reads=["ccb1o"],
                 writes=["b_gath"], q="gpsimd", tag="b_ccr")
        ps3 = self.sbP("b_ps3", [self.NC * 3, 3], F32)
        self.dma(ps3[:], self.prevsel3_hbm[:, :], reads=[], writes=["b_ps3"])
        xh2 = self.sbP("b_xh2", [P, D], F32)
        hstage = self.sbP("b_hstage", [3, D], F32)
        scr2 = self.nc.dram_tensor("b_scr2", [3, D], F32)
        self.ve(lambda e: e.memset(xh2[:], 0.0), reads=[], writes=["b_xh2"],
                tag="bxz")
        for hh in range(2):
            self.pe(lambda e, hh=hh: e.matmul(
                self.pb[hh][0:3, :], ps3[:], gath[:, hh * 512:(hh + 1) * 512],
                start=True, stop=True),
                reads=["b_gath", "b_ps3"], writes=[f"pb{hh}"], tag="bhsel")
            self.ve(lambda e, hh=hh: e.tensor_copy(
                hstage[0:3, hh * 512:(hh + 1) * 512], self.pb[hh][0:3, :]),
                reads=[f"pb{hh}"], writes=["b_hstage"], tag="bhev")
        self.dma(scr2[:, :], hstage[:], reads=["b_hstage"], writes=["b_scr2"],
                 q="gpsimd", tag="bhscw")
        self.dma(xh2[125:128, :], scr2[:, :], reads=["b_scr2", "b_xh2"],
                 writes=["b_xh2"], q="gpsimd", tag="bhscr")
        tiles = [(xh2, "b_xh2")] + [(self.xres[t], f"xres{t}") for t in range(NT)]
        self.ln_stats(tiles)
        xln = self.sbP("b_xln", [P, D], BF16)
        for t in range(NT1):
            xt, xkey = tiles[t]
            self.ln_apply(xt, xkey, t, xln[:], xln.name)
            for kk in range(8):
                self.transpose_to(xln[:, kk * P:(kk + 1) * P], xln.name,
                                  self.x2lnT[:, kk * self.TW2 + t * P:
                                             kk * self.TW2 + (t + 1) * P],
                                  f"x2lnT_t{t}")

    def _ab_proj(self):
        NT, NT1, Tc, TW2 = self.NT, self.NT + 1, self.Tc, self.TW2
        grps = [(g * 512, min(TW2, (g + 1) * 512))
                for g in range((TW2 + 511) // 512)]
        xkeys = [f"x2lnT_t{t}" for t in range(NT1)]
        wbuf = [self.sbP(f"bw_d{i}", [P, 1024], BF16) for i in range(2)]
        wld = [self.sbP(f"bw_ld{i}", [P, 1024], F32) for i in range(2)]
        stage = self.sbP("b_stage", [P, TW2], BF16)
        ct1 = self.sbP("b_ct1", [P, Tc], BF16)
        ct2 = self.sbP("b_ct2", [P, Tc], BF16)
        cf = self.sbP("b_cf", [P, Tc], BF16)
        sq = self.sbP("b_sq", [P, Tc], BF16)
        ssb2 = self.sbP("b_ss2", [2, Tc], F32)
        rst = self.sbP("b_rst", [2, Tc], F32)
        stg = self.sbP("b_stg", [P, 512], BF16)
        e6 = self.fconst(1e-6)
        lg8 = self.fconst(float(np.log(1.0 / 8.0)))

        def wsrc(dc):
            if dc < 8:
                return self.W["bw_q"], dc, "q"
            if dc < 16:
                return self.W["bw_k"], dc - 8, "k"
            return self.W["bw_v"], dc - 16, "v"

        for dc in range(32):
            hbm, dcl, which = wsrc(dc)
            b = dc % 2
            self.dma(wld[b][:].rearrange("r (kk c) -> r kk c", kk=8),
                     hbm[:, dcl * P:(dcl + 1) * P].rearrange(
                         "(kk r) c -> r kk c", r=P),
                     reads=[f"bwld_used{b}"], writes=[f"bw_ld{b}"], tag="bwl")
            self.ve(lambda e, b=b: e.tensor_copy(wbuf[b][:], wld[b][:]),
                    reads=[f"bw_ld{b}"], writes=[f"bw_d{b}", f"bwld_used{b}"],
                    tag="bwc")
            for gi, (lo, hi) in enumerate(grps):
                pbk = self.pb[gi % 2]
                pk = f"pb{gi%2}"
                g = self.nm("gbp")
                for kk in range(8):
                    self.pe(lambda e, b=b, kk=kk, lo=lo, hi=hi, pbk=pbk: e.matmul(
                        pbk[0:P, 0:hi - lo], wbuf[b][:, kk * P:(kk + 1) * P],
                        self.x2lnT[:, kk * TW2 + lo:kk * TW2 + hi],
                        start=(kk == 0), stop=(kk == 7)),
                        reads=[f"bw_d{b}"] + xkeys, writes=[pk], tag="mm_bp",
                        group=g)
                self.ve(lambda e, lo=lo, hi=hi, pbk=pbk: e.tensor_copy(
                    stage[:, lo:hi], pbk[0:P, 0:hi - lo]),
                    reads=[pk], writes=["b_stage"], tag="bpev")
            cwc = [self.bcw[:, dc * 4 + k:dc * 4 + k + 1] for k in range(4)]
            self.ve(lambda e, cwc=cwc: e.tensor_scalar(
                ct1[:], stage[:, 125:125 + Tc], cwc[0], None, ALU.mult),
                reads=["b_stage", "bcw"], writes=["b_ct1"], tag="bcv0")
            self.ve(lambda e, cwc=cwc: e.scalar_tensor_tensor(
                ct2[:], stage[:, 126:126 + Tc], cwc[1], ct1[:], ALU.mult, ALU.add),
                reads=["b_stage", "bcw", "b_ct1"], writes=["b_ct2"], tag="bcv1")
            self.ve(lambda e, cwc=cwc: e.scalar_tensor_tensor(
                ct1[:], stage[:, 127:127 + Tc], cwc[2], ct2[:], ALU.mult, ALU.add),
                reads=["b_stage", "bcw", "b_ct2"], writes=["b_ct1"], tag="bcv2")
            self.ve(lambda e, cwc=cwc: e.scalar_tensor_tensor(
                ct2[:], stage[:, 128:128 + Tc], cwc[3], ct1[:], ALU.mult, ALU.add),
                reads=["b_stage", "bcw", "b_ct1"], writes=["b_ct2"], tag="bcv3")
            self.se(lambda e: e.activation(cf[:], ct2[:], AF.Silu, bias=0.0,
                                           scale=1.0),
                    reads=["b_ct2"], writes=["b_cf"], tag="bcsilu")
            if which in ("q", "k"):
                self.se(lambda e: e.activation(sq[:], cf[:], AF.Square, bias=0.0,
                                               scale=1.0),
                        reads=["b_cf"], writes=["b_sq"], tag="bsq")
                for c0 in range(0, Tc, 512):
                    cw = min(512, Tc - c0)
                    self.pe(lambda e, c0=c0, cw=cw: e.matmul(
                        self.pb[2][0:2, 0:cw], self.half01[:], sq[:, c0:c0 + cw],
                        start=True, stop=True),
                        reads=["b_sq", "half01"], writes=["pb2"], tag="mm_ss")
                    self.ve(lambda e, c0=c0, cw=cw: e.tensor_copy(
                        ssb2[0:2, c0:c0 + cw], self.pb[2][0:2, 0:cw]),
                        reads=["pb2"], writes=["b_ss2"], tag="ssev")
                self.se(lambda e: e.activation(rst[:], ssb2[:], AF.Ln,
                                               bias=e6[0:2, 0:1], scale=1.0),
                        reads=["b_ss2", e6.name], writes=["b_rst"], tag="bl2ln")
                if which == "q":
                    self.se(lambda e: e.activation(rst[:], rst[:], AF.Exp,
                                                   bias=lg8[0:2, 0:1],
                                                   scale=-0.5),
                            reads=["b_rst", lg8.name], writes=["b_rst"],
                            tag="bl2exq")
                else:
                    self.se(lambda e: e.activation(rst[:], rst[:], AF.Exp,
                                                   bias=0.0, scale=-0.5),
                            reads=["b_rst"], writes=["b_rst"], tag="bl2exk")
                for c0 in range(0, Tc, 512):
                    cw = min(512, Tc - c0)
                    self.pe(lambda e, c0=c0, cw=cw: e.matmul(
                        self.pb[3][0:P, 0:cw], self.sel2f[:],
                        rst[0:2, c0:c0 + cw], start=True, stop=True),
                        reads=["b_rst", "sel2f"], writes=["pb3"], tag="mm_rsb")
                    self.ve(lambda e, c0=c0, cw=cw: e.tensor_tensor(
                        cf[:, c0:c0 + cw], cf[:, c0:c0 + cw],
                        self.pb[3][0:P, 0:cw], ALU.mult),
                        reads=["pb3", "b_cf"], writes=["b_cf"], tag="bqks")
                dst = self.qf_hbm if which == "q" else self.kf_hbm
                self.dma(dst[dcl, :, :], cf[:], reads=["b_cf"],
                         writes=["qf_sp" if which == "q" else "kf_sp"],
                         q="gpsimd", tag="bqk_sp")
                if which == "k":
                    for t0 in range(0, NT, 4):
                        n4 = min(4, NT - t0)
                        for i in range(n4):
                            tp = self.tpb4[i % 2][:]
                            self.pe(lambda e, tp=tp, t=t0 + i: e.transpose(
                                tp[:], cf[:, t * P:(t + 1) * P], self.id_bf[:]),
                                reads=["b_cf", "id_bf"], writes=[f"tpb4_{i % 2}"],
                                tag="bkT")
                            self.ve(lambda e, tp=tp, i=i: e.tensor_copy(
                                stg[:, i * P:(i + 1) * P], tp),
                                reads=[f"tpb4_{i % 2}"], writes=["b_stg"],
                                tag="bkTe")
                        for i in range(n4):
                            t_abs = t0 + i
                            self.dma(self.kt_hbm[t_abs * P:(t_abs + 1) * P,
                                                 dcl * P:(dcl + 1) * P],
                                     stg[:, i * P:(i + 1) * P],
                                     reads=["b_stg"], writes=["kt_sp"],
                                     q="gpsimd", tag="bkt_sp")
                            for hf in range(2):
                                h = dcl * 2 + hf
                                self.dma(
                                    self.uw_hbm[t_abs * P:(t_abs + 1) * P,
                                                h * 192 + 128:h * 192 + 192],
                                    stg[:, i * P + hf * 64:i * P + hf * 64 + 64],
                                    reads=["b_stg"], writes=["uw_sp"],
                                    q="gpsimd", tag="bku_sp")
            else:
                for t0 in range(0, NT, 4):
                    n4 = min(4, NT - t0)
                    for i in range(n4):
                        tp = self.tpb4[i % 2][:]
                        self.pe(lambda e, tp=tp, t=t0 + i: e.transpose(
                            tp[:], cf[:, t * P:(t + 1) * P], self.id_bf[:]),
                            reads=["b_cf", "id_bf"], writes=[f"tpb4_{i % 2}"],
                            tag="bvT")
                        self.ve(lambda e, tp=tp, i=i: e.tensor_copy(
                            stg[:, i * P:(i + 1) * P], tp),
                            reads=[f"tpb4_{i % 2}"], writes=["b_stg"], tag="bvTe")
                    for i in range(n4):
                        self.dma(self.uw_hbm[(t0 + i) * P:(t0 + i + 1) * P,
                                             dcl * 192:dcl * 192 + 128],
                                 stg[:, i * P:(i + 1) * P],
                                 reads=["b_stg"], writes=["uw_sp"], q="gpsimd",
                                 tag="bv_sp")
        self._ab_ab()

    def _ab_ab(self):
        NT, TW2 = self.NT, self.TW2
        wab = self.sbP("b_wab", [P, 8 * 32], BF16)
        wab_f = self.sbP("b_wabf", [P, 32], F32)
        for kk in range(8):
            self.dma(wab_f[:, 0:16], self.W["bw_a"][kk * P:(kk + 1) * P, :],
                     reads=["b_wabf"], writes=["b_wabf"], tag="babl")
            self.dma(wab_f[:, 16:32], self.W["bw_b"][kk * P:(kk + 1) * P, :],
                     reads=["b_wabf"], writes=["b_wabf"], tag="babl")
            self.ve(lambda e, kk=kk: e.tensor_copy(wab[:, kk * 32:(kk + 1) * 32],
                                                   wab_f[:]),
                    reads=["b_wabf"], writes=["b_wab"], tag="babc")
        abp = self.sbP("b_abp", [P, NT * 32], F32)
        for t in range(NT):
            g = self.nm("gab")
            for kk in range(8):
                self.pe(lambda e, kk=kk, t=t: e.matmul(
                    self.pb[2][0:P, 0:32],
                    self.x2lnT[:, kk * TW2 + (t + 1) * P:kk * TW2 + (t + 2) * P],
                    wab[:, kk * 32:(kk + 1) * 32], start=(kk == 0), stop=(kk == 7)),
                    reads=["b_wab", f"x2lnT_t{t+1}"], writes=["pb2"], tag="mm_ab",
                    group=g)
            self.ve(lambda e, t=t: e.tensor_copy(
                abp[:, t * 32:(t + 1) * 32], self.pb[2][0:P, 0:32]),
                reads=["pb2"], writes=["b_abp"], tag="abev")
        av = abp[:].rearrange("p (t two h) -> p t two h", two=2, h=16)[:, :, 0:1, :]
        bv = abp[:].rearrange("p (t two h) -> p t two h", two=2, h=16)[:, :, 1:2, :]
        bview = self.ab_beta[:].rearrange("p (t h) -> p t h", h=16).unsqueeze(2)
        self.se(lambda e: e.activation(bview, bv, AF.Sigmoid, bias=0.0, scale=1.0),
                reads=["b_abp"], writes=["ab_beta"], tag="bsig")
        self.ve(lambda e: e.tensor_scalar(self.ab_beta[:], self.ab_beta[:], 2.0,
                                          None, ALU.mult),
                reads=["ab_beta"], writes=["ab_beta"], tag="b2x")
        gt = self.sbP("b_gt", [P, NT * 16], F32)
        self.ve(lambda e: e.tensor_tensor(
            gt[:].rearrange("p (t h) -> p t h", h=16).unsqueeze(2), av,
            self.bdtb[:].unsqueeze(1).unsqueeze(2).broadcast_to((P, NT, 1, 16)),
            ALU.add),
            reads=["b_abp", "bdtb_b"], writes=["b_gt"], tag="bgb")
        ab_ = self.sbP("b_gabs", [P, NT * 16], F32)
        self.se(lambda e: e.activation(ab_[:], gt[:], AF.Abs, bias=0.0,
                                       scale=1.0),
                reads=["b_gt"], writes=["b_gabs"], tag="bgab")
        one = self.fconst(1.0)
        self.se(lambda e: e.activation(ab_[:], ab_[:], AF.Exp, bias=0.0, scale=-1.0),
                reads=["b_gabs"], writes=["b_gabs"], tag="bgex")
        self.se(lambda e: e.activation(ab_[:], ab_[:], AF.Ln, bias=one[:], scale=1.0),
                reads=["b_gabs", one.name], writes=["b_gabs"], tag="bgln")
        rl = self.sbP("b_grl", [P, NT * 16], F32)
        self.ve(lambda e: e.tensor_scalar(rl[:], gt[:], 0.0, None, ALU.max),
                reads=["b_gt"], writes=["b_grl"], tag="bgrl")
        self.ve(lambda e: e.tensor_tensor(gt[:], rl[:], ab_[:], ALU.add),
                reads=["b_grl", "b_gabs"], writes=["b_gt"], tag="bgsp")
        gv = self.sbP("b_g", [P, NT * 16], F32)
        self.ve(lambda e: e.tensor_tensor(
            gv[:].rearrange("p (t h) -> p t h", h=16),
            gt[:].rearrange("p (t h) -> p t h", h=16),
            self.bnegA[:].unsqueeze(1).broadcast_to((P, NT, 16)), ALU.mult),
            reads=["b_gt", "bnegA_b"], writes=["b_g"], tag="bgmk")
        for ch in range(NT):
            s3 = self._tps % 4
            self._tps += 1
            tp = self.tpf[:, 128 * s3:128 * s3 + 128]
            tk = "tpfB"
            self.pe(lambda e, tp=tp, ch=ch: e.matmul(
                tp[0:P, 0:16], self.tri_f32[:], gv[:, ch * 16:(ch + 1) * 16],
                start=True, stop=True),
                reads=["b_g", "tri_f32"], writes=[tk], tag="bcs_mm")
            self.ve(lambda e, tp=tp, ch=ch: e.tensor_copy(
                self.ab_G[:, ch * 16:(ch + 1) * 16], tp[0:P, 0:16]),
                reads=[tk], writes=["ab_G"], tag="bcs_ev")
            s4 = self._tps % 4
            self._tps += 1
            tp2 = self.tpf[:, 128 * s4:128 * s4 + 128]
            tk2 = "tpfB"
            self.pe(lambda e, tp2=tp2, ch=ch: e.matmul(
                tp2[0:1, 0:16], self.sel_last[:],
                self.ab_G[:, ch * 16:(ch + 1) * 16], start=True, stop=True),
                reads=["ab_G", "sel_last"], writes=[tk2], tag="bcse_mm")
            self.ve(lambda e, tp2=tp2, ch=ch: e.tensor_copy(
                self.ab_CSE[0:1, ch * 16:(ch + 1) * 16], tp2[0:1, 0:16]),
                reads=[tk2], writes=["ab_CSE"], tag="bcse_ev")
            s5 = self._tps % 4
            self._tps += 1
            tp3 = self.tpf[:, 128 * s5:128 * s5 + 128]
            tk3 = "tpfB"
            self.pe(lambda e, tp3=tp3, ch=ch: e.matmul(
                tp3[0:P, 0:16], self.ones_row[:],
                self.ab_CSE[0:1, ch * 16:(ch + 1) * 16], start=True, stop=True),
                reads=["ab_CSE", "ones_row"], writes=[tk3], tag="bdd_bc")
            self.ve(lambda e, tp3=tp3, ch=ch: e.tensor_tensor(
                self.ab_Dbdec[:, ch * 16:(ch + 1) * 16], tp3[0:P, 0:16],
                self.ab_G[:, ch * 16:(ch + 1) * 16], ALU.subtract),
                reads=[tk3, "ab_G"], writes=["ab_Dbdec"], tag="bdd_sub")
        self.se(lambda e: e.activation(self.ab_Dbdec[:], self.ab_Dbdec[:], AF.Exp,
                                       bias=0.0, scale=1.0),
                reads=["ab_Dbdec"], writes=["ab_Dbdec"], tag="bdd_exp")
        self.se(lambda e: e.activation(self.ab_LamT[:], self.ab_G[:], AF.Exp,
                                       bias=0.0, scale=1.0),
                reads=["ab_G"], writes=["ab_LamT"], tag="bLam")
        eC = self.sbP("b_eCSE", [1, NT * 16], F32)
        self.se(lambda e: e.activation(eC[:], self.ab_CSE[:], AF.Exp, bias=0.0,
                                       scale=1.0),
                reads=["ab_CSE"], writes=["b_eCSE"], tag="becse")
        nb = (NT * 16 + 511) // 512
        for bn in range(nb):
            lo, hi = bn * 512, min(NT * 16, (bn + 1) * 512)
            pbk = self.pb[bn % 2]
            self.pe(lambda e, lo=lo, hi=hi, pbk=pbk: e.matmul(
                pbk[0:P, 0:hi - lo], self.ones_row[:], eC[0:1, lo:hi],
                start=True, stop=True),
                reads=["b_eCSE", "ones_row"], writes=[f"pb{bn%2}"], tag="beG_bc")
            self.ve(lambda e, lo=lo, hi=hi, pbk=pbk: e.tensor_copy(
                self.ab_eG[:, lo:hi], pbk[0:P, 0:hi - lo]),
                reads=[f"pb{bn%2}"], writes=["ab_eG"], tag="beG_ev")

    def _ab_gg_wout(self):
        NT, TW2 = self.NT, self.TW2
        zw = [self.sbP(f"b_zw{i}", [P, 512], BF16) for i in range(2)]
        zf = self.sbP("b_zf", [P, 512], F32)
        zs = self.sbP("b_zsil", [P, 512], BF16)
        ngw = 0
        for tg in range((NT + 3) // 4):
            tiles = [tg * 4 + i for i in range(4) if tg * 4 + i < NT]
            for zg in range(4):
                g = self.nm("gz2")
                for kk in range(8):
                    b = ngw % 2
                    ngw += 1
                    self.dma(zf[:], self.W["bw_g"][kk * P:(kk + 1) * P,
                                                   zg * 512:(zg + 1) * 512],
                             reads=["b_zf"], writes=["b_zf"], tag="ggwl")
                    self.ve(lambda e, b=b: e.tensor_copy(zw[b][:], zf[:]),
                            reads=["b_zf"], writes=[f"b_zw{b}"], tag="ggwc")
                    for i, t in enumerate(tiles):
                        self.pe(lambda e, b=b, kk=kk, t=t, i=i: e.matmul(
                            self.pb[i][:],
                            self.x2lnT[:, kk * TW2 + (t + 1) * P:
                                       kk * TW2 + (t + 2) * P],
                            zw[b][:], start=(kk == 0), stop=(kk == 7)),
                            reads=[f"b_zw{b}", f"x2lnT_t{t+1}"],
                            writes=[f"pb{i}"], tag="mm_gg", group=g + str(i))
                for i, t in enumerate(tiles):
                    self.se(lambda e, i=i: e.activation(
                        zs[:], self.pb[i][:], AF.Silu, bias=0.0, scale=1.0),
                        reads=[f"pb{i}"], writes=["b_zsil"], tag="ggsilu")
                    self.dma(self.gg_hbm[t * P:(t + 1) * P,
                                         zg * 512:(zg + 1) * 512],
                             zs[:], reads=["b_zsil"], writes=["gg_sp"],
                             q="gpsimd", tag="ggspill")
        wo_f = self.sbP("b_wof", [P, 1024], F32)
        wo_b = self.sbP("b_wob", [P, 1024], BF16)
        for kk in range(16):
            self.dma(wo_f[:], self.W["bw_out"][kk * P:(kk + 1) * P, :],
                     reads=["b_wof"], writes=["b_wof"], tag="wol")
            self.ve(lambda e: e.tensor_copy(wo_b[:], wo_f[:]),
                    reads=["b_wof"], writes=["b_wob"], tag="woc")
            self.dma(self.wout_hbm[kk, :, :], wo_b[:], reads=["b_wob"],
                     writes=["wout_sp", "b_wof"], q="gpsimd", tag="wosp")

    def _ab_logD(self, ch, orientA=True, orientB=True):
        csp, hi, mid, lo = self.s_csp, self.s_hi, self.s_mid, self.s_lo
        self.ve(lambda e, ch=ch: e.tensor_copy(
            csp[:], self.ab_G[:, ch * 16:(ch + 1) * 16]),
            reads=["ab_G"], writes=["s_csp"], tag="bcsp")
        self.ve(lambda e: e.tensor_copy(hi[:], csp[:]),
                reads=["s_csp"], writes=["s_hi"], tag="bhi")
        self.ve(lambda e: e.tensor_tensor(csp[:], csp[:], hi[:], ALU.subtract),
                reads=["s_csp", "s_hi"], writes=["s_csp"], tag="br1")
        self.ve(lambda e: e.tensor_copy(mid[:], csp[:]),
                reads=["s_csp"], writes=["s_mid"], tag="bmid")
        self.ve(lambda e: e.tensor_tensor(csp[:], csp[:], mid[:], ALU.subtract),
                reads=["s_csp", "s_mid"], writes=["s_csp"], tag="br2")
        self.ve(lambda e: e.tensor_copy(lo[:], csp[:]),
                reads=["s_csp"], writes=["s_lo"], tag="blo")
        for rr, (t_bf, t_key) in enumerate(
                ((hi, "s_hi"), (mid, "s_mid"), (lo, "s_lo"))):
            tpb = self.tpb4[rr % 2][:]
            self.pe(lambda e, tpb=tpb, t_bf=t_bf: e.transpose(
                tpb[0:16, 0:P], t_bf[:], self.id_bf[:]),
                reads=[t_key, "id_bf"], writes=[f"tpb4_{rr % 2}"], tag="bcsT")
            self.ve(lambda e, tpb=tpb, rr=rr: e.tensor_copy(
                self.lhsT99b[32 * rr:32 * rr + 16, :], tpb[0:16, 0:P]),
                reads=[f"tpb4_{rr % 2}"], writes=["lhsT99b"], tag="bcsTe")
            self.dma(self.scrb_hbm[4 + rr:5 + rr, 0:2048].rearrange(
                "a (h l) -> (a h) l", h=16),
                self.lhsT99b[32 * rr:32 * rr + 16, :],
                reads=["lhsT99b"], writes=[f"bfl{rr}"], q="gpsimd", tag="bflw")
            self.dma(self.rhs99b[96 + rr:97 + rr, :],
                     self.scrb_hbm[4 + rr:5 + rr, 0:2048],
                     reads=[f"bfl{rr}"], writes=["rhs99b"], q="gpsimd",
                     tag="bflr")
        g = self.nm("gld")
        for q in range(4):
            self.pe(lambda e, q=q: e.matmul(
                self.pb[q][:], self.lhsT99b[:],
                self.rhs99b[:, q * 512:(q + 1) * 512], start=True, stop=True),
                reads=["lhsT99b", "rhs99b"], writes=[f"pb{q}"], tag="mm_ld",
                group=g)
        if orientA:
            for q in range(4):
                self.ve(lambda e, q=q: e.tensor_scalar(
                    self.s_eAt[:, q * 512:(q + 1) * 512], self.pb[q][:], 0.0,
                    None, ALU.min),
                    reads=[f"pb{q}"], writes=["s_eAt"], tag="ldAev")
            self.se(lambda e: e.activation(self.s_eAt[:], self.s_eAt[:], AF.Exp,
                                           bias=0.0, scale=1.0),
                    reads=["s_eAt"], writes=["s_eAt"], tag="ldAexp")
        if orientB:
            for q in range(4):
                self.ve(lambda e, q=q: e.tensor_scalar(
                    self.s_eBt[:, q * 512:(q + 1) * 512], self.pb[q][:], -1.0,
                    0.0, ALU.mult, ALU.min),
                    reads=[f"pb{q}"], writes=["s_eBt"], tag="ldBev")
            self.se(lambda e: e.activation(self.s_eBt[:], self.s_eBt[:], AF.Exp,
                                           bias=0.0, scale=1.0),
                    reads=["s_eBt"], writes=["s_eBt"], tag="ldBexp")

    def _ab_solve(self):
        NT, Tc = self.NT, self.Tc
        kfc = self.sbP("s_kfc", [64, 2048], BF16)
        ktc = self.sbP("s_ktc", [P, 1024], BF16)
        kbt = self.sbP("s_kbt", [P, 1024], BF16)
        kbf = self.sbP("s_kbf", [64, 2048], BF16)
        U1 = self.sbP("s_U1", [P, 2048], BF16)
        U2 = self.sbP("s_U2", [P, 1024], BF16)
        Bh = [self.sbP(f"s_Bh{i}", [P, 2048], BF16) for i in range(2)]
        Bu = [self.sbP(f"s_Bu{i}", [P, 2048], BF16) for i in range(2)]
        Kt = self.sbP("s_Kt", [P, 1024], BF16)
        bl = self.sbP("s_bl", [P, 16], F32)
        vt = [self.sbP(f"s_vt{i}", [P, 3072], BF16) for i in range(2)]
        self.s_pcT = self.sbP("s_pcT", [64, 1024], F32)
        self.s_pcU = self.sbP("s_pcU", [64, 1024], F32)
        self.s_lamI = self.sbP("s_lamI", [64, 1024], F32)
        self.s_pcTb = self.sbP("s_pcTb", [64, 1024], BF16)
        self.s_Pub = self.sbP("s_Pub", [64, 1024], BF16)
        self.s_Qrb = self.sbP("s_Qrb", [64, 2048], BF16)
        self.s_Pr = self.sbP("s_Pr", [64, 1024], F32)
        self.s_Pu = self.sbP("s_Pu", [64, 1024], F32)
        self.s_Qr = self.sbP("s_Qr", [64, 2048], F32)

        for ch in range(NT):
            b = ch % 2
            for h2 in range(2):
                self.dma(kfc[:].rearrange("p (d t) -> p d t", t=256)[
                    :, :, h2 * 128:(h2 + 1) * 128],
                    self.kf_hbm[:, h2 * 64:h2 * 64 + 64,
                                ch * P:(ch + 1) * P].rearrange("d p t -> p d t"),
                    reads=["kf_sp", "s_kfc"], writes=["s_kfc"], tag="skf")
            self.dma(ktc[:], self.kt_hbm[ch * P:(ch + 1) * P, :],
                     reads=["kt_sp", "s_ktc"], writes=["s_ktc"], tag="skt")
            self.dma(vt[b][:], self.uw_hbm[ch * P:(ch + 1) * P, :],
                     reads=["uw_sp", f"s_vt{b}"], writes=[f"s_vt{b}"], tag="svt")
            self.ve(lambda e, ch=ch: e.tensor_tensor(
                kbt[:].rearrange("p (h k) -> p h k", h=16),
                ktc[:].rearrange("p (h k) -> p h k", h=16),
                self.ab_beta[:, ch * 16:(ch + 1) * 16].unsqueeze(2).broadcast_to(
                    (P, 16, 64)), ALU.mult),
                reads=["s_ktc", "ab_beta"], writes=["s_kbt"], tag="skbt")
            self.ve(lambda e, ch=ch: e.tensor_tensor(
                bl[:], self.ab_beta[:, ch * 16:(ch + 1) * 16],
                self.ab_LamT[:, ch * 16:(ch + 1) * 16], ALU.mult),
                reads=["ab_beta", "ab_LamT"], writes=["s_bl"], tag="sbl")
            self.ve(lambda e: e.tensor_tensor(
                U2[:].rearrange("p (h k) -> p h k", h=16),
                ktc[:].rearrange("p (h k) -> p h k", h=16),
                bl[:].unsqueeze(2).broadcast_to((P, 16, 64)), ALU.mult),
                reads=["s_ktc", "s_bl"], writes=["s_U2"], tag="sU2")
            self.ve(lambda e, ch=ch, b=b: e.tensor_tensor(
                U1[:].rearrange("p (h k) -> p h k", h=16),
                vt[b][:].rearrange("p (h c) -> p h c", c=192)[:, :, 0:128],
                self.ab_beta[:, ch * 16:(ch + 1) * 16].unsqueeze(2).broadcast_to(
                    (P, 16, 128)), ALU.mult),
                reads=[f"s_vt{b}", "ab_beta"], writes=["s_U1"], tag="sU1")
            self.ve(lambda e, ch=ch: e.tensor_tensor(
                Kt[:].rearrange("p (h k) -> p h k", h=16),
                ktc[:].rearrange("p (h k) -> p h k", h=16),
                self.ab_Dbdec[:, ch * 16:(ch + 1) * 16].unsqueeze(2).broadcast_to(
                    (P, 16, 64)), ALU.mult),
                reads=["s_ktc", "ab_Dbdec"], writes=["s_Kt"], tag="sKt")
            for i in range(8):
                sl = i % 2
                tp = self.tpb4[sl][:]
                self.pe(lambda e, tp=tp, i=i: e.transpose(
                    tp[:], kbt[:, i * P:(i + 1) * P], self.id_bf[:]),
                    reads=["s_kbt", "id_bf"], writes=[f"tpb4_{sl}"], tag="skbT")
                self.ve(lambda e, tp=tp, i=i: e.tensor_copy(
                    kbf[0:64, (2 * i) * P:(2 * i + 1) * P], tp[0:64, :]),
                    reads=[f"tpb4_{sl}"], writes=["s_kbf"], tag="skbTe")
                self.ve(lambda e, tp=tp, i=i: e.tensor_copy(
                    kbf[0:64, (2 * i + 1) * P:(2 * i + 2) * P], tp[64:128, :]),
                    reads=[f"tpb4_{sl}"], writes=["s_kbf"], tag="skbTe2")
            self._ab_logD(ch)
            self._ab_kkt(kfc, kbf, Bh[0], Bu[0])
            for j in range(7):
                cj, nj = j % 2, (j + 1) % 2
                for hg in range(2):
                    g1 = self.nm("gbu")
                    for hh in range(8):
                        h = hg * 8 + hh
                        self.pe(lambda e, h=h, hh=hh, cj=cj: e.matmul(
                            self.pb[hh // 4][:, (hh % 4) * 128:(hh % 4) * 128 + 128],
                            Bh[cj][:, h * P:(h + 1) * P], U1[:, h * P:(h + 1) * P],
                            start=True, stop=True),
                            reads=[f"s_Bh{cj}", "s_U1"], writes=[f"pb{hh//4}"],
                            tag="mm_bu1", group=g1 + str(hh // 4))
                        self.pe(lambda e, h=h, hh=hh, cj=cj: e.matmul(
                            self.pb[2][:, hh * 64:hh * 64 + 64],
                            Bh[cj][:, h * P:(h + 1) * P],
                            U2[:, h * 64:(h + 1) * 64],
                            start=True, stop=True),
                            reads=[f"s_Bh{cj}", "s_U2"], writes=["pb2"],
                            tag="mm_bu2", group=g1 + "u")
                    for qq in range(2):
                        self.ve(lambda e, hg=hg, qq=qq: e.tensor_tensor(
                            U1[:, hg * 1024 + qq * 512:hg * 1024 + (qq + 1) * 512],
                            U1[:, hg * 1024 + qq * 512:hg * 1024 + (qq + 1) * 512],
                            self.pb[qq][:], ALU.add),
                            reads=[f"pb{qq}", "s_U1"], writes=["s_U1"],
                            tag="bu1ev")
                    self.ve(lambda e, hg=hg: e.tensor_tensor(
                        U2[:, hg * 512:(hg + 1) * 512],
                        U2[:, hg * 512:(hg + 1) * 512], self.pb[2][:], ALU.add),
                        reads=["pb2", "s_U2"], writes=["s_U2"], tag="bu2ev")
                if j == 6:
                    break
                for r0 in range(4):
                    g3 = self.nm("gsq")
                    for hh in range(4):
                        h = r0 * 4 + hh
                        self.pe(lambda e, h=h, hh=hh, cj=cj: e.matmul(
                            self.pb[hh][:, 0:128],
                            Bu[cj][:, h * P:(h + 1) * P],
                            Bh[cj][:, h * P:(h + 1) * P],
                            start=True, stop=True),
                            reads=[f"s_Bu{cj}", f"s_Bh{cj}"],
                            writes=[f"pb{hh}"], tag="mm_sqh", group=g3)
                        self.pe(lambda e, h=h, hh=hh, cj=cj: e.matmul(
                            self.pb[hh][:, 128:256],
                            Bh[cj][:, h * P:(h + 1) * P],
                            Bu[cj][:, h * P:(h + 1) * P],
                            start=True, stop=True),
                            reads=[f"s_Bu{cj}", f"s_Bh{cj}"],
                            writes=[f"pb{hh}"], tag="mm_squ", group=g3)
                    for hh in range(4):
                        h = r0 * 4 + hh
                        self.ve(lambda e, h=h, hh=hh, nj=nj: e.tensor_copy(
                            Bh[nj][:, h * P:(h + 1) * P], self.pb[hh][:, 0:128]),
                            reads=[f"pb{hh}"], writes=[f"s_Bh{nj}"], tag="sqhev")
                        self.ve(lambda e, h=h, hh=hh, nj=nj: e.tensor_copy(
                            Bu[nj][:, h * P:(h + 1) * P], self.pb[hh][:, 128:256]),
                            reads=[f"pb{hh}"], writes=[f"s_Bu{nj}"], tag="squev")
            self.ve(lambda e, b=b: e.tensor_copy(
                vt[b][:].rearrange("p (h c) -> p h c", c=192)[:, :, 0:128],
                U1[:].rearrange("p (h c) -> p h c", c=128)),
                reads=["s_U1", f"s_vt{b}"], writes=[f"s_vt{b}"], tag="suw1")
            self.ve(lambda e, b=b: e.tensor_copy(
                vt[b][:].rearrange("p (h c) -> p h c", c=192)[:, :, 128:192],
                U2[:].rearrange("p (h c) -> p h c", c=64)),
                reads=["s_U2", f"s_vt{b}"], writes=[f"s_vt{b}"], tag="suw2")
            self.dma(self.suw_hbm[ch * P:(ch + 1) * P, :], vt[b][:],
                     reads=[f"s_vt{b}"], writes=["suw_sp"], q="gpsimd",
                     tag="suw_sp")
            self.ve(lambda e, ch=ch: e.tensor_tensor(
                self.s_lamI[:].rearrange("p (h k) -> p h k", h=16),
                self.i64t[:].rearrange("p (h k) -> p h k", h=16),
                self.ab_eG[0:64, ch * 16:(ch + 1) * 16].unsqueeze(2).broadcast_to(
                    (64, 16, 64)), ALU.mult),
                reads=["i64t", "ab_eG"], writes=["s_lamI"], tag="lamI")
            for grp in range(2):
                gph = self.nm("gph")
                for hh in range(8):
                    h = grp * 8 + hh
                    self.pe(lambda e, h=h, hh=hh: e.matmul(
                        self.pb[hh // 2][0:64, (hh % 2) * 256:(hh % 2) * 256 + 64],
                        U2[:, h * 64:(h + 1) * 64], Kt[:, h * 64:(h + 1) * 64],
                        start=True, stop=True),
                        reads=["s_U2", "s_Kt"], writes=[f"pb{hh//2}"],
                        tag="mm_ph", group=gph)
                    self.pe(lambda e, h=h, hh=hh: e.matmul(
                        self.pb[hh // 2][0:64, (hh % 2) * 256 + 64:
                                         (hh % 2) * 256 + 128],
                        Kt[:, h * 64:(h + 1) * 64], U2[:, h * 64:(h + 1) * 64],
                        start=True, stop=True),
                        reads=["s_U2", "s_Kt"], writes=[f"pb{hh//2}"],
                        tag="mm_pu", group=gph)
                for hh in range(8):
                    h = grp * 8 + hh
                    base = (hh % 2) * 256
                    self.ve(lambda e, h=h, hh=hh, base=base:
                            e.scalar_tensor_tensor(
                                self.s_pcT[0:64, h * 64:(h + 1) * 64],
                                self.pb[hh // 2][0:64, base:base + 64], -1.0,
                                self.s_lamI[0:64, h * 64:(h + 1) * 64],
                                ALU.mult, ALU.add),
                            reads=[f"pb{hh//2}", "s_lamI"], writes=["s_pcT"],
                            tag="phev")
                    self.ve(lambda e, h=h, hh=hh, base=base:
                            e.scalar_tensor_tensor(
                                self.s_pcU[0:64, h * 64:(h + 1) * 64],
                                self.pb[hh // 2][0:64, base + 64:base + 128],
                                -1.0, self.s_lamI[0:64, h * 64:(h + 1) * 64],
                                ALU.mult, ALU.add),
                            reads=[f"pb{hh//2}", "s_lamI"], writes=["s_pcU"],
                            tag="puev")
            self._ab_compose(ch, Kt, U1, ch == 0)
        self.dma(self.ccb2_in[:, 0:1024], self.s_Pr[:], reads=["s_Pr"],
                 writes=["ccb2i"], q="gpsimd", tag="e3w1")
        self.dma(self.ccb2_in[:, 1024:3072], self.s_Qr[:], reads=["s_Qr"],
                 writes=["ccb2i2"], q="gpsimd", tag="e3w2")
        self.gp(lambda e: e.collective_compute(
            "AllGather", ALU.bypass, replica_groups=[list(range(self.NC))],
            ins=[self.ccb2_in.ap().opt()], outs=[self.ccb2_out.ap().opt()]),
            reads=["ccb2i", "ccb2i2"], writes=["ccb2o"], kind="cc", tag="e3ag")

    def _ab_kkt(self, kfc, kbf, B0h, B0u):
        for q in range(4):
            g = self.nm("gk1")
            for hh in range(4):
                h = q * 4 + hh
                self.pe(lambda e, h=h, hh=hh, q=q: e.matmul(
                    self.pb[q][:, hh * 128:(hh + 1) * 128],
                    kfc[0:64, h * P:(h + 1) * P], kbf[0:64, h * P:(h + 1) * P],
                    start=True, stop=True),
                    reads=["s_kfc", "s_kbf"], writes=[f"pb{q}"], tag="mm_kkA",
                    group=g)
            for hh in range(4):
                h = q * 4 + hh
                self.ve(lambda e, h=h, hh=hh, q=q: e.tensor_tensor(
                    B0h[:, h * 128:(h + 1) * 128],
                    self.pb[q][:, hh * 128:(hh + 1) * 128], self.ntrius[:],
                    ALU.mult),
                    reads=[f"pb{q}", "ntrius"], writes=["s_Bh0"], tag="kkAe1")
        self.ve(lambda e, B0h=B0h: e.tensor_tensor(B0h[:], B0h[:], self.s_eAt[:],
                                                   ALU.mult),
                reads=["s_Bh0", "s_eAt"], writes=["s_Bh0"], tag="kkAe2")
        for q in range(4):
            g = self.nm("gk2")
            for hh in range(4):
                h = q * 4 + hh
                self.pe(lambda e, h=h, hh=hh, q=q: e.matmul(
                    self.pb[q][:, hh * 128:(hh + 1) * 128],
                    kbf[0:64, h * P:(h + 1) * P], kfc[0:64, h * P:(h + 1) * P],
                    start=True, stop=True),
                    reads=["s_kfc", "s_kbf"], writes=[f"pb{q}"], tag="mm_kkB",
                    group=g)
            for hh in range(4):
                h = q * 4 + hh
                self.ve(lambda e, h=h, hh=hh, q=q: e.tensor_tensor(
                    B0u[:, h * 128:(h + 1) * 128],
                    self.pb[q][:, hh * 128:(hh + 1) * 128], self.ntril[:],
                    ALU.mult),
                    reads=[f"pb{q}", "ntril"], writes=["s_Bu0"], tag="kkBe1")
        self.ve(lambda e, B0u=B0u: e.tensor_tensor(B0u[:], B0u[:], self.s_eBt[:],
                                                   ALU.mult),
                reads=["s_Bu0", "s_eBt"], writes=["s_Bu0"], tag="kkBe2")

    def _ab_compose(self, ch, Kt, U1, first):
        for grp in range(2):
            g = self.nm("gqc")
            for hh in range(8):
                h = grp * 8 + hh
                if not first:
                    self.pe(lambda e, h=h, hh=hh: e.matmul(
                        self.pb[hh // 2][0:64,
                                         (hh % 2) * 256:(hh % 2) * 256 + 128],
                        self.s_pcTb[0:64, h * 64:(h + 1) * 64],
                        self.s_Qrb[0:64, h * 128:(h + 1) * 128],
                        start=True, stop=False),
                        reads=["s_pcTb", "s_Qrb"], writes=[f"pb{hh//2}"],
                        tag="mm_pq", group=g)
                self.pe(lambda e, h=h, hh=hh, first=first: e.matmul(
                    self.pb[hh // 2][0:64, (hh % 2) * 256:(hh % 2) * 256 + 128],
                    Kt[:, h * 64:(h + 1) * 64], U1[:, h * P:(h + 1) * P],
                    start=first, stop=True),
                    reads=["s_Kt", "s_U1"], writes=[f"pb{hh//2}"], tag="mm_qc",
                    group=g)
            for hh in range(8):
                h = grp * 8 + hh
                self.ve(lambda e, h=h, hh=hh: e.tensor_copy(
                    self.s_Qr[0:64, h * 128:(h + 1) * 128],
                    self.pb[hh // 2][0:64, (hh % 2) * 256:(hh % 2) * 256 + 128]),
                    reads=[f"pb{hh//2}"], writes=["s_Qr"], tag="qcev")
        if first:
            self.ve(lambda e: e.tensor_copy(self.s_Pr[:], self.s_pcT[:]),
                    reads=["s_pcT"], writes=["s_Pr"], tag="prcp")
            self.ve(lambda e: e.tensor_copy(self.s_Pu[:], self.s_pcU[:]),
                    reads=["s_pcU"], writes=["s_Pu"], tag="pucp")
        else:
            for grp in range(2):
                g = self.nm("gpr")
                for hh in range(8):
                    h = grp * 8 + hh
                    self.pe(lambda e, h=h, hh=hh: e.matmul(
                        self.pb[hh // 2][0:64,
                                         (hh % 2) * 256:(hh % 2) * 256 + 64],
                        self.s_Pub[0:64, h * 64:(h + 1) * 64],
                        self.s_pcTb[0:64, h * 64:(h + 1) * 64],
                        start=True, stop=True),
                        reads=["s_Pub", "s_pcTb"], writes=[f"pb{hh//2}"],
                        tag="mm_prT", group=g)
                    self.pe(lambda e, h=h, hh=hh: e.matmul(
                        self.pb[hh // 2][0:64, (hh % 2) * 256 + 64:
                                         (hh % 2) * 256 + 128],
                        self.s_pcTb[0:64, h * 64:(h + 1) * 64],
                        self.s_Pub[0:64, h * 64:(h + 1) * 64],
                        start=True, stop=True),
                        reads=["s_Pub", "s_pcTb"], writes=[f"pb{hh//2}"],
                        tag="mm_prU", group=g)
                for hh in range(8):
                    h = grp * 8 + hh
                    base = (hh % 2) * 256
                    self.ve(lambda e, h=h, hh=hh, base=base: e.tensor_copy(
                        self.s_Pr[0:64, h * 64:(h + 1) * 64],
                        self.pb[hh // 2][0:64, base:base + 64]),
                        reads=[f"pb{hh//2}"], writes=["s_Pr"], tag="prev")
                    self.ve(lambda e, h=h, hh=hh, base=base: e.tensor_copy(
                        self.s_Pu[0:64, h * 64:(h + 1) * 64],
                        self.pb[hh // 2][0:64, base + 64:base + 128]),
                        reads=[f"pb{hh//2}"], writes=["s_Pu"], tag="puev2")
        self.ve(lambda e: e.tensor_copy(self.s_pcTb[:], self.s_pcT[:]),
                reads=["s_pcT"], writes=["s_pcTb"], tag="pctc")
        self.ve(lambda e: e.tensor_copy(self.s_Pub[:], self.s_Pu[:]),
                reads=["s_Pu"], writes=["s_Pub"], tag="pubc")
        self.ve(lambda e: e.tensor_copy(self.s_Qrb[:], self.s_Qr[:]),
                reads=["s_Qr"], writes=["s_Qrb"], tag="qrbc")

    def _ab_sin(self):
        NC = self.NC
        gth = self.sbP("bs_gth", [64, 3072], F32)
        pmr = self.sbP("bs_pmr", [1, NC], F32)
        self.dma(pmr[:], self.prevmask_hbm.rearrange("a b -> b a"), reads=[],
                 writes=["bs_pmr"], tag="bpmr")
        pmb = self.bcast_row(pmr[:], "bs_pmr", NC, "bs_pmb")
        omb = self.sbP("bs_omb", [P, NC], F32)
        self.ve(lambda e: e.tensor_scalar(omb[:], pmb[:], -1.0, 1.0, ALU.mult,
                                          ALU.add),
                reads=["bs_pmb"], writes=["bs_omb"], tag="bomb")
        Pt = self.sbP("bs_Pt", [64, 1024], BF16)
        Qt = self.sbP("bs_Qt", [64, 2048], F32)
        Qtb = self.sbP("bs_Qtb", [64, 2048], BF16)
        Sb = self.sbP("bs_Sb", [64, 2048], BF16)
        imI = self.sbP("bs_imI", [64, 1024], F32)
        Ptf = self.sbP("bs_Ptf", [64, 1024], F32)
        self.ve(lambda e: e.memset(self.ab_S[:], 0.0), reads=[], writes=["ab_S"],
                tag="bsz")
        for j in range(NC):
            self.dma(gth[:], self.ccb2_out[j * 64:(j + 1) * 64, :],
                     reads=["ccb2o", "bs_gth"], writes=["bs_gth"], tag="be3r")
            self.ve(lambda e, j=j: e.tensor_scalar(
                Ptf[:], gth[:, 0:1024], pmb[0:64, j:j + 1], None, ALU.mult),
                reads=["bs_gth", "bs_pmb"], writes=["bs_Ptf"], tag="bptm")
            self.ve(lambda e, j=j: e.tensor_scalar(
                imI[:], self.i64t[:], omb[0:64, j:j + 1], None, ALU.mult),
                reads=["i64t", "bs_omb"], writes=["bs_imI"], tag="bimi")
            self.ve(lambda e: e.tensor_tensor(Ptf[:], Ptf[:], imI[:], ALU.add),
                    reads=["bs_Ptf", "bs_imI"], writes=["bs_Ptf"], tag="bpta")
            self.ve(lambda e: e.tensor_copy(Pt[:], Ptf[:]),
                    reads=["bs_Ptf"], writes=["bs_Pt"], tag="bptc")
            self.ve(lambda e, j=j: e.tensor_scalar(
                Qt[:], gth[:, 1024:3072], pmb[0:64, j:j + 1], None, ALU.mult),
                reads=["bs_gth", "bs_pmb"], writes=["bs_Qt"], tag="bqtm")
            self.ve(lambda e: e.tensor_copy(Sb[:], self.ab_S[:]),
                    reads=["ab_S"], writes=["bs_Sb"], tag="bsbc")
            g = self.nm("gsin")
            for hh in range(16):
                self.pe(lambda e, hh=hh: e.matmul(
                    self.pb[hh // 4][0:64, (hh % 4) * 128:(hh % 4) * 128 + 128],
                    Pt[:, hh * 64:(hh + 1) * 64], Sb[:, hh * 128:(hh + 1) * 128],
                    start=True, stop=True),
                    reads=["bs_Pt", "bs_Sb"], writes=[f"pb{hh//4}"],
                    tag="mm_sin", group=g)
            for q in range(4):
                self.ve(lambda e, q=q: e.tensor_tensor(
                    self.ab_S[0:64, q * 512:(q + 1) * 512],
                    Qt[0:64, q * 512:(q + 1) * 512], self.pb[q][0:64, :],
                    ALU.add),
                    reads=[f"pb{q}", "bs_Qt", "ab_S"], writes=["ab_S"],
                    tag="bsup")

    def _ab_pass2(self):
        NT, Tc = self.NT, self.Tc
        suw = [self.sbP(f"p3_suw{i}", [P, 3072], BF16) for i in range(2)]
        gg = [self.sbP(f"p3_gg{i}", [P, 2048], BF16) for i in range(2)]
        qfc = self.sbP("p3_qfc", [64, 2048], BF16)
        kfc = self.sbP("p3_kfc", [64, 2048], BF16)
        ktc = self.sbP("p3_ktc", [P, 1024], BF16)
        Kt = self.sbP("p3_Kt", [P, 1024], BF16)
        Wh = self.sbP("p3_Wh", [64, 2048], BF16)
        Sb = self.sbP("p3_Sb", [P, 2048], BF16)
        Up = self.sbP("p3_Up", [P, 2048], BF16)
        Ps = self.sbP("p3_Ps", [P, 2048], BF16)
        O1 = self.sbP("p3_O1", [P, 2048], F32)
        ob = self.sbP("p3_ob", [P, 2048], F32)
        scr = self.sbP("p3_scr", [P, 2048], BF16)
        ssb = self.sbP("p3_ss", [P, 16], F32)
        tA = self.sbP("p3_tA", [64, 2048], F32)
        yb = self.sbP("p3_yb", [P, 2048], BF16)
        t2b = self.sbP("p3_t2b", [P, 2048], BF16)
        ystg = self.sbP("p3_ystg", [P, 512], BF16)
        e5 = self.fconst(LN_EPS)
        s128 = self.fconst(1.0 / 128.0)

        for ch in range(NT):
            b = ch % 2
            self.dma(suw[b][:], self.suw_hbm[ch * P:(ch + 1) * P, :],
                     reads=["suw_sp", f"p3_suw{b}"], writes=[f"p3_suw{b}"],
                     tag="p3suw")
            self.dma(gg[b][:], self.gg_hbm[ch * P:(ch + 1) * P, :],
                     reads=["gg_sp", f"p3_gg{b}"], writes=[f"p3_gg{b}"],
                     tag="p3gg")
            for h2 in range(2):
                self.dma(qfc[:].rearrange("p (d t) -> p d t", t=256)[
                    :, :, h2 * 128:(h2 + 1) * 128],
                    self.qf_hbm[:, h2 * 64:h2 * 64 + 64,
                                ch * P:(ch + 1) * P].rearrange("d p t -> p d t"),
                    reads=["qf_sp", "p3_qfc"], writes=["p3_qfc"], tag="p3qf")
                self.dma(kfc[:].rearrange("p (d t) -> p d t", t=256)[
                    :, :, h2 * 128:(h2 + 1) * 128],
                    self.kf_hbm[:, h2 * 64:h2 * 64 + 64,
                                ch * P:(ch + 1) * P].rearrange("d p t -> p d t"),
                    reads=["kf_sp", "p3_kfc"], writes=["p3_kfc"], tag="p3kf")
            self.dma(ktc[:], self.kt_hbm[ch * P:(ch + 1) * P, :],
                     reads=["kt_sp", "p3_ktc"], writes=["p3_ktc"], tag="p3kt")
            self.ve(lambda e, ch=ch: e.tensor_tensor(
                Kt[:].rearrange("p (h k) -> p h k", h=16),
                ktc[:].rearrange("p (h k) -> p h k", h=16),
                self.ab_Dbdec[:, ch * 16:(ch + 1) * 16].unsqueeze(2).broadcast_to(
                    (P, 16, 64)), ALU.mult),
                reads=["p3_ktc", "ab_Dbdec"], writes=["p3_Kt"], tag="p3Kt")
            self.ve(lambda e: e.tensor_copy(Sb[0:64, :], self.ab_S[:]),
                    reads=["ab_S"], writes=["p3_Sb"], tag="p3sbc")
            for i in range(16):
                sl = i % 2
                tp = self.tpb4[sl][:]
                self.pe(lambda e, tp=tp, i=i, b=b: e.transpose(
                    tp[0:64, 0:P],
                    suw[b][:, i * 192 + 128:i * 192 + 192],
                    self.id_bf[:]),
                    reads=[f"p3_suw{b}", "id_bf"], writes=[f"tpb4_{sl}"],
                    tag="p3WT")
                self.ve(lambda e, tp=tp, i=i, sl=sl: e.tensor_scalar(
                    Wh[0:64, i * 128:(i + 1) * 128], tp[0:64, 0:P], -1.0, None,
                    ALU.mult),
                    reads=[f"tpb4_{sl}"], writes=["p3_Wh"], tag="p3WTe")
            for grp in range(2):
                g = self.nm("gufix")
                for hh in range(8):
                    h = grp * 8 + hh
                    self.pe(lambda e, h=h, hh=hh: e.matmul(
                        self.pb[hh // 2][:, (hh % 2) * 256:(hh % 2) * 256 + 128],
                        Wh[0:64, h * 128:(h + 1) * 128],
                        Sb[0:64, h * 128:(h + 1) * 128],
                        start=True, stop=False),
                        reads=["p3_Wh", "p3_Sb"], writes=[f"pb{hh//2}"],
                        tag="mm_ws", group=g)
                    self.pe(lambda e, h=h, hh=hh, b=b: e.matmul(
                        self.pb[hh // 2][:, (hh % 2) * 256:(hh % 2) * 256 + 128],
                        self.id_bf[:],
                        suw[b][:, h * 192:h * 192 + 128],
                        start=False, stop=True),
                        reads=[f"p3_suw{b}", "id_bf"], writes=[f"pb{hh//2}"],
                        tag="mm_uv", group=g)
                for hh in range(8):
                    h = grp * 8 + hh
                    self.ve(lambda e, h=h, hh=hh: e.tensor_copy(
                        Up[:, h * 128:(h + 1) * 128],
                        self.pb[hh // 2][:, (hh % 2) * 256:(hh % 2) * 256 + 128]),
                        reads=[f"pb{hh//2}"], writes=["p3_Up"], tag="ufixev")
            self._ab_logD(ch, orientA=True, orientB=False)
            for q in range(4):
                g = self.nm("gps")
                for hh in range(4):
                    h = q * 4 + hh
                    self.pe(lambda e, h=h, hh=hh, q=q: e.matmul(
                        self.pb[q][:, hh * 128:(hh + 1) * 128],
                        kfc[0:64, h * P:(h + 1) * P],
                        qfc[0:64, h * P:(h + 1) * P],
                        start=True, stop=True),
                        reads=["p3_kfc", "p3_qfc"], writes=[f"pb{q}"],
                        tag="mm_ps", group=g)
                for hh in range(4):
                    h = q * 4 + hh
                    self.ve(lambda e, h=h, hh=hh, q=q: e.tensor_tensor(
                        Ps[:, h * 128:(h + 1) * 128],
                        self.pb[q][:, hh * 128:(hh + 1) * 128], self.triu_bf[:],
                        ALU.mult),
                        reads=[f"pb{q}", "triu_bf"], writes=["p3_Ps"], tag="psev")
            self.ve(lambda e: e.tensor_tensor(Ps[:], Ps[:], self.s_eAt[:],
                                              ALU.mult),
                    reads=["p3_Ps", "s_eAt"], writes=["p3_Ps"], tag="psev2")
            g = self.nm("gqs")
            for hh in range(16):
                h = hh
                self.pe(lambda e, h=h, hh=hh: e.matmul(
                    self.pb[hh // 4][:, (hh % 4) * 128:(hh % 4) * 128 + 128],
                    qfc[0:64, h * P:(h + 1) * P],
                    Sb[0:64, h * 128:(h + 1) * 128],
                    start=True, stop=True),
                    reads=["p3_qfc", "p3_Sb"], writes=[f"pb{hh//4}"],
                    tag="mm_qs", group=g)
            for q in range(4):
                self.ve(lambda e, q=q, ch=ch: e.tensor_tensor(
                    O1[:, q * 512:(q + 1) * 512].rearrange(
                        "p (h v) -> p h v", v=128),
                    self.pb[q][:].rearrange("p (h v) -> p h v", v=128),
                    self.ab_LamT[:, ch * 16 + q * 4:ch * 16 + (q + 1) * 4
                                 ].unsqueeze(2).broadcast_to((P, 4, 128)),
                    ALU.mult),
                    reads=[f"pb{q}", "ab_LamT"], writes=["p3_O1"], tag="o1ev")
            g = self.nm("gpu2")
            for hh in range(16):
                h = hh
                self.pe(lambda e, h=h, hh=hh: e.matmul(
                    self.pb[hh // 4][:, (hh % 4) * 128:(hh % 4) * 128 + 128],
                    Ps[:, h * 128:(h + 1) * 128], Up[:, h * 128:(h + 1) * 128],
                    start=True, stop=True),
                    reads=["p3_Ps", "p3_Up"], writes=[f"pb{hh//4}"],
                    tag="mm_psu", group=g)
            for q in range(4):
                self.ve(lambda e, q=q: e.tensor_tensor(
                    ob[:, q * 512:(q + 1) * 512], O1[:, q * 512:(q + 1) * 512],
                    self.pb[q][:], ALU.add),
                    reads=[f"pb{q}", "p3_O1"], writes=["p3_ob"], tag="oev")
            self.ve(lambda e, ch=ch: e.tensor_tensor(
                tA[:].rearrange("p (h v) -> p h v", h=16),
                self.ab_S[0:64, :].rearrange("p (h v) -> p h v", h=16),
                self.ab_eG[0:64, ch * 16:(ch + 1) * 16].unsqueeze(2).broadcast_to(
                    (64, 16, 128)), ALU.mult),
                reads=["ab_S", "ab_eG"], writes=["p3_tA"], tag="p3smul")
            g = self.nm("gsu")
            for hh in range(16):
                h = hh
                self.pe(lambda e, h=h, hh=hh: e.matmul(
                    self.pb[hh // 4][0:64, (hh % 4) * 128:(hh % 4) * 128 + 128],
                    Kt[:, h * 64:(h + 1) * 64], Up[:, h * 128:(h + 1) * 128],
                    start=True, stop=True),
                    reads=["p3_Kt", "p3_Up"], writes=[f"pb{hh//4}"],
                    tag="mm_su", group=g)
            for q in range(4):
                self.ve(lambda e, q=q: e.tensor_tensor(
                    self.ab_S[0:64, q * 512:(q + 1) * 512],
                    tA[0:64, q * 512:(q + 1) * 512], self.pb[q][0:64, :],
                    ALU.add),
                    reads=[f"pb{q}", "p3_tA", "ab_S"], writes=["ab_S"],
                    tag="p3sadd")
            self.se(lambda e: e.activation(scr[:], ob[:], AF.Square, bias=0.0,
                                           scale=1.0),
                    reads=["p3_ob"], writes=["p3_scr"], tag="p3sq")
            self.ve(lambda e: e.tensor_reduce(
                ssb[:].rearrange("p (h o) -> p h o", o=1),
                scr[:].rearrange("p (h v) -> p h v", v=128), AX.X, ALU.add),
                reads=["p3_scr"], writes=["p3_ss"], tag="p3red")
            self.se(lambda e: e.activation(ssb[:], ssb[:], AF.Ln, bias=e5[:],
                                           scale=s128[:]),
                    reads=["p3_ss", e5.name, s128.name], writes=["p3_ss"],
                    tag="p3ln")
            self.se(lambda e: e.activation(ssb[:], ssb[:], AF.Exp, bias=0.0,
                                           scale=-0.5),
                    reads=["p3_ss"], writes=["p3_ss"], tag="p3ex")
            self.ve(lambda e: e.tensor_tensor(
                ob[:].rearrange("p (h v) -> p h v", v=128),
                ob[:].rearrange("p (h v) -> p h v", v=128),
                ssb[:].unsqueeze(2).broadcast_to((P, 16, 128)), ALU.mult),
                reads=["p3_ob", "p3_ss"], writes=["p3_ob"], tag="p3rms")
            self.ve(lambda e, b=b: e.tensor_tensor(
                t2b[:].rearrange("p (h v) -> p h v", v=128),
                gg[b][:].rearrange("p (h v) -> p h v", v=128),
                self.bnorm[:].unsqueeze(1).broadcast_to((P, 16, 128)), ALU.mult),
                reads=[f"p3_gg{b}", "bnorm_b"], writes=["p3_t2b"], tag="p3gn")
            self.ve(lambda e: e.tensor_tensor(yb[:], ob[:], t2b[:], ALU.mult),
                    reads=["p3_ob", "p3_t2b"], writes=["p3_yb"], tag="p3y")
            for k0 in range(0, 16, 4):
                for i in range(4):
                    tpb = self.tpb4[i % 2][:]
                    self.pe(lambda e, tpb=tpb, kk=k0 + i: e.transpose(
                        tpb[:], yb[:, kk * P:(kk + 1) * P], self.id_bf[:]),
                        reads=["p3_yb", "id_bf"], writes=[f"tpb4_{i % 2}"],
                        tag="p3yT")
                    self.ve(lambda e, tpb=tpb, i=i: e.tensor_copy(
                        ystg[:, i * P:(i + 1) * P], tpb),
                        reads=[f"tpb4_{i % 2}"], writes=["p3_ystg"], tag="p3yTe")
                for i in range(4):
                    self.dma(self.yt_hbm[k0 + i, :, ch * P:(ch + 1) * P],
                             ystg[:, i * P:(i + 1) * P],
                             reads=["p3_ystg"], writes=["yt_sp"], q="gpsimd",
                             tag="p3ysp")
        oW = [self.sbP(f"p3_ow{i}", [P, 512], BF16) for i in range(2)]
        q9b = [self.sbP(f"p3_yt{i}", [P, 512], BF16) for i in range(16)]
        nw = 0
        for tg in range((NT + 3) // 4):
            tiles = [tg * 4 + i for i in range(4) if tg * 4 + i < NT]
            w = len(tiles) * P
            for kk in range(16):
                self.dma(q9b[kk][:, 0:w],
                         self.yt_hbm[kk, :, tiles[0] * P:tiles[0] * P + w],
                         reads=[f"p3ytu{kk}", "yt_sp"], writes=[f"p3_yt{kk}"],
                         tag="p3ytl")
            for nn in range(2):
                for kk in range(16):
                    bw = nw % 2
                    nw += 1
                    self.dma(oW[bw][:], self.wout_hbm[kk, :,
                                                      nn * 512:(nn + 1) * 512],
                             reads=[f"p3owu{bw}", "wout_sp"],
                             writes=[f"p3_ow{bw}"], tag="p3owl")
                    for i, t in enumerate(tiles):
                        self.pe(lambda e, bw=bw, kk=kk, i=i: e.matmul(
                            self.pb[i][:], q9b[kk][:, i * P:(i + 1) * P],
                            oW[bw][:], start=(kk == 0), stop=(kk == 15)),
                            reads=[f"p3_ow{bw}", f"p3_yt{kk}"],
                            writes=[f"pb{i}"] + ([f"p3owu{bw}"]
                                                 if i == len(tiles) - 1 else []),
                            tag="mm_wo", group=f"gwo{tg}_{nn}_{i}")
                for i, t in enumerate(tiles):
                    self.ve(lambda e, i=i, t=t, nn=nn: e.tensor_tensor(
                        self.xres[t][:, nn * 512:(nn + 1) * 512],
                        self.xres[t][:, nn * 512:(nn + 1) * 512],
                        self.pb[i][:], ALU.add),
                        reads=[f"pb{i}", f"xres{t}"],
                        writes=[f"xres{t}"] + ([f"p3ytu{kk2}" for kk2 in
                                                range(16)]
                                               if nn == 1 and i == len(tiles) - 1
                                               else []),
                        tag="wo_ev")



_BUILD_CACHE = {}


def get_builder(Tc, n_cores, flags=(True, True, True, True)):
    key = (Tc, n_cores, flags)
    if key not in _BUILD_CACHE:
        b = Builder(Tc, n_cores, *flags)
        nc = b.build()
        _BUILD_CACHE[key] = (b, nc)
    return _BUILD_CACHE[key]


WMAP = {
    "aw_norm": "L0_attn_a_norm_w", "aw_inproj": "L0_attn_a_in_proj_w",
    "aw_convw": "L0_attn_a_conv_w", "aw_convb": "L0_attn_a_conv_b",
    "aw_Alog": "L0_attn_a_A_log", "aw_Dp": "L0_attn_a_Dp",
    "aw_dtb": "L0_attn_a_dt_bias", "aw_onorm": "L0_attn_a_out_norm_w",
    "aw_outproj": "L0_attn_a_out_proj_w",
    "bw_q": "L0_attn_b_w_q", "bw_k": "L0_attn_b_w_k", "bw_v": "L0_attn_b_w_v",
    "bw_a": "L0_attn_b_w_a", "bw_b": "L0_attn_b_w_b", "bw_g": "L0_attn_b_w_g",
    "bw_out": "L0_attn_b_w_out", "bw_Alog": "L0_attn_b_A_log",
    "bw_dtb": "L0_attn_b_dt_bias", "bw_convq": "L0_attn_b_conv_q",
    "bw_convk": "L0_attn_b_conv_k", "bw_convv": "L0_attn_b_conv_v",
    "bw_norm": "L0_attn_b_norm_w",
    "fw_r": "L0_ffn_W_r", "fw_k": "L0_ffn_W_k", "fw_v": "L0_ffn_W_v",
    "fw_mur": "L0_ffn_mu_r", "fw_muk": "L0_ffn_mu_k",
}


def make_in_maps(b, x, inputs):
    T, Dm = x.shape
    n = b.NC
    Tc = b.Tc
    in_maps = []
    for c in range(n):
        m = {}
        m["x_in"] = np.ascontiguousarray(x[c * Tc:(c + 1) * Tc])
        m["x_halo"] = (np.zeros((3, Dm), np.float32) if c == 0
                       else np.ascontiguousarray(x[c * Tc - 3:c * Tc]))
        pm = np.zeros((n, 1), np.float32)
        pm[:c, 0] = 1.0
        ps = np.zeros((n, 1), np.float32)
        if c > 0:
            ps[c - 1, 0] = 1.0
        m["prevmask"] = pm
        m["prevsel"] = ps
        p3 = np.zeros((n * 3, 3), np.float32)
        if c > 0:
            for r in range(3):
                p3[(c - 1) * 3 + r, r] = 1.0
        m["prevsel3"] = p3
        for k, v in WMAP.items():
            m[k] = np.ascontiguousarray(inputs[v], np.float32)
        for k, v in b.hostc.items():
            m[k] = v
        valid = set(b.inputs) | set(b.hostc)
        m = {k: v for k, v in m.items() if k in valid}
        in_maps.append(m)
    return in_maps


def kernel(**inputs):
    x = np.ascontiguousarray(inputs["L0_attn_a_hidden_states"], np.float32)
    B, T, Dm = x.shape
    n = 8
    b, nc = get_builder(T // n, n)
    in_maps = make_in_maps(b, x.reshape(T, Dm), inputs)
    res = run_bass_kernel_spmd(nc, in_maps, core_ids=list(range(n)))
    out = np.concatenate([res.results[c]["x_out"] for c in range(n)], axis=0)
    return out.reshape(B, T, Dm)


# version 05 (final)
# speedup vs baseline: 1.0092x; 1.0092x over previous
import contextlib
import numpy as np

import concourse.bass as bass
import concourse.mybir as mybir
from concourse.bass_utils import run_bass_kernel_spmd

F32 = mybir.dt.float32
BF16 = mybir.dt.bfloat16
AF = mybir.ActivationFunctionType
ALU = mybir.AluOpType
AX = mybir.AxisListType

P = 128
D = 1024


QUEUES = ("sync", "tensor", "vector", "scalar", "gpsimd")
MERGE_KNOW = True
STEP = {"cmp": 1, "dma": 16, "cc": 1}
NSLOT = {"dma": 8, "cc": 2}


class Op:
    __slots__ = ("queue", "kind", "fn", "deps", "qidx", "need_inc", "tag",
                 "know", "pe_group", "_waits", "slot", "cnt", "cmp_idx", "incs")

    def __init__(self, queue, kind, fn, tag, pe_group):
        self.queue = queue
        self.kind = kind
        self.fn = fn
        self.deps = set()
        self.qidx = -1
        self.need_inc = kind != "cmp"
        self.tag = tag
        self.know = None
        self.pe_group = pe_group
        self._waits = ()
        self.slot = None
        self.cnt = 0
        self.cmp_idx = -1
        self.incs = 1

    def semkey(self):
        if self.kind == "cmp":
            return ("cmp", self.queue)
        return (self.kind, self.queue, self.slot)


class Prog:
    def __init__(self):
        self.ops = []
        self.queues = {q: [] for q in QUEUES}
        self.last_writer = {}
        self.readers = {}
        self._slotrr = {}
        self._slotlast = {}
        self._cmpidx = {q: 0 for q in QUEUES}
        self.last_real = {}

    def op(self, queue, fn, reads=(), writes=(), kind="cmp", tag="",
           pe_group=None, incs=1, extra_deps=()):
        o = Op(queue, kind, fn, tag, pe_group)
        o.incs = incs
        o.qidx = len(self.queues[queue])
        deps = set()
        for r in reads:
            w = self.last_writer.get(r)
            if w is not None:
                deps.add(w)
        for w in writes:
            lw = self.last_writer.get(w)
            if lw is not None:
                deps.add(lw)
            for rd in self.readers.get(w, ()):
                deps.add(rd)
        for d in extra_deps:
            deps.add(d)
        deps.discard(o)
        if pe_group is not None:
            deps = {d for d in deps
                    if not (d.queue == "tensor" and d.pe_group == pe_group)}
        if kind == "cmp":
            o.cmp_idx = self._cmpidx[queue]
            self._cmpidx[queue] += 1
        else:
            rr = self._slotrr.get((queue, kind), 0)
            o.slot = rr % NSLOT[kind]
            self._slotrr[(queue, kind)] = rr + 1
            skey = (queue, kind, o.slot)
            prev = self._slotlast.get(skey)
            if prev is not None:
                deps.add(prev)
                o.cnt = prev.cnt + 1
            else:
                o.cnt = 1
            self._slotlast[skey] = o
        o.deps = deps
        for r in reads:
            self.readers.setdefault(r, []).append(o)
        for w in writes:
            self.last_writer[w] = o
            self.readers[w] = [o]
        self.ops.append(o)
        self.queues[queue].append(o)
        if not tag.startswith("bar"):
            self.last_real[queue] = o
        return o

    @staticmethod
    def _need(d):
        return d.cnt if d.kind != "cmp" else None

    def resolve(self):
        know = {q: {} for q in QUEUES}
        for o in self.ops:
            k = know[o.queue]
            waits = []
            for d in sorted(o.deps, key=lambda d: (d.queue, d.kind or "", -(d.qidx))):
                key = d.semkey()
                need = d.cnt if d.kind != "cmp" else d.cmp_idx + 1
                if k.get(key, 0) >= need:
                    continue
                waits.append(d)
                d.need_inc = True
                if MERGE_KNOW and d.know:
                    for sq, c in d.know.items():
                        if k.get(sq, 0) < c:
                            k[sq] = c
                if k.get(key, 0) < need:
                    k[key] = need
            o._waits = waits
            o.know = dict(k)
            okey = o.semkey()
            oneed = o.cnt if o.kind != "cmp" else o.cmp_idx + 1
            o.know[okey] = max(o.know.get(okey, 0), oneed)
        self.wait_val = {}
        for q in QUEUES:
            c = 0
            for o in self.queues[q]:
                if o.kind == "cmp" and o.need_inc:
                    c += o.incs
                    self.wait_val[o] = c

    def emit_queue(self, q, eng, sems):
        for o in self.queues[q]:
            merged = {}
            for d in o._waits:
                if d.kind == "cmp":
                    val = self.wait_val[d]
                else:
                    val = d.cnt * STEP[d.kind]
                key = d.semkey()
                merged[key] = max(merged.get(key, 0), val)
            for key, val in sorted(merged.items()):
                eng.wait_ge(sems[key], val)
            inst = o.fn(eng)
            if o.need_inc:
                inst.then_inc(sems[o.semkey()], STEP[o.kind])



LN_EPS = 1e-5


class Builder:
    def __init__(self, Tc=2048, n_cores=8,
                 do_a=True, do_f1=True, do_b=True, do_f2=True):
        self.Tc = Tc
        self.NT = Tc // P
        self.NC = n_cores
        self.do = dict(a=do_a, f1=do_f1, b=do_b, f2=do_f2)
        self.nc = bass.Bass("TRN2", target_bir_lowering=False)
        self.pr = Prog()
        self.ctx = contextlib.ExitStack()
        self._n = 0
        self._tps = 0
        self.hostc = {}
        self.inputs = []
        self._shr = {"addr_space": "Shared"} if n_cores > 4 else {}

    def nm(self, p):
        self._n += 1
        return f"{p}{self._n}"

    def sb(self, name, shape, dtype):
        return self.ctx.enter_context(
            self.nc.sbuf_tensor(name, list(shape), dtype, side="right"))

    def psum(self, shape, dtype=F32, name=None):
        if name is None:
            name = self.nm("ps")
        return self.ctx.enter_context(self.nc.psum_tensor(name, list(shape), dtype))

    def const(self, name, arr):
        arr = np.ascontiguousarray(arr, np.float32)
        self.hostc[name] = arr
        return self.nc.dram_tensor(name, list(arr.shape), F32, kind="ExternalInput").ap()

    def inp(self, name, shape):
        self.inputs.append(name)
        return self.nc.dram_tensor(name, list(shape), F32, kind="ExternalInput").ap()

    def begin_phase(self):
        if not hasattr(self, "phases"):
            self.phases = []
        self.phases.append(contextlib.ExitStack())

    def sbP(self, name, shape, dtype):
        return self.phases[-1].enter_context(
            self.nc.sbuf_tensor(name, list(shape), dtype))

    def sbx(self, name, shape, dtype):
        if getattr(self, "phases", None):
            return self.sbP(name, shape, dtype)
        return self.sb(name, shape, dtype)

    def barrier(self):
        targets = list(self.pr.last_real.values()) + \
            list(self.pr._slotlast.values())
        for q in QUEUES:
            self.pr.op(q, lambda e: e.nop(), reads=[], writes=[],
                       tag="bar", extra_deps=targets)

    def end_phase(self):
        self.barrier()
        self.phases.pop().close()
        if hasattr(self, "_qscr"):
            del self._qscr

    def fconst(self, val):
        if not hasattr(self, "_fc"):
            self._fc = {}
        if val not in self._fc:
            t = self.sb(self.nm("fc"), [P, 1], F32)
            self.ve(lambda e, t=t, v=val: e.memset(t[:], v),
                    reads=[], writes=[t.name], tag="fconst")
            self._fc[val] = t
        return self._fc[val]

    def dma(self, dst, src, reads, writes, q="sync", tag="dma"):
        self.pr.op(q, lambda e, d=dst, s=src: e.dma_start(out=d, in_=s),
                   reads=reads, writes=writes, kind="dma", tag=tag)

    def ve(self, fn, reads, writes, tag="ve"):
        self.pr.op("vector", fn, reads=reads, writes=writes, tag=tag)

    def se(self, fn, reads, writes, tag="se", incs=1):
        self.pr.op("scalar", fn, reads=reads, writes=writes, tag=tag, incs=incs)

    def pe(self, fn, reads, writes, tag="pe", group=None):
        self.pr.op("tensor", fn, reads=reads, writes=writes, tag=tag, pe_group=group)

    def gp(self, fn, reads, writes, kind="cmp", tag="gp"):
        self.pr.op("gpsimd", fn, reads=reads, writes=writes, kind=kind, tag=tag)

    def build(self):
        nc, NT, Tc = self.nc, self.NT, self.Tc

        self.x_hbm = self.inp("x_in", [Tc, D])
        self.xh_hbm = self.inp("x_halo", [3, D])
        self.prevmask_hbm = self.inp("prevmask", [self.NC, 1])
        self.prevsel_hbm = self.inp("prevsel", [self.NC, 1])

        wnames = {
            "aw_norm": [D], "aw_inproj": [4384, D], "aw_convw": [2304, 4],
            "aw_convb": [2304], "aw_Alog": [32], "aw_Dp": [32], "aw_dtb": [32],
            "aw_onorm": [2048], "aw_outproj": [D, 2048],
            "bw_q": [D, D], "bw_k": [D, D], "bw_v": [D, 2048], "bw_a": [D, 16],
            "bw_b": [D, 16], "bw_g": [D, 2048], "bw_out": [2048, D],
            "bw_Alog": [16], "bw_dtb": [16], "bw_convq": [D, 4], "bw_convk": [D, 4],
            "bw_convv": [2048, 4], "bw_norm": [128],
            "fw_r": [D, D], "fw_k": [D, D], "fw_v": [D, D],
            "fw_mur": [D], "fw_muk": [D],
        }
        self.W = {k: self.inp(k, v) for k, v in wnames.items()}
        self.out_hbm = nc.dram_tensor("x_out", [Tc, D], F32, kind="ExternalOutput").ap()

        self.c_id_hbm = self.const("c_id", np.eye(P, dtype=np.float32))
        self.c_triu_hbm = self.const("c_triu", np.triu(np.ones((P, P), np.float32)))

        self.xres = None
        self.id_bf = self.sb("id_bf", [P, P], BF16)
        self.id_f32 = self.sb("id_f32", [P, P], F32)
        self.triu_bf = self.sb("triu_bf", [P, P], BF16)
        self.tri_f32 = self.sb("tri_f32", [P, P], F32)
        self.mv = self.sb("mv_stats", [P, 2 * (NT + 1)], F32)
        self.rstd = self.sb("rstd_all", [P, NT + 1], F32)
        self.nmr = self.sb("nmr_all", [P, NT + 1], F32)
        self.prevmask = self.sb("prevmask_sb", [self.NC, 1], F32)
        self.prevsel = self.sb("prevsel_sb", [self.NC, 1], F32)
        self.prevsel_bf = self.sb("prevsel_bf", [self.NC, 1], BF16)

        self.tpf = self.psum([P, 512], F32)
        self.pb = [self.psum([P, 512], F32) for _ in range(4)]
        self.tpb4 = [self.psum([P, P], BF16) for _ in range(2)]
        self.tpf4 = [self.psum([P, P], F32) for _ in range(1)]
        self.scr_hbm = nc.dram_tensor("scr_bounce", [16, D], BF16)

        self.scrb_hbm = nc.dram_tensor("scrb_hbm", [8, 4096], BF16)
        self.c_ones_row_h = self.const("c_ones_row", np.ones((1, P), np.float32))
        self.c_sel_last_h = self.const("c_sel_last",
                                       np.eye(P, dtype=np.float32)[:, -1:])
        self.c_negrow_h = self.const("c_negrow", -np.ones((1, P), np.float32))
        self.c_ones3_h = self.const("c_ones3", np.ones((3, P), np.float32))
        self.ones_row = self.sb("ones_row", [1, P], F32)
        self.dma(self.ones_row[:], self.c_ones_row_h[:], reads=[],
                 writes=["ones_row"])
        self.sel_last = self.sb("sel_last", [P, 1], F32)
        self.dma(self.sel_last[:], self.c_sel_last_h[:], reads=[],
                 writes=["sel_last"])
        nrow_f = self.sb("negrow_f", [1, P], F32)
        nrow_b = self.sb("negrow_b", [1, P], BF16)
        self.dma(nrow_f[:], self.c_negrow_h[:], reads=[], writes=["negrow_f"])
        self.ve(lambda e: e.tensor_copy(nrow_b[:], nrow_f[:]),
                reads=["negrow_f"], writes=["negrow_b"], tag="nrcast")
        self.dma(self.scrb_hbm[3:4, 0:P], nrow_b[:], reads=["negrow_b"],
                 writes=["scrb_nr"], q="gpsimd")
        o3f = self.sb("ones3_f", [3, P], F32)
        o3b = self.sb("ones3_b", [3, P], BF16)
        self.dma(o3f[:], self.c_ones3_h[:], reads=[], writes=["ones3_f"])
        self.ve(lambda e: e.tensor_copy(o3b[:], o3f[:]),
                reads=["ones3_f"], writes=["ones3_b"], tag="o3cast")
        self.dma(self.scrb_hbm[0:3, 0:P], o3b[:], reads=["ones3_b"],
                 writes=["scrb3"], q="gpsimd")

        self.dma(self.id_f32[:], self.c_id_hbm[:], reads=[], writes=["id_f32"])
        self.ve(lambda e: e.tensor_copy(self.id_bf[:], self.id_f32[:]),
                reads=["id_f32"], writes=["id_bf"], tag="idcast")
        self.dma(self.tri_f32[:], self.c_triu_hbm[:], reads=[], writes=["tri_f32"])
        self.ve(lambda e: e.tensor_copy(self.triu_bf[:], self.tri_f32[:]),
                reads=["tri_f32"], writes=["triu_bf"], tag="tricast")
        self.dma(self.prevmask[:], self.prevmask_hbm[:], reads=[], writes=["prevmask"])
        self.dma(self.prevsel[:], self.prevsel_hbm[:], reads=[], writes=["prevsel"])
        self.ve(lambda e: e.tensor_copy(self.prevsel_bf[:], self.prevsel[:]),
                reads=["prevsel"], writes=["prevsel_bf"], tag="pselcast")

        self.phases = []
        if self.do["a"]:
            self.attn_a()
        self.ensure_xres(load=not self.do["a"])
        if self.do["f1"]:
            self.ffn(0)
        if self.do["b"]:
            self.attn_b()
        if self.do["f2"]:
            self.ffn(1)

        for t in range(NT):
            self.dma(self.out_hbm[t * P:(t + 1) * P, :], self.xres[t][:],
                     reads=[f"xres{t}"], writes=[f"OUT{t}"], q="gpsimd", tag="store")
        self.gp(lambda e: e.nop(), reads=[f"OUT{t}" for t in range(NT)],
                writes=[], tag="final")

        self.pr.resolve()
        semkeys = set()
        for o in self.pr.ops:
            semkeys.add(o.semkey())
        sems = {}
        for key in sorted(semkeys):
            name = "sem_" + "_".join(str(x) for x in key)
            sems[key] = self.ctx.enter_context(self.nc.semaphore(name))
        with self.nc.Block() as block:
            @block.sync
            def _(eng):
                self.pr.emit_queue("sync", eng, sems)

            @block.tensor
            def _(eng):
                self.pr.emit_queue("tensor", eng, sems)

            @block.vector
            def _(eng):
                self.pr.emit_queue("vector", eng, sems)

            @block.scalar
            def _(eng):
                self.pr.emit_queue("scalar", eng, sems)

            @block.gpsimd
            def _(eng):
                self.pr.emit_queue("gpsimd", eng, sems)
        self.ctx.close()
        return self.nc

    def ensure_xres(self, load):
        if self.xres is not None:
            return
        self.xres = [self.sb(f"xres{t}", [P, D], F32) for t in range(self.NT)]
        if load:
            for t in range(self.NT):
                self.dma(self.xres[t][:], self.x_hbm[t * P:(t + 1) * P, :],
                         reads=[], writes=[f"xres{t}"])

    def ln_stats(self, xtiles):
        NT = len(xtiles)
        for t, (xt, xkey) in enumerate(xtiles):
            F = xt.shape[1] if hasattr(xt, "shape") else D
            nchunk = (F + 511) // 512
            bt = self.sbx(self.nm("bnst"), [P, 6 * nchunk], F32)
            for ch in range(nchunk):
                lo, hi = ch * 512, min((ch + 1) * 512, F)
                self.ve(lambda e, b=bt, x=xt, ch=ch, lo=lo, hi=hi: e.bn_stats(
                    b[:, 6 * ch:6 * ch + 6], x[:, lo:hi]),
                    reads=[xkey], writes=[bt.name], tag="bnstat")
            self.ve(lambda e, b=bt, t=t: e.bn_aggr(self.mv[:, 2 * t:2 * t + 2], b[:]),
                    reads=[bt.name], writes=["mv"], tag="bnaggr")
        self.ln_stats_finish(NT)

    def ln_stats_finish(self, NT):
        mv3 = self.mv[:].rearrange("p (t two) -> p t two", two=2)
        varv = mv3[:, 0:NT, 1:2]
        meanv = mv3[:, 0:NT, 0:1]
        lnv = self.sbx(self.nm("lnv"), [P, NT], F32)
        eps = self.fconst(LN_EPS)
        self.se(lambda e: e.activation(lnv[:], varv, AF.Ln, bias=eps[:], scale=1.0),
                reads=["mv", eps.name], writes=[lnv.name], tag="ln_v")
        self.se(lambda e: e.activation(self.rstd[:, 0:NT], lnv[:], AF.Exp,
                                       bias=0.0, scale=-0.5),
                reads=[lnv.name], writes=["rstd"], tag="exp_v")
        self.ve(lambda e: e.tensor_tensor(self.nmr[:, 0:NT], self.rstd[:, 0:NT],
                                          meanv, ALU.mult),
                reads=["rstd", "mv"], writes=["nmr"], tag="nmr1")
        self.ve(lambda e: e.tensor_scalar(self.nmr[:, 0:NT], self.nmr[:, 0:NT],
                                          -1.0, None, ALU.mult),
                reads=["nmr"], writes=["nmr"], tag="nmr2")

    def ln_apply(self, xt, xkey, t, out_ap, outkey):
        self.ve(lambda e: e.tensor_scalar(out_ap, xt[:], self.rstd[:, t:t + 1],
                                          self.nmr[:, t:t + 1], ALU.mult, ALU.add),
                reads=[xkey, "rstd", "nmr"], writes=[outkey], tag="lnapply")

    def transpose_to(self, src_ap, srckey, dst_ap, dstkey, ncol=P):
        s = self._tps % 2
        self._tps += 1
        tk = f"tpb4_{s}"
        tp = self.tpb4[s][0:ncol, 0:P]
        self.pe(lambda e: e.transpose(tp, src_ap, self.id_bf[:]),
                reads=[srckey, "id_bf"], writes=[tk], tag="transp")
        self.ve(lambda e: e.tensor_copy(dst_ap, tp),
                reads=[tk], writes=[dstkey], tag="tevac")

    def feature_cols(self, hbm_ap, F, name):
        nk = F // P
        stage = self.sbx(self.nm(name + "_st"), [nk, P], F32)
        out = self.sbx(name, [P, nk], F32)
        self.dma(stage[:], hbm_ap.rearrange("(a b) -> a b", b=P),
                 reads=[], writes=[stage.name], tag="fcload")
        for c0 in range(0, nk, P):
            c1 = min(nk, c0 + P)
            sl = 0
            tp = self.tpf4[0][:]
            tk = "tpf4_0"
            self.pe(lambda e, tp=tp, c0=c0, c1=c1: e.transpose(
                tp[0:P, 0:c1 - c0], stage[c0:c1, :], self.id_f32[0:c1 - c0, 0:c1 - c0]),
                reads=[stage.name, "id_f32"], writes=[tk], tag="fcT")
            self.ve(lambda e, tp=tp, c0=c0, c1=c1: e.tensor_copy(
                out[:, c0:c1], tp[0:P, 0:c1 - c0]),
                reads=[tk], writes=[name], tag="fcTev")
        return out

    def ffn_prep_weights(self, pre):
        self.WTt = {w: [self.sbP(f"{pre}WT_{w}_{kk}", [P, D], BF16)
                        for kk in range(8)] for w in ("r", "k", "v")}
        wtmp = [self.sbP(f"{pre}w_tmp{i}", [P, D], F32) for i in range(2)]
        wtmp_bf = [self.sbP(f"{pre}w_tmpbf{i}", [P, D], BF16) for i in range(2)]
        i = 0
        for w in ("r", "k", "v"):
            hbm = self.W[f"fw_{w}"]
            for dd in range(8):
                b = i % 2
                i += 1
                self.dma(wtmp[b][:], hbm[dd * P:(dd + 1) * P, :],
                         reads=[f"{pre}w_tmpbf{b}"], writes=[f"{pre}w_tmp{b}"],
                         tag="fwload")
                self.ve(lambda e, b=b: e.tensor_copy(wtmp_bf[b][:], wtmp[b][:]),
                        reads=[f"{pre}w_tmp{b}"], writes=[f"{pre}w_tmpbf{b}"],
                        tag="fwcast")
                for kk in range(8):
                    dst = self.WTt[w][kk][:, dd * P:(dd + 1) * P]
                    self.transpose_to(wtmp_bf[b][:, kk * P:(kk + 1) * P],
                                      f"{pre}w_tmpbf{b}", dst, f"{pre}WT_{w}_{kk}")
        self.mu = {}
        for m in ("mur", "muk"):
            self.mu[m] = self.feature_cols(self.W[f"fw_{m}"], D, f"{pre}mu_{m}")

    def ffn(self, which):
        NT = self.NT
        pre = f"f{which}_"
        self.begin_phase()
        self.ffn_prep_weights(pre)
        self.pre = pre

        self.ln_stats([(self.xres[t], f"xres{t}") for t in range(NT)])

        xln = self.sbP(self.nm("f_xln"), [P, D], BF16)
        xlt = [[self.sbP(self.nm("f_xlt"), [P, P + 1], BF16) for kk in range(8)]
               for par in range(2)]
        xrT = self.sbP(self.nm("f_xrT"), [P, D], BF16)
        xkT = self.sbP(self.nm("f_xkT"), [P, D], BF16)
        tsub = self.sbP(self.nm("f_tsub"), [P, P], BF16)
        r_bf = self.sbP(self.nm("f_r"), [P, D], BF16)
        kr_bf = self.sbP(self.nm("f_kr"), [P, D], BF16)
        ksq = self.sbP(self.nm("f_ksq"), [P, D], BF16)
        ksqT = self.sbP(self.nm("f_ksqT"), [P, D], BF16)
        vv = self.sbP(self.nm("f_vv"), [P, D], F32)

        cc_in = self.nc.dram_tensor(self.nm("fcc_in"), [1, D], BF16)
        cc_out = self.nc.dram_tensor(self.nm("fcc_out"), [self.NC, D], BF16)
        ck = f"fcc{which}"
        halo_sb = self.sbP(self.nm("f_halo"), [self.NC, D], BF16)
        halo_row = self.sbP(self.nm("f_halor"), [1, D], BF16)
        halo8 = self.sbP(self.nm("f_halo8"), [8, P], BF16)
        halo_col = self.sbP(self.nm("f_halocol"), [P, 8], BF16)

        def transposes(t, par):
            self.ln_apply(self.xres[t], f"xres{t}", t, xln[:], xln.name)
            for kk in range(8):
                self.transpose_to(xln[:, kk * P:(kk + 1) * P], xln.name,
                                  xlt[par][kk][:, 1:P + 1], xlt[par][kk].name)

        t_last = NT - 1
        transposes(t_last, t_last % 2)
        self.dma(cc_in[0:1, :], xln[P - 1:P, :], reads=[xln.name], writes=[ck + "i"],
                 q="gpsimd", tag="ffn_ccw")
        self.gp(lambda e: e.collective_compute(
            "AllGather", ALU.bypass, replica_groups=[list(range(self.NC))],
            ins=[cc_in.ap().opt()], outs=[cc_out.ap().opt()]),
            reads=[ck + "i"], writes=[ck + "o"], kind="cc", tag="ffn_ag")
        self.dma(halo_sb[:], cc_out[:, :], reads=[ck + "o"], writes=[halo_sb.name],
                 q="gpsimd", tag="ffn_ccr")
        for hh in range(2):
            self.pe(lambda e, hh=hh: e.matmul(
                self.pb[hh][0:1, :], self.prevsel_bf[:],
                halo_sb[:, hh * 512:(hh + 1) * 512], start=True, stop=True),
                reads=[halo_sb.name, "prevsel_bf"], writes=[f"pb{hh}"],
                tag="halosel")
            self.ve(lambda e, hh=hh: e.tensor_copy(
                halo_row[:, hh * 512:(hh + 1) * 512], self.pb[hh][0:1, :]),
                reads=[f"pb{hh}"], writes=[halo_row.name], tag="haloev")
        self.dma(self.scr_hbm[0:1, :], halo_row[:], reads=[halo_row.name],
                 writes=[ck + "sc"], q="gpsimd", tag="haloscw")
        self.dma(halo8[:], self.scr_hbm[0:1, :].rearrange("a (k p) -> (a k) p", p=P),
                 reads=[ck + "sc"], writes=[halo8.name], q="gpsimd", tag="haloscr")
        self.pe(lambda e: e.transpose(self.tpb4[0][0:P, 0:8], halo8[0:8, :],
                                      self.id_bf[0:8, 0:8]),
                reads=[halo8.name, "id_bf"], writes=["tpb4_0"], tag="haloT")
        self.ve(lambda e: e.tensor_copy(halo_col[:], self.tpb4[0][0:P, 0:8]),
                reads=["tpb4_0"], writes=[halo_col.name], tag="haloTev")

        for t in list(range(NT - 1)) + [t_last]:
            par = t % 2
            transposes(t, par)
            for kk in range(8):
                if t == 0:
                    self.ve(lambda e, kk=kk, par=par: e.tensor_copy(
                        xlt[par][kk][:, 0:1], halo_col[:, kk:kk + 1]),
                        reads=[halo_col.name], writes=[xlt[par][kk].name], tag="bcol0")
                else:
                    pp = (t - 1) % 2
                    self.ve(lambda e, kk=kk, par=par, pp=pp: e.tensor_copy(
                        xlt[par][kk][:, 0:1], xlt[pp][kk][:, P:P + 1]),
                        reads=[xlt[pp][kk].name], writes=[xlt[par][kk].name], tag="bcol")
            self._ffn_tile(t, par, xlt, xrT, xkT, tsub, r_bf, kr_bf, ksq, ksqT, vv)
        self.end_phase()

    def _ffn_tile(self, t, par, xlt, xrT, xkT, tsub, r_bf, kr_bf, ksq, ksqT, vv):
        for kk in range(8):
            xc = xlt[par][kk][:, 1:P + 1]
            xp = xlt[par][kk][:, 0:P]
            xk_ = xlt[par][kk].name
            self.ve(lambda e, xc=xc, xp=xp: e.tensor_tensor(tsub[:], xc, xp, ALU.subtract),
                    reads=[xk_], writes=[tsub.name], tag="fmixsub")
            self.ve(lambda e, xp=xp, kk=kk, m=self.mu["mur"]: e.scalar_tensor_tensor(
                xrT[:, kk * P:(kk + 1) * P], tsub[:], m[:, kk:kk + 1], xp,
                ALU.mult, ALU.add),
                reads=[tsub.name, f"{self.pre}mu_mur", xk_], writes=[xrT.name], tag="fmixr")
            self.ve(lambda e, xp=xp, kk=kk, m=self.mu["muk"]: e.scalar_tensor_tensor(
                xkT[:, kk * P:(kk + 1) * P], tsub[:], m[:, kk:kk + 1], xp,
                ALU.mult, ALU.add),
                reads=[tsub.name, f"{self.pre}mu_muk", xk_], writes=[xkT.name], tag="fmixk")
        for nn in range(2):
            pbank, pk = self.pb[nn], f"pb{nn}"
            g = self.nm("gr")
            for kk in range(8):
                self.pe(lambda e, nn=nn, kk=kk, pbank=pbank, w=self.WTt["r"][kk]:
                        e.matmul(pbank[:], xrT[:, kk * P:(kk + 1) * P],
                                 w[:, nn * 512:(nn + 1) * 512],
                                 start=(kk == 0), stop=(kk == 7)),
                    reads=[xrT.name, f"{self.pre}WT_r_{kk}"], writes=[pk], tag="mm_r", group=g)
            self.se(lambda e, nn=nn, pbank=pbank: e.activation(
                r_bf[:, nn * 512:(nn + 1) * 512], pbank[:], AF.Sigmoid,
                bias=0.0, scale=1.0),
                reads=[pk], writes=[r_bf.name], tag="sig_r")
        for nn in range(2):
            pbank, pk = self.pb[2 + nn], f"pb{2+nn}"
            g = self.nm("gk")
            for kk in range(8):
                self.pe(lambda e, nn=nn, kk=kk, pbank=pbank, w=self.WTt["k"][kk]:
                        e.matmul(pbank[:], xkT[:, kk * P:(kk + 1) * P],
                                 w[:, nn * 512:(nn + 1) * 512],
                                 start=(kk == 0), stop=(kk == 7)),
                    reads=[xkT.name, f"{self.pre}WT_k_{kk}"], writes=[pk], tag="mm_k", group=g)
            self.se(lambda e, nn=nn, pbank=pbank: e.activation(
                kr_bf[:, nn * 512:(nn + 1) * 512], pbank[:], AF.Relu,
                bias=0.0, scale=1.0),
                reads=[pk], writes=[kr_bf.name], tag="relu_k")
        self.ve(lambda e: e.tensor_tensor(ksq[:], kr_bf[:], kr_bf[:], ALU.mult),
                reads=[kr_bf.name], writes=[ksq.name], tag="ksq")
        for kk in range(8):
            self.transpose_to(ksq[:, kk * P:(kk + 1) * P], ksq.name,
                              ksqT[:, kk * P:(kk + 1) * P], ksqT.name)
        for nn in range(2):
            pbank, pk = self.pb[nn], f"pb{nn}"
            g = self.nm("gv")
            for kk in range(8):
                self.pe(lambda e, nn=nn, kk=kk, pbank=pbank, w=self.WTt["v"][kk]:
                        e.matmul(pbank[:], ksqT[:, kk * P:(kk + 1) * P],
                                 w[:, nn * 512:(nn + 1) * 512],
                                 start=(kk == 0), stop=(kk == 7)),
                    reads=[ksqT.name, f"{self.pre}WT_v_{kk}"], writes=[pk], tag="mm_v", group=g)
            self.ve(lambda e, nn=nn, pbank=pbank: e.scalar_tensor_tensor(
                vv[:, nn * 512:(nn + 1) * 512], pbank[:], 0.0,
                r_bf[:, nn * 512:(nn + 1) * 512], ALU.bypass, ALU.mult),
                reads=[pk, r_bf.name], writes=[vv.name], tag="gate_v")
        self.ve(lambda e, t=t: e.tensor_tensor(self.xres[t][:], self.xres[t][:],
                                               vv[:], ALU.add),
                reads=[vv.name, f"xres{t}"], writes=[f"xres{t}"], tag="resadd")

    def attn_a(self):
        self.begin_phase()
        NT = self.NT
        nc = self.nc
        self.u_hbm = nc.dram_tensor("u_hbm", [self.Tc, 2048], BF16)
        self.zs_hbm = nc.dram_tensor("zs_hbm", [self.Tc, 2048], BF16)
        self.q9t_hbm = nc.dram_tensor("q9t_hbm", [16, P, self.Tc], BF16)
        self.aWx_hbm = nc.dram_tensor("aWx_hbm", [19, P, 1024], BF16)
        self.aWz_hbm = nc.dram_tensor("aWz_hbm", [8, P, 2048], BF16)
        self.oWT_hbm = nc.dram_tensor("oWT_hbm", [16, P, 1024], BF16)
        self.cca_in = nc.dram_tensor("cca_in", [P, 2080], F32)
        self.cca_out = nc.dram_tensor("cca_out", [self.NC * P, 2080], F32, **self._shr)

        self._aa_consts()
        NTL = self.NT
        self.B_f = self.sbP("B_f", [P, self.Tc], BF16)
        self.C_f = self.sbP("C_f", [P, self.Tc], BF16)
        self.B_tok = self.sbP("B_tok", [P, self.Tc], BF16)
        self.dt_all = self.sbP("dt_all", [P, (NTL + 1) * 32], F32)
        self.cs_all = self.sbP("cs_all", [P, NTL * 32], F32)
        self.CSE = self.sbP("CSE", [1, NTL * 32], F32)
        self.Ddec = self.sbP("Ddec", [P, NTL * 32], F32)
        self.Doff = self.sbP("Doff", [P, NTL * 32], F32)
        self.eA_rep = self.sbP("eA_rep", [P, NTL * 32], F32)
        self.S_run = self.sbP("S_run", [P, 2048], F32)

        self.begin_phase()
        NT1 = self.NT + 1
        self.q8T_all = self.sbP("q8T_all", [P, 8 * NT1 * P], BF16)
        self.TW = NT1 * P
        self.begin_phase()
        self._aa_wprep()
        self._aa_quant()
        self.end_phase()
        self.begin_phase()
        self._aa_inproj_conv()
        self._aa_dt()
        self._aa_states()
        self._aa_z()
        self.end_phase()
        self.end_phase()
        self.begin_phase()
        self.begin_phase()
        self._aa_sin()
        self.end_phase()
        self._aa_pass2()
        self.end_phase()
        self.ensure_xres(load=True)
        self.begin_phase()
        self._aa_outproj()
        self.end_phase()
        self.end_phase()

    def rowload(self, hbm_ap, n, name):
        t = self.sbP(self.nm(name), [1, n], F32)
        self.dma(t[:], hbm_ap.rearrange("(a b) -> a b", a=1), reads=[], writes=[t.name])
        return t

    def bcast_row(self, row_ap, rowkey, n, name, dtype=F32):
        out = self.sbP(name, [P, n], dtype)
        nb = (n + 511) // 512
        for b in range(nb):
            lo, hi = b * 512, min(n, (b + 1) * 512)
            s = self._tps % 4
            self._tps += 1
            ps = self.tpf[:, 128 * s:128 * s + 128] if hi - lo <= 128 else self.pb[b % 4]
            pk = "tpfB" if hi - lo <= 128 else f"pb{b%4}"
            self.pe(lambda e, ps=ps, lo=lo, hi=hi: e.matmul(
                ps[0:P, 0:hi - lo], self.ones_row[:], row_ap[0:1, lo:hi],
                start=True, stop=True),
                reads=[rowkey, "ones_row"], writes=[pk], tag="bcast")
            self.ve(lambda e, ps=ps, lo=lo, hi=hi: e.tensor_copy(
                out[:, lo:hi], ps[0:P, 0:hi - lo]),
                reads=[pk], writes=[name], tag="bcastev")
        return out

    def ln_single(self, x_ap, xkey, F, pref):
        nch = (F + 511) // 512
        bt = self.sbP(self.nm(pref + "bn"), [P, 6 * nch], F32)
        mvt = self.sbP(self.nm(pref + "mv"), [P, 2], F32)
        r = self.sbP(self.nm(pref + "r"), [P, 1], F32)
        nm = self.sbP(self.nm(pref + "nm"), [P, 1], F32)
        for ch in range(nch):
            lo, hi = ch * 512, min(F, (ch + 1) * 512)
            self.ve(lambda e, ch=ch, lo=lo, hi=hi: e.bn_stats(
                bt[:, 6 * ch:6 * ch + 6], x_ap[:, lo:hi]),
                reads=[xkey], writes=[bt.name], tag="lns_bn")
        self.ve(lambda e: e.bn_aggr(mvt[:], bt[:]),
                reads=[bt.name], writes=[mvt.name], tag="lns_ag")
        eps = self.fconst(LN_EPS)
        self.se(lambda e: e.activation(r[:], mvt[:, 1:2], AF.Ln, bias=eps[:], scale=1.0),
                reads=[mvt.name, eps.name], writes=[r.name], tag="lns_ln")
        self.se(lambda e: e.activation(r[:], r[:], AF.Exp, bias=0.0, scale=-0.5),
                reads=[r.name], writes=[r.name], tag="lns_ex")
        self.ve(lambda e: e.tensor_tensor(nm[:], r[:], mvt[:, 0:1], ALU.mult),
                reads=[r.name, mvt.name], writes=[nm.name], tag="lns_n1")
        self.ve(lambda e: e.tensor_scalar(nm[:], nm[:], -1.0, None, ALU.mult),
                reads=[nm.name], writes=[nm.name], tag="lns_n2")
        return r, nm

    def quant_chain(self, x_ap, xkey, F, out_ap, outkey, pref):
        r2, nm2 = self.ln_single(x_ap, xkey, F, pref)
        if not hasattr(self, "_qscr"):
            self._qscr = {}
        if F not in self._qscr:
            self._qscr[F] = (self.sbP(self.nm("qy2"), [P, F], F32),
                             self.sbP(self.nm("qtq"), [P, F], F32),
                             self.sbP(self.nm("qsg"), [P, F], F32),
                             self.sbP(self.nm("qi"), [P, F],
                                      mybir.dt.int32))
        y2, tq, sg, qi = self._qscr[F]
        self.ve(lambda e: e.tensor_scalar(y2[:], x_ap, r2[:], nm2[:], ALU.mult, ALU.add),
                reads=[xkey, r2.name, nm2.name], writes=[y2.name], tag="q_ln2")
        am = self.sbP(self.nm(pref + "am"), [P, 1], F32)
        sc = self.sbP(self.nm(pref + "sc"), [P, 1], F32)
        si = self.sbP(self.nm(pref + "si"), [P, 1], F32)
        self.ve(lambda e: e.tensor_reduce(am[:], y2[:], AX.X, ALU.max,
                                          apply_absolute_value=True),
                reads=[y2.name], writes=[am.name], tag="q_am")
        self.ve(lambda e: e.tensor_scalar(am[:], am[:], 1e-5, None, ALU.max),
                reads=[am.name], writes=[am.name], tag="q_amc")
        self.ve(lambda e: e.reciprocal(sc[:], am[:]),
                reads=[am.name], writes=[sc.name], tag="q_rc")
        self.ve(lambda e: e.tensor_scalar(sc[:], sc[:], 127.0, None, ALU.mult),
                reads=[sc.name], writes=[sc.name], tag="q_sc")
        self.ve(lambda e: e.tensor_scalar(si[:], am[:], 1.0 / 127.0, None, ALU.mult),
                reads=[am.name], writes=[si.name], tag="q_si")
        self.ve(lambda e: e.tensor_scalar(tq[:], y2[:], sc[:], None, ALU.mult),
                reads=[y2.name, sc.name], writes=[tq.name], tag="q_t")
        self.ve(lambda e: e.tensor_copy(qi[:], tq[:]),
                reads=[tq.name], writes=[qi.name], tag="q_i")
        self.ve(lambda e: e.tensor_scalar(sg[:], qi[:], 1.0, None, ALU.mult),
                reads=[qi.name], writes=[sg.name], tag="q_b")
        self.ve(lambda e: e.tensor_tensor(tq[:], tq[:], sg[:], ALU.subtract),
                reads=[tq.name, sg.name], writes=[tq.name], tag="q_r")
        self.ve(lambda e: e.tensor_scalar(y2[:], tq[:], 0.5, None, ALU.is_gt),
                reads=[tq.name], writes=[y2.name], tag="q_c1")
        self.ve(lambda e: e.tensor_scalar(tq[:], tq[:], -0.5, None, ALU.is_lt),
                reads=[tq.name], writes=[tq.name], tag="q_c2")
        self.ve(lambda e: e.tensor_tensor(sg[:], sg[:], y2[:], ALU.add),
                reads=[sg.name, y2.name], writes=[sg.name], tag="q_a1")
        self.ve(lambda e: e.tensor_tensor(sg[:], sg[:], tq[:], ALU.subtract),
                reads=[sg.name, tq.name], writes=[sg.name], tag="q_a2")
        self.ve(lambda e: e.tensor_scalar(out_ap, sg[:], si[:], None, ALU.mult),
                reads=[sg.name, si.name], writes=[outkey], tag="q_o")

    def _aa_consts(self):
        nwr = self.rowload(self.W["aw_norm"], D, "nw_row")
        self.normw_b = self.bcast_row(nwr[:], nwr.name, D, "normw_b")
        onr = self.rowload(self.W["aw_onorm"], 2048, "onw_row")
        self.onw_b = self.bcast_row(onr[:], onr.name, 2048, "onw_b")
        alog = self.rowload(self.W["aw_Alog"], 32, "alog_row")
        negA = self.sbP("negA_row", [1, 32], F32)
        self.se(lambda e: e.activation(negA[:], alog[:], AF.Exp, bias=0.0, scale=1.0),
                reads=[alog.name], writes=["negA_row"], tag="negA1")
        self.ve(lambda e: e.tensor_scalar(negA[:], negA[:], -1.0, None, ALU.mult),
                reads=["negA_row"], writes=["negA_row"], tag="negA2")
        self.negA_b = self.bcast_row(negA[:], "negA_row", 32, "negA_b")
        dtb = self.rowload(self.W["aw_dtb"], 32, "dtb_row")
        self.dtb_b = self.bcast_row(dtb[:], dtb.name, 32, "dtb_b")
        dpr = self.rowload(self.W["aw_Dp"], 32, "dp_row")
        self.Dp_b = self.bcast_row(dpr[:], dpr.name, 32, "Dp_b")
        self.cw = self.sbP("cw_all", [P, 18 * 4], F32)
        for dc in range(18):
            self.dma(self.cw[:, dc * 4:(dc + 1) * 4],
                     self.W["aw_convw"][dc * P:(dc + 1) * P, :],
                     reads=[], writes=["cw_all"])
        self.cb = self.feature_cols(self.W["aw_convb"], 2304, "cb_all")
        self.rhs99 = self.sbP("rhs99", [99, 4096], BF16)
        self.ve(lambda e: e.memset(self.rhs99[0:96, :], 0.0),
                reads=[], writes=["rhs99"], tag="rhsz")
        for rr in range(3):
            for h in range(32):
                self.dma(self.rhs99[32 * rr + h:32 * rr + h + 1,
                                    h * P:(h + 1) * P],
                         self.scrb_hbm[3:4, 0:P],
                         reads=["scrb_nr"], writes=["rhs99"], q="gpsimd",
                         tag="kron")
        self.lhsT99 = self.sbP("lhsT99", [99, P], BF16)
        self.dma(self.lhsT99[96:99, :], self.scrb_hbm[0:3, 0:P],
                 reads=["scrb3"], writes=["lhsT99"], q="gpsimd")

    def _aa_wprep(self):
        wld = [self.sbP(f"aw_ld{i}", [P, 2048], F32) for i in range(2)]
        wq = self.sbP("aw_q", [P, 2048], BF16)
        wt = self.sbP("aw_t", [P, 2048], F32)
        wm = self.sbP("aw_m", [P, 2048], F32)
        wi = self.sbP("aw_i", [P, 2048], mybir.dt.int32)
        wabs = self.sbP("aw_abs", [P, 2048], BF16)
        wstg = [self.sbP(f"aw_stg{i}", [P, P], BF16) for i in range(2)]

        zpad = self.sbP("aw_zpad", [P, 1024], F32)
        zpad_bf = self.sbP("aw_zpadb", [P, 1024], BF16)
        self.ve(lambda e: e.memset(zpad_bf[:], 0.0), reads=[], writes=["aw_zpadb"],
                tag="zpad")
        self.dma(self.aWx_hbm[18, :, :], zpad_bf[:], reads=["aw_zpadb"],
                 writes=["awsp_ip"], q="gpsimd", tag="zpadw")
        for wname, hbm, ntile, K, spills in (
                ("ip", self.W["aw_inproj"], 35, D, None),
                ("op", self.W["aw_outproj"], 8, 2048, None)):
            acc = self.sbP(f"aw_acc_{wname}", [P, ntile], F32)
            self.ve(lambda e, acc=acc: e.memset(acc[:], 0.0),
                    reads=[], writes=[acc.name], tag="accz")
            for dd in range(ntile):
                rows = 32 if (wname == "ip" and dd == 34) else P
                b = dd % 2
                self.dma(wld[b][0:rows, 0:K], hbm[dd * P:dd * P + rows, :],
                         reads=[f"aw_absdone{b}"], writes=[f"aw_ld{b}"], tag="awl")
                self.se(lambda e, b=b, dd=dd, rows=rows, K=K, acc=acc: e.activation(
                    wabs[0:rows, 0:K], wld[b][0:rows, 0:K], AF.Abs,
                    bias=0.0, scale=1.0, accum_out=acc[0:rows, dd:dd + 1]),
                    reads=[f"aw_ld{b}"], writes=[acc.name, "aw_abs", f"aw_absdone{b}"],
                    tag="awabs")
            cs1 = self.sbP(f"aw_cs_{wname}", [P, 1], F32)
            self.ve(lambda e, acc=acc, cs1=cs1: e.tensor_reduce(cs1[:], acc[:], AX.X, ALU.add),
                    reads=[acc.name], writes=[cs1.name], tag="awred")
            s = self._tps % 4
            self._tps += 1
            ps = self.tpf[:, 128 * s:128 * s + 128]
            pk = "tpfB"
            ones = self.fconst(1.0)
            self.pe(lambda e, ps=ps, cs1=cs1, ones=ones: e.matmul(ps[0:1, 0:1], cs1[:], ones[:],
                                              start=True, stop=True),
                    reads=[cs1.name, ones.name], writes=[pk], tag="awtot")
            tot = self.sbP(f"aw_tot_{wname}", [1, 1], F32)
            self.ve(lambda e, ps=ps, tot=tot: e.tensor_copy(tot[:], ps[0:1, 0:1]),
                    reads=[pk], writes=[tot.name], tag="awtev")
            self.ve(lambda e, tot=tot: e.reciprocal(tot[:], tot[:]),
                    reads=[tot.name], writes=[tot.name], tag="awrcp")
            NEL = float(ntile * P * K) if wname != "ip" else float(4384 * D)
            self.ve(lambda e, NEL=NEL, tot=tot: e.tensor_scalar(tot[:], tot[:], NEL, None,
                                                       ALU.mult),
                    reads=[tot.name], writes=[tot.name], tag="awsc")
            rs_col = self.sbP(f"aw_rs_{wname}", [P, 1], F32)
            s2 = self._tps % 4
            self._tps += 1
            ps2 = self.tpf[:, 128 * s2:128 * s2 + 128]
            pk2 = "tpfB"
            self.pe(lambda e, ps2=ps2, tot=tot: e.matmul(ps2[0:P, 0:1], self.ones_row[:],
                                                tot[:], start=True, stop=True),
                    reads=[tot.name, "ones_row"], writes=[pk2], tag="awbc")
            self.ve(lambda e, ps2=ps2, rs_col=rs_col: e.tensor_copy(rs_col[:], ps2[0:P, 0:1]),
                    reads=[pk2], writes=[rs_col.name], tag="awbcev")
            for dd in range(ntile):
                rows = 32 if (wname == "ip" and dd == 34) else P
                b = dd % 2
                self.dma(wld[b][0:rows, 0:K], hbm[dd * P:dd * P + rows, :],
                         reads=[f"aw_qdone{b}"], writes=[f"aw_ld{b}"], tag="awl2")
                self.ve(lambda e, b=b, rows=rows, K=K, rs_col=rs_col: e.tensor_scalar(
                    wt[0:rows, 0:K], wld[b][0:rows, 0:K], rs_col[0:rows, 0:1],
                    None, ALU.mult),
                    reads=[f"aw_ld{b}", rs_col.name],
                    writes=["aw_t", f"aw_qdone{b}"], tag="awt")
                self.ve(lambda e, rows=rows, K=K: e.tensor_copy(
                    wi[0:rows, 0:K], wt[0:rows, 0:K]),
                    reads=["aw_t"], writes=["aw_i"], tag="awi")
                self.ve(lambda e, rows=rows, K=K: e.tensor_scalar(
                    wm[0:rows, 0:K], wi[0:rows, 0:K], 1.0, None, ALU.mult),
                    reads=["aw_i"], writes=["aw_m"], tag="awb")
                self.ve(lambda e, rows=rows, K=K: e.tensor_tensor(
                    wt[0:rows, 0:K], wt[0:rows, 0:K], wm[0:rows, 0:K],
                    ALU.subtract),
                    reads=["aw_t", "aw_m"], writes=["aw_t"], tag="awr")
                self.ve(lambda e, rows=rows, K=K: e.tensor_scalar(
                    wq[0:rows, 0:K], wt[0:rows, 0:K], 0.5, None, ALU.is_gt),
                    reads=["aw_t"], writes=["aw_q"], tag="awc1")
                self.ve(lambda e, rows=rows, K=K: e.tensor_tensor(
                    wm[0:rows, 0:K], wm[0:rows, 0:K], wq[0:rows, 0:K], ALU.add),
                    reads=["aw_m", "aw_q"], writes=["aw_m"], tag="awa1")
                self.ve(lambda e, rows=rows, K=K: e.tensor_scalar(
                    wq[0:rows, 0:K], wt[0:rows, 0:K], -0.5, None, ALU.is_lt),
                    reads=["aw_t"], writes=["aw_q"], tag="awc2")
                self.ve(lambda e, rows=rows, K=K: e.tensor_tensor(
                    wm[0:rows, 0:K], wm[0:rows, 0:K], wq[0:rows, 0:K],
                    ALU.subtract),
                    reads=["aw_m", "aw_q"], writes=["aw_m"], tag="awa2")
                self.ve(lambda e, rows=rows, K=K: e.tensor_scalar(
                    wq[0:rows, 0:K], wm[0:rows, 0:K], 1.0, -1.0, ALU.min,
                    ALU.max),
                    reads=["aw_m"], writes=["aw_q"], tag="awclip")
                for kk in range(K // P):
                    s3 = self._tps % 2
                    self._tps += 1
                    tp = self.tpb4[s3][:]
                    tk = f"tpb4_{s3}"
                    wb = kk % 2
                    self.pe(lambda e, tp=tp, kk=kk, rows=rows: e.transpose(
                        tp[0:P, 0:rows], wq[0:rows, kk * P:(kk + 1) * P],
                        self.id_bf[0:rows, 0:rows]),
                        reads=["aw_q", "id_bf"], writes=[tk], tag="awT")
                    self.ve(lambda e, tp=tp, wb=wb, rows=rows: e.tensor_copy(
                        wstg[wb][0:P, 0:rows], tp[0:P, 0:rows]),
                        reads=[tk], writes=[f"aw_stg{wb}"], tag="awTev")
                    if wname == "ip":
                        if dd < 16:
                            dst = self.aWz_hbm[kk, :, dd * P:dd * P + rows]
                        else:
                            dst = self.aWx_hbm[dd - 16, :, kk * P:kk * P + rows]
                    else:
                        dst = self.oWT_hbm[kk, :, dd * P:dd * P + rows]
                    self.dma(dst, wstg[wb][0:P, 0:rows], reads=[f"aw_stg{wb}"],
                             writes=[f"awsp_{wname}", f"aw_stg{wb}"], q="gpsimd",
                             tag="awspill")

    def _aa_quant(self):
        NT1 = self.NT + 1
        xb = [self.sbP(f"aa_x{i}", [P, D], F32) for i in range(2)]

        def load_tile(t, b, rkey):
            if t == 0:
                self.ve(lambda e, b=b: e.memset(xb[b][:], 0.0),
                        reads=[rkey], writes=[f"aa_x{b}"], tag="xhz")
                self.dma(xb[b][125:128, :], self.xh_hbm[:, :],
                         reads=[f"aa_x{b}"], writes=[f"aa_x{b}"], tag="xhload")
            else:
                self.dma(xb[b][:], self.x_hbm[(t - 1) * P:t * P, :],
                         reads=[rkey], writes=[f"aa_x{b}"], tag="xload")

        for t in range(NT1):
            b = t % 2
            load_tile(t, b, f"aa_xs{b}")
            nchunk = 2
            bt = self.sbP(self.nm("bnst"), [P, 6 * nchunk], F32)
            for ch in range(nchunk):
                self.ve(lambda e, b=b, ch=ch, bt=bt: e.bn_stats(
                    bt[:, 6 * ch:6 * ch + 6], xb[b][:, ch * 512:(ch + 1) * 512]),
                    reads=[f"aa_x{b}"], writes=[bt.name], tag="bnstat")
            self.ve(lambda e, b=b, bt=bt, t=t: e.bn_aggr(
                self.mv[:, 2 * t:2 * t + 2], bt[:]),
                reads=[bt.name], writes=["mv", f"aa_xs{b}"], tag="bnaggr")
        self.ln_stats_finish(NT1)
        vtmp = self.sbP("aa_vt", [P, NT1], F32)
        rmsf = self.sbP("aa_rmsf", [P, NT1], F32)
        mv3 = self.mv[:].rearrange("p (t two) -> p t two", two=2)
        self.ve(lambda e: e.tensor_copy(vtmp[:].rearrange("p (t o) -> p t o", o=1),
                                        mv3[:, 0:NT1, 1:2]),
                reads=["mv"], writes=["aa_vt"], tag="aa_v")
        ve1 = self.sbP("aa_ve1", [P, NT1], F32)
        self.ve(lambda e: e.tensor_scalar(ve1[:], vtmp[:], LN_EPS, None, ALU.add),
                reads=["aa_vt"], writes=["aa_ve1"], tag="aa_ve")
        self.ve(lambda e: e.reciprocal(ve1[:], ve1[:]),
                reads=["aa_ve1"], writes=["aa_ve1"], tag="aa_rc")
        self.ve(lambda e: e.tensor_tensor(rmsf[:], vtmp[:], ve1[:], ALU.mult),
                reads=["aa_vt", "aa_ve1"], writes=["aa_rmsf"], tag="aa_ms")
        e6 = self.fconst(1e-6)
        self.se(lambda e: e.activation(rmsf[:], rmsf[:], AF.Ln, bias=e6[:], scale=1.0),
                reads=["aa_rmsf", e6.name], writes=["aa_rmsf"], tag="aa_ln")
        self.se(lambda e: e.activation(rmsf[:], rmsf[:], AF.Exp, bias=0.0, scale=-0.5),
                reads=["aa_rmsf"], writes=["aa_rmsf"], tag="aa_ex")

        y1 = self.sbP("aa_y1", [P, D], F32)
        hh = self.sbP("aa_h", [P, D], F32)
        q8s = self.sbP("aa_q8s", [P, D], BF16)
        for t in range(NT1):
            b = t % 2
            load_tile(t, b, f"aa_xa{b}")
            self.ln_apply(xb[b], f"aa_x{b}", t, y1[:], y1.name)
            self.ve(lambda e, t=t: e.scalar_tensor_tensor(
                hh[:], y1[:], rmsf[:, t:t + 1], self.normw_b[:], ALU.mult, ALU.mult),
                reads=[y1.name, "aa_rmsf", "normw_b"], writes=[hh.name], tag="aa_hh")
            self.quant_chain(hh[:], hh.name, D, q8s[:], q8s.name, f"aaq{t}")
            for kk in range(8):
                self.transpose_to(q8s[:, kk * P:(kk + 1) * P], q8s.name,
                                  self.q8T_all[:, kk * self.TW + t * P:
                                               kk * self.TW + (t + 1) * P],
                                  f"q8T_t{t}")

    def _aa_inproj_conv(self):
        NT1 = self.NT + 1
        TW = self.TW
        Tc = self.Tc
        grps = [(g * 512, min(TW, (g + 1) * 512)) for g in range((TW + 511) // 512)]
        aWd = [self.sbP(f"aWx_d{i}", [P, 1024], BF16) for i in range(2)]
        stage = self.sbP("xbc_stage", [P, TW], BF16)
        ct1 = self.sbP("aa_ct1", [P, Tc], BF16)
        ct2 = self.sbP("aa_ct2", [P, Tc], BF16)
        uf = self.sbP("aa_uf", [P, Tc], BF16)
        self.ustg = self.sbP("ustg", [P, 512], BF16)
        self.dtf = self.sbP("dtf", [32, TW], F32)

        q8keys = [f"q8T_t{t}" for t in range(NT1)]
        for dc in range(19):
            rows = 128 if dc < 18 else 32
            b = dc % 2
            self.dma(aWd[b][:], self.aWx_hbm[dc, :, :],
                     reads=[f"aWx_used{b}", "awsp_ip"],
                     writes=[f"aWx_d{b}"], tag="aWxl")
            for gi, (lo, hi) in enumerate(grps):
                pb = self.pb[gi % 2]
                pk = f"pb{gi % 2}"
                g = self.nm("gip")
                for kk in range(8):
                    self.pe(lambda e, b=b, kk=kk, lo=lo, hi=hi, pb=pb: e.matmul(
                        pb[0:P, 0:hi - lo], aWd[b][:, kk * P:(kk + 1) * P],
                        self.q8T_all[:, kk * TW + lo:kk * TW + hi],
                        start=(kk == 0), stop=(kk == 7)),
                        reads=[f"aWx_d{b}"] + q8keys, writes=[pk],
                        tag="mm_ip", group=g)
                last = gi == len(grps) - 1
                if dc < 18:
                    self.ve(lambda e, lo=lo, hi=hi, pb=pb: e.tensor_copy(
                        stage[:, lo:hi], pb[0:P, 0:hi - lo]),
                        reads=[pk], writes=["xbc_stage"] + ([f"aWx_used{b}"] if last else []),
                        tag="ipev")
                else:
                    self.ve(lambda e, lo=lo, hi=hi, pb=pb: e.tensor_copy(
                        self.dtf[0:32, lo:hi], pb[0:32, 0:hi - lo]),
                        reads=[pk], writes=["dtf"] + ([f"aWx_used{b}"] if last else []),
                        tag="ipevdt")
            if dc >= 18:
                continue
            cwc = [self.cw[:, dc * 4 + k:dc * 4 + k + 1] for k in range(4)]
            self.ve(lambda e, cwc=cwc: e.tensor_scalar(
                ct1[:], stage[:, 125:125 + Tc], cwc[0], None, ALU.mult),
                reads=["xbc_stage", "cw_all"], writes=["aa_ct1"], tag="cv0")
            self.ve(lambda e, cwc=cwc: e.scalar_tensor_tensor(
                ct2[:], stage[:, 126:126 + Tc], cwc[1], ct1[:], ALU.mult, ALU.add),
                reads=["xbc_stage", "cw_all", "aa_ct1"], writes=["aa_ct2"], tag="cv1")
            self.ve(lambda e, cwc=cwc: e.scalar_tensor_tensor(
                ct1[:], stage[:, 127:127 + Tc], cwc[2], ct2[:], ALU.mult, ALU.add),
                reads=["xbc_stage", "cw_all", "aa_ct2"], writes=["aa_ct1"], tag="cv2")
            self.ve(lambda e, cwc=cwc: e.scalar_tensor_tensor(
                ct2[:], stage[:, 128:128 + Tc], cwc[3], ct1[:], ALU.mult, ALU.add),
                reads=["xbc_stage", "cw_all", "aa_ct1"], writes=["aa_ct2"], tag="cv3")
            dst, dkey = ((uf, "aa_uf") if dc < 16 else
                         (self.B_f, "B_f") if dc == 16 else (self.C_f, "C_f"))
            self.se(lambda e, dst=dst, dc=dc: e.activation(
                dst[:], ct2[:], AF.Silu, bias=self.cb[:, dc:dc + 1], scale=1.0),
                reads=["aa_ct2", "cb_all"], writes=[dkey], tag="cvsilu")
            if dc < 16:
                for t0 in range(0, self.NT, 4):
                    nt4 = min(4, self.NT - t0)
                    for i in range(nt4):
                        tp = self.tpb4[i % 2][:]
                        self.pe(lambda e, tp=tp, t=t0 + i: e.transpose(
                            tp[:], uf[:, t * P:(t + 1) * P], self.id_bf[:]),
                            reads=["aa_uf", "id_bf"], writes=[f"tpb4_{i % 2}"], tag="uT")
                        self.ve(lambda e, tp=tp, i=i: e.tensor_copy(
                            self.ustg[:, i * P:(i + 1) * P], tp),
                            reads=[f"tpb4_{i % 2}"], writes=["ustg"], tag="uTev")
                    for i in range(nt4):
                        self.dma(self.u_hbm[(t0 + i) * P:(t0 + i + 1) * P,
                                            dc * P:(dc + 1) * P],
                                 self.ustg[:, i * P:(i + 1) * P],
                                 reads=["ustg"], writes=["u_sp"], q="gpsimd",
                                 tag="uspill")
            elif dc == 16:
                for t in range(self.NT):
                    self.transpose_to(self.B_f[:, t * P:(t + 1) * P], "B_f",
                                      self.B_tok[:, t * P:(t + 1) * P], "B_tok")

    def _aa_dt(self):
        NT1, NT = self.NT + 1, self.NT
        W = NT1 * 32
        for t in range(NT1):
            s3 = 0
            tp = self.tpf4[0][:]
            tk = "tpf4_0"
            self.pe(lambda e, tp=tp, t=t: e.transpose(
                tp[0:P, 0:32], self.dtf[0:32, t * P:(t + 1) * P],
                self.id_f32[0:32, 0:32]),
                reads=["dtf", "id_f32"], writes=[tk], tag="dtT")
            self.ve(lambda e, tp=tp, t=t: e.tensor_copy(
                self.dt_all[:, t * 32:(t + 1) * 32], tp[0:P, 0:32]),
                reads=[tk], writes=["dt_all"], tag="dtTev")
        dtbAP = self.dtb_b[:].unsqueeze(1).broadcast_to((P, NT1, 32))
        self.ve(lambda e: e.tensor_tensor(
            self.dt_all[:].rearrange("p (t h) -> p t h", h=32), self.dt_all[:].rearrange("p (t h) -> p t h", h=32), dtbAP, ALU.add),
            reads=["dt_all", "dtb_b"], writes=["dt_all"], tag="dt_b")
        ab = self.sbP("aa_ab", [P, W], F32)
        self.se(lambda e: e.activation(ab[:], self.dt_all[:], AF.Abs, bias=0.0,
                                       scale=1.0),
                reads=["dt_all"], writes=["aa_ab"], tag="dt_abs")
        self.se(lambda e: e.activation(ab[:], ab[:], AF.Exp, bias=0.0, scale=-1.0),
                reads=["aa_ab"], writes=["aa_ab"], tag="dt_exp")
        one = self.fconst(1.0)
        self.se(lambda e: e.activation(ab[:], ab[:], AF.Ln, bias=one[:], scale=1.0),
                reads=["aa_ab", one.name], writes=["aa_ab"], tag="dt_ln")
        rl = self.sbP("aa_rl", [P, W], F32)
        self.ve(lambda e: e.tensor_scalar(rl[:], self.dt_all[:], 0.0, None, ALU.max),
                reads=["dt_all"], writes=["aa_rl"], tag="dt_rl")
        self.ve(lambda e: e.tensor_tensor(self.dt_all[:], rl[:], ab[:], ALU.add),
                reads=["aa_rl", "aa_ab"], writes=["dt_all"], tag="dt_sp")
        self.a_all = self.sbP("a_all", [P, W], F32)
        negAAP = self.negA_b[:].unsqueeze(1).broadcast_to((P, NT1, 32))
        self.ve(lambda e: e.tensor_tensor(
            self.a_all[:].rearrange("p (t h) -> p t h", h=32),
            self.dt_all[:].rearrange("p (t h) -> p t h", h=32), negAAP, ALU.mult),
            reads=["dt_all", "negA_b"], writes=["a_all"], tag="a_mk")
        tri = self.tri_f32
        for ch in range(NT):
            t = ch + 1
            s3 = self._tps % 4
            self._tps += 1
            tp = self.tpf[:, 128 * s3:128 * s3 + 128]
            tk = "tpfB"
            self.pe(lambda e, tp=tp, t=t: e.matmul(
                tp[0:P, 0:32], tri[:], self.a_all[:, t * 32:(t + 1) * 32],
                start=True, stop=True),
                reads=["a_all", "tri_f32"], writes=[tk], tag="cs_mm")
            self.ve(lambda e, tp=tp, ch=ch: e.tensor_copy(
                self.cs_all[:, ch * 32:(ch + 1) * 32], tp[0:P, 0:32]),
                reads=[tk], writes=["cs_all"], tag="cs_ev")
            s4 = self._tps % 4
            self._tps += 1
            tp2 = self.tpf[:, 128 * s4:128 * s4 + 128]
            tk2 = "tpfB"
            self.pe(lambda e, tp2=tp2, ch=ch: e.matmul(
                tp2[0:1, 0:32], self.sel_last[:],
                self.cs_all[:, ch * 32:(ch + 1) * 32], start=True, stop=True),
                reads=["cs_all", "sel_last"], writes=[tk2], tag="cse_mm")
            self.ve(lambda e, tp2=tp2, ch=ch: e.tensor_copy(
                self.CSE[0:1, ch * 32:(ch + 1) * 32], tp2[0:1, 0:32]),
                reads=[tk2], writes=["CSE"], tag="cse_ev")
            s5 = self._tps % 4
            self._tps += 1
            tp3 = self.tpf[:, 128 * s5:128 * s5 + 128]
            tk3 = "tpfB"
            self.pe(lambda e, tp3=tp3, ch=ch: e.matmul(
                tp3[0:P, 0:32], self.ones_row[:],
                self.CSE[0:1, ch * 32:(ch + 1) * 32], start=True, stop=True),
                reads=["CSE", "ones_row"], writes=[tk3], tag="dd_bc")
            self.ve(lambda e, tp3=tp3, ch=ch: e.tensor_tensor(
                self.Ddec[:, ch * 32:(ch + 1) * 32], tp3[0:P, 0:32],
                self.cs_all[:, ch * 32:(ch + 1) * 32], ALU.subtract),
                reads=[tk3, "cs_all"], writes=["Ddec"], tag="dd_sub")
        self.se(lambda e: e.activation(self.Ddec[:], self.Ddec[:], AF.Exp,
                                       bias=0.0, scale=1.0),
                reads=["Ddec"], writes=["Ddec"], tag="dd_exp")
        self.se(lambda e: e.activation(self.Doff[:], self.cs_all[:], AF.Exp,
                                       bias=0.0, scale=1.0),
                reads=["cs_all"], writes=["Doff"], tag="doff")
        eC = self.sbP("eCSE", [1, NT * 32], F32)
        self.se(lambda e: e.activation(eC[:], self.CSE[:], AF.Exp, bias=0.0,
                                       scale=1.0),
                reads=["CSE"], writes=["eCSE"], tag="ecse")
        nb = (NT * 32 + 511) // 512
        for bnum in range(nb):
            lo, hi = bnum * 512, min(NT * 32, (bnum + 1) * 512)
            pbk = self.pb[bnum % 2]
            pk = f"pb{bnum % 2}"
            self.pe(lambda e, lo=lo, hi=hi, pbk=pbk: e.matmul(
                pbk[0:P, 0:hi - lo], self.ones_row[:], eC[0:1, lo:hi],
                start=True, stop=True),
                reads=["eCSE", "ones_row"], writes=[pk], tag="eA_bc")
            self.ve(lambda e, lo=lo, hi=hi, pbk=pbk: e.tensor_copy(
                self.eA_rep[:, lo:hi], pbk[0:P, 0:hi - lo]),
                reads=[pk], writes=["eA_rep"], tag="eA_ev")
        self.logLam = self.sbP("logLam", [1, 32], F32)
        self.ve(lambda e: e.tensor_reduce(
            self.logLam[:].unsqueeze(2), self.CSE[:].rearrange(
                "p (c h) -> p h c", h=32), AX.X, ALU.add),
            reads=["CSE"], writes=["logLam"], tag="loglam")

    def _aa_states(self):
        NT, Tc = self.NT, self.Tc
        self.ve(lambda e: e.memset(self.S_run[:], 0.0), reads=[], writes=["S_run"],
                tag="s0")
        u_ch = [self.sbP(f"u_ch{i}", [P, 2048], BF16) for i in range(2)]
        ud = self.sbP("aa_ud", [P, 2048], BF16)
        ddt = self.sbP("aa_ddt", [P, 32], F32)
        tA = self.sbP("aa_tA", [P, 2048], F32)
        for ch in range(NT):
            b = ch % 2
            self.dma(u_ch[b][:], self.u_hbm[ch * P:(ch + 1) * P, :],
                     reads=[f"u_used{b}", "u_sp"], writes=[f"u_ch{b}"], tag="uload")
            self.ve(lambda e, ch=ch: e.tensor_tensor(
                ddt[:], self.Ddec[:, ch * 32:(ch + 1) * 32],
                self.dt_all[:, (ch + 1) * 32:(ch + 2) * 32], ALU.mult),
                reads=["Ddec", "dt_all"], writes=["aa_ddt"], tag="ddt")
            self.ve(lambda e, b=b: e.tensor_tensor(
                ud[:].rearrange("p (h q) -> p h q", h=32), u_ch[b][:].rearrange(
                    "p (h q) -> p h q", h=32),
                ddt[:].unsqueeze(2).broadcast_to((P, 32, 64)), ALU.mult),
                reads=[f"u_ch{b}", "aa_ddt"], writes=["aa_ud", f"u_used{b}"],
                tag="udec")
            g = self.nm("gst")
            for q in range(4):
                self.pe(lambda e, ch=ch, q=q: e.matmul(
                    self.pb[q][:], self.B_tok[:, ch * P:(ch + 1) * P],
                    ud[:, q * 512:(q + 1) * 512], start=True, stop=True),
                    reads=["B_tok", "aa_ud"], writes=[f"pb{q}"], tag="mm_st", group=g)
            self.ve(lambda e, ch=ch: e.tensor_tensor(
                tA[:].rearrange("p (h q) -> p h q", h=32),
                self.S_run[:].rearrange("p (h q) -> p h q", h=32),
                self.eA_rep[:, ch * 32:(ch + 1) * 32].unsqueeze(2).broadcast_to(
                    (P, 32, 64)), ALU.mult),
                reads=["S_run", "eA_rep"], writes=["aa_tA"], tag="s_mul")
            for q in range(4):
                self.ve(lambda e, q=q: e.tensor_tensor(
                    self.S_run[:, q * 512:(q + 1) * 512],
                    tA[:, q * 512:(q + 1) * 512], self.pb[q][:], ALU.add),
                    reads=["aa_tA", f"pb{q}"], writes=["S_run"], tag="s_add")
        llb = self.bcast_row(self.logLam[:], "logLam", 32, "logLam_b")
        self.dma(self.cca_in[:, 0:2048], self.S_run[:], reads=["S_run"],
                 writes=["cca_i"], q="gpsimd", tag="e1w")
        self.dma(self.cca_in[:, 2048:2080], llb[:], reads=["logLam_b"],
                 writes=["cca_i2"], q="gpsimd", tag="e1w2")
        self.gp(lambda e: e.collective_compute(
            "AllGather", ALU.bypass, replica_groups=[list(range(self.NC))],
            ins=[self.cca_in.ap().opt()], outs=[self.cca_out.ap().opt()]),
            reads=["cca_i", "cca_i2"], writes=["cca_o"], kind="cc", tag="e1ag")

    def _aa_z(self):
        NT, TW = self.NT, self.TW
        zw = [self.sbP(f"aa_zw{i}", [P, 512], BF16) for i in range(2)]
        self._zsil = self.sbP("aa_zsil", [P, 512], BF16)
        ngw = 0
        for tg in range((NT + 3) // 4):
            tiles = [tg * 4 + i for i in range(4) if tg * 4 + i < NT]
            for zg in range(4):
                g = self.nm("gz")
                for kk in range(8):
                    b = ngw % 2
                    ngw += 1
                    self.dma(zw[b][:], self.aWz_hbm[kk, :, zg * 512:(zg + 1) * 512],
                             reads=[f"zw_used{b}", "awsp_ip"],
                             writes=[f"aa_zw{b}"], tag="zwl")
                    for i, t in enumerate(tiles):
                        self.pe(lambda e, b=b, kk=kk, t=t, i=i: e.matmul(
                            self.pb[i][:],
                            self.q8T_all[:, kk * TW + (t + 1) * P:
                                         kk * TW + (t + 2) * P],
                            zw[b][:], start=(kk == 0), stop=(kk == 7)),
                            reads=[f"aa_zw{b}", f"q8T_t{t+1}"],
                            writes=[f"pb{i}"] + ([f"zw_used{b}"] if i == len(tiles) - 1 else []),
                            tag="mm_z", group=g + str(i))
                for i, t in enumerate(tiles):
                    self.se(lambda e, i=i: e.activation(
                        self._zsil[:], self.pb[i][:],
                        AF.Silu, bias=0.0, scale=1.0),
                        reads=[f"pb{i}"], writes=["aa_zsil"], tag="zsilu")
                    self.dma(self.zs_hbm[t * P:(t + 1) * P, zg * 512:(zg + 1) * 512],
                             self._zsil[:], reads=["aa_zsil"], writes=["zs_sp"],
                             q="gpsimd", tag="zspill")

    def _aa_sin(self):
        NC = self.NC
        Lall = self.sbP("aa_Lall", [P, NC * 32], F32)
        for j in range(NC):
            self.dma(Lall[:, j * 32:(j + 1) * 32],
                     self.cca_out[j * P:(j + 1) * P, 2048:2080],
                     reads=["cca_o"], writes=["aa_Lall"], tag="e1rL")
        pmr = self.sbP("pm_row", [1, NC], F32)
        self.dma(pmr[:], self.prevmask_hbm.rearrange("a b -> b a"), reads=[],
                 writes=["pm_row"], tag="pmr")
        pmb = self.bcast_row(pmr[:], "pm_row", NC, "aa_pmb")
        Lm = self.sbP("aa_Lm", [P, NC * 32], F32)
        self.ve(lambda e: e.tensor_tensor(
            Lm[:].rearrange("p (j h) -> p j h", h=32),
            Lall[:].rearrange("p (j h) -> p j h", h=32),
            pmb[:].unsqueeze(2).broadcast_to((P, NC, 32)), ALU.mult),
            reads=["aa_Lall", "aa_pmb"], writes=["aa_Lm"], tag="lm")
        sfx = self.sbP("aa_sfx", [P, NC * 32], F32)
        self.ve(lambda e: e.memset(sfx[:, (NC - 1) * 32:NC * 32], 0.0),
                reads=[], writes=["aa_sfx"], tag="sfz")
        for j in range(NC - 2, -1, -1):
            self.ve(lambda e, j=j: e.tensor_tensor(
                sfx[:, j * 32:(j + 1) * 32], sfx[:, (j + 1) * 32:(j + 2) * 32],
                Lm[:, (j + 1) * 32:(j + 2) * 32], ALU.add),
                reads=["aa_sfx", "aa_Lm"], writes=["aa_sfx"], tag="sfx")
        wj = self.sbP("aa_wj", [P, NC * 32], F32)
        self.se(lambda e: e.activation(wj[:], sfx[:], AF.Exp, bias=0.0, scale=1.0),
                reads=["aa_sfx"], writes=["aa_wj"], tag="wj_e")
        self.ve(lambda e: e.tensor_tensor(
            wj[:].rearrange("p (j h) -> p j h", h=32),
            wj[:].rearrange("p (j h) -> p j h", h=32),
            pmb[:].unsqueeze(2).broadcast_to((P, NC, 32)), ALU.mult),
            reads=["aa_wj", "aa_pmb"], writes=["aa_wj"], tag="wj_m")
        self.ve(lambda e: e.memset(self.S_run[:], 0.0), reads=["S_run"],
                writes=["S_run"], tag="sin0")
        Sg = [self.sbP(f"aa_Sg{i}", [P, 2048], F32) for i in range(2)]
        tS = self.sbP("aa_tS", [P, 2048], F32)
        for j in range(NC):
            b = j % 2
            self.dma(Sg[b][:], self.cca_out[j * P:(j + 1) * P, 0:2048],
                     reads=[f"Sg_used{b}", "cca_o"], writes=[f"aa_Sg{b}"], tag="e1rS")
            self.ve(lambda e, b=b, j=j: e.tensor_tensor(
                tS[:].rearrange("p (h q) -> p h q", h=32),
                Sg[b][:].rearrange("p (h q) -> p h q", h=32),
                wj[:, j * 32:(j + 1) * 32].unsqueeze(2).broadcast_to((P, 32, 64)),
                ALU.mult),
                reads=[f"aa_Sg{b}", "aa_wj"], writes=["aa_tS", f"Sg_used{b}"],
                tag="sin_m")
            self.ve(lambda e: e.tensor_tensor(self.S_run[:], self.S_run[:], tS[:],
                                              ALU.add),
                    reads=["aa_tS", "S_run"], writes=["S_run"], tag="sin_a")

    def _aa_pass2(self):
        NT, Tc, TW = self.NT, self.Tc, self.TW
        u_ch = [self.sbP(f"p2_u{i}", [P, 2048], BF16) for i in range(2)]
        zs_ch = [self.sbP(f"p2_z{i}", [P, 2048], BF16) for i in range(2)]
        S_bf = self.sbP("p2_Sbf", [P, 2048], BF16)
        msc = self.sbP("p2_msc", [P, P], BF16)
        mt = [self.sbP(f"p2_mt{i}", [P, 2048], BF16) for i in range(2)]
        csp = self.sbP("p2_csp", [P, 32], F32)
        hi = self.sbP("p2_hi", [P, 32], BF16)
        mid = self.sbP("p2_mid", [P, 32], BF16)
        lo = self.sbP("p2_lo", [P, 32], BF16)
        yoff = self.sbP("p2_yoff", [P, 2048], F32)
        tdp = self.sbP("p2_tdp", [P, 2048], F32)
        udt = self.sbP("p2_udt", [P, 2048], BF16)
        yg = self.sbP("p2_yg", [P, 2048], F32)
        scr = self.sbP("p2_scr", [P, 2048], BF16)
        ssum = self.sbP("p2_ssum", [P, 1], F32)
        rmsr = self.sbP("p2_rmsr", [P, 1], F32)
        h2 = self.sbP("p2_h2", [P, 2048], F32)
        q9s = self.sbP("p2_q9s", [P, 2048], BF16)
        self.q9stg = self.sbP("q9stg", [P, 512], BF16)
        ud = self.sbP("p2_ud", [P, 2048], BF16)
        ddt = self.sbP("p2_ddt", [P, 32], F32)
        tA = self.sbP("p2_tA", [P, 2048], F32)
        sc2048 = self.fconst(1.0 / 2048.0)
        e6 = self.fconst(1e-6)

        for ch in range(NT):
            b = ch % 2
            self.dma(u_ch[b][:], self.u_hbm[ch * P:(ch + 1) * P, :],
                     reads=[f"p2u_used{b}", "u_sp"], writes=[f"p2_u{b}"], tag="p2ul")
            self.dma(zs_ch[b][:], self.zs_hbm[ch * P:(ch + 1) * P, :],
                     reads=[f"p2z_used{b}", "zs_sp"], writes=[f"p2_z{b}"], tag="p2zl")
            self.ve(lambda e: e.tensor_copy(S_bf[:], self.S_run[:]),
                    reads=["S_run"], writes=["p2_Sbf"], tag="p2sbf")
            s3 = self._tps % 4
            self._tps += 1
            tp = self.tpf[:, 128 * s3:128 * s3 + 128]
            tk = "tpfB"
            self.pe(lambda e, tp=tp, ch=ch: e.matmul(
                tp[:, 0:P], self.B_f[:, ch * P:(ch + 1) * P],
                self.C_f[:, ch * P:(ch + 1) * P], start=True, stop=True),
                reads=["B_f", "C_f"], writes=[tk], tag="p2sc")
            self.ve(lambda e, tp=tp: e.tensor_tensor(msc[:], tp[:, 0:P],
                                                     self.triu_bf[:], ALU.mult),
                    reads=[tk, "triu_bf"], writes=["p2_msc"], tag="p2scm")
            self.ve(lambda e, ch=ch: e.tensor_copy(
                csp[:], self.cs_all[:, ch * 32:(ch + 1) * 32]),
                reads=["cs_all"], writes=["p2_csp"], tag="p2cs")
            self.ve(lambda e: e.tensor_copy(hi[:], csp[:]),
                    reads=["p2_csp"], writes=["p2_hi"], tag="p2hi")
            self.ve(lambda e: e.tensor_tensor(csp[:], csp[:], hi[:], ALU.subtract),
                    reads=["p2_csp", "p2_hi"], writes=["p2_csp"], tag="p2r1")
            self.ve(lambda e: e.tensor_copy(mid[:], csp[:]),
                    reads=["p2_csp"], writes=["p2_mid"], tag="p2mid")
            self.ve(lambda e: e.tensor_tensor(csp[:], csp[:], mid[:], ALU.subtract),
                    reads=["p2_csp", "p2_mid"], writes=["p2_csp"], tag="p2r2")
            self.ve(lambda e: e.tensor_copy(lo[:], csp[:]),
                    reads=["p2_csp"], writes=["p2_lo"], tag="p2lo")
            for rr, (t_bf, t_key) in enumerate(
                    ((hi, "p2_hi"), (mid, "p2_mid"), (lo, "p2_lo"))):
                s4 = self._tps % 2
                self._tps += 1
                tpb = self.tpb4[s4][:]
                tbk = f"tpb4_{s4}"
                self.pe(lambda e, tpb=tpb, t_bf=t_bf: e.transpose(
                    tpb[0:32, 0:P], t_bf[:], self.id_bf[:]),
                    reads=[t_key, "id_bf"], writes=[tbk], tag="p2csT")
                self.ve(lambda e, tpb=tpb, rr=rr: e.tensor_copy(
                    self.lhsT99[32 * rr:32 * rr + 32, :], tpb[0:32, 0:P]),
                    reads=[tbk], writes=["lhsT99"], tag="p2csTe")
                self.dma(self.scrb_hbm[4 + rr:5 + rr, :].rearrange(
                    "a (h l) -> (a h) l", h=32),
                    self.lhsT99[32 * rr:32 * rr + 32, :],
                    reads=["lhsT99"], writes=[f"p2fl{rr}"], q="gpsimd", tag="p2flw")
                self.dma(self.rhs99[96 + rr:97 + rr, :],
                         self.scrb_hbm[4 + rr:5 + rr, 0:4096],
                         reads=[f"p2fl{rr}"], writes=["rhs99"], q="gpsimd",
                         tag="p2flr")
            for half in range(2):
                g = self.nm("gld")
                for q in range(4):
                    self.pe(lambda e, half=half, q=q: e.matmul(
                        self.pb[q][:], self.lhsT99[:],
                        self.rhs99[:, half * 2048 + q * 512:half * 2048 + (q + 1) * 512],
                        start=True, stop=True),
                        reads=["lhsT99", "rhs99"], writes=[f"pb{q}"],
                        tag="mm_ld", group=g)
                for q in range(4):
                    self.ve(lambda e, half=half, q=q: e.tensor_scalar(
                        mt[half][:, q * 512:(q + 1) * 512], self.pb[q][:],
                        0.0, None, ALU.min),
                        reads=[f"pb{q}"], writes=[f"p2_mt{half}"], tag="ld_ev")
                self.se(lambda e, half=half: e.activation(
                    mt[half][:], mt[half][:], AF.Exp, bias=0.0, scale=1.0),
                    reads=[f"p2_mt{half}"], writes=[f"p2_mt{half}"], tag="ld_exp")
                self.ve(lambda e, half=half: e.tensor_tensor(
                    mt[half][:].rearrange("p (h l) -> p h l", l=P),
                    mt[half][:].rearrange("p (h l) -> p h l", l=P),
                    msc[:].unsqueeze(1).broadcast_to((P, 16, P)), ALU.mult),
                    reads=[f"p2_mt{half}", "p2_msc"], writes=[f"p2_mt{half}"],
                    tag="ld_m")
            g = self.nm("gyo")
            for q in range(4):
                self.pe(lambda e, ch=ch, q=q: e.matmul(
                    self.pb[q][:], self.C_f[:, ch * P:(ch + 1) * P],
                    S_bf[:, q * 512:(q + 1) * 512], start=True, stop=True),
                    reads=["C_f", "p2_Sbf"], writes=[f"pb{q}"], tag="mm_yo", group=g)
            for q in range(4):
                self.ve(lambda e, ch=ch, q=q: e.tensor_tensor(
                    yoff[:, q * 512:(q + 1) * 512].rearrange(
                        "p (h z) -> p h z", z=64),
                    self.pb[q][:].rearrange("p (h z) -> p h z", z=64),
                    self.Doff[:, ch * 32 + q * 8:ch * 32 + (q + 1) * 8].unsqueeze(
                        2).broadcast_to((P, 8, 64)), ALU.mult),
                    reads=[f"pb{q}", "Doff"], writes=["p2_yoff"], tag="yo_ev")
            self.ve(lambda e, ch=ch, b=b: e.tensor_tensor(
                udt[:].rearrange("p (h z) -> p h z", h=32),
                u_ch[b][:].rearrange("p (h z) -> p h z", h=32),
                self.dt_all[:, (ch + 1) * 32:(ch + 2) * 32].unsqueeze(2).broadcast_to(
                    (P, 32, 64)), ALU.mult),
                reads=[f"p2_u{b}", "dt_all"], writes=["p2_udt"], tag="udt")
            for q in range(4):
                g = self.nm("gyd")
                for hh in range(8):
                    h = q * 8 + hh
                    self.pe(lambda e, h=h, hh=hh, q=q: e.matmul(
                        self.pb[q][:, hh * 64:(hh + 1) * 64],
                        mt[h // 16][:, (h % 16) * P:(h % 16 + 1) * P],
                        udt[:, h * 64:(h + 1) * 64],
                        start=True, stop=True),
                        reads=[f"p2_mt{h//16}", "p2_udt"], writes=[f"pb{q}"],
                        tag="mm_yd", group=g)
            self.ve(lambda e, b=b: e.tensor_tensor(
                tdp[:].rearrange("p (h z) -> p h z", h=32),
                u_ch[b][:].rearrange("p (h z) -> p h z", h=32),
                self.Dp_b[:].unsqueeze(2).broadcast_to((P, 32, 64)), ALU.mult),
                reads=[f"p2_u{b}", "Dp_b"], writes=["p2_tdp"], tag="tdp")
            self.ve(lambda e: e.tensor_tensor(yg[:], yoff[:], tdp[:], ALU.add),
                    reads=["p2_yoff", "p2_tdp"], writes=["p2_yg"], tag="yg1")
            for q in range(4):
                self.ve(lambda e, q=q: e.tensor_tensor(
                    yg[:, q * 512:(q + 1) * 512], yg[:, q * 512:(q + 1) * 512],
                    self.pb[q][:], ALU.add),
                    reads=["p2_yg", f"pb{q}"], writes=["p2_yg"], tag="yg2")
            self.ve(lambda e, b=b: e.tensor_tensor(yg[:], yg[:], zs_ch[b][:],
                                                   ALU.mult),
                    reads=["p2_yg", f"p2_z{b}"],
                    writes=["p2_yg", f"p2z_used{b}"], tag="yg3")
            self.se(lambda e: e.activation(scr[:], yg[:], AF.Square, bias=0.0,
                                           scale=1.0, accum_out=ssum[:]),
                    reads=["p2_yg"], writes=["p2_scr", "p2_ssum"], tag="rms_sq")
            self.se(lambda e: e.activation(rmsr[:], ssum[:], AF.Ln, bias=e6[:],
                                           scale=sc2048[:]),
                    reads=["p2_ssum", e6.name, sc2048.name], writes=["p2_rmsr"],
                    tag="rms_ln")
            self.se(lambda e: e.activation(rmsr[:], rmsr[:], AF.Exp, bias=0.0,
                                           scale=-0.5),
                    reads=["p2_rmsr"], writes=["p2_rmsr"], tag="rms_ex")
            self.ve(lambda e: e.scalar_tensor_tensor(
                h2[:], yg[:], rmsr[:], self.onw_b[:], ALU.mult, ALU.mult),
                reads=["p2_yg", "p2_rmsr", "onw_b"], writes=["p2_h2"], tag="rms_ap")
            self.quant_chain(h2[:], "p2_h2", 2048, q9s[:], "p2_q9s", f"p2q{ch}")
            for k0 in range(0, 16, 4):
                for i in range(4):
                    tpb = self.tpb4[i % 2][:]
                    self.pe(lambda e, tpb=tpb, kk=k0 + i: e.transpose(
                        tpb[:], q9s[:, kk * P:(kk + 1) * P], self.id_bf[:]),
                        reads=["p2_q9s", "id_bf"], writes=[f"tpb4_{i % 2}"], tag="q9T")
                    self.ve(lambda e, tpb=tpb, i=i: e.tensor_copy(
                        self.q9stg[:, i * P:(i + 1) * P], tpb),
                        reads=[f"tpb4_{i % 2}"], writes=["q9stg"], tag="q9Tev")
                for i in range(4):
                    self.dma(self.q9t_hbm[k0 + i, :, ch * P:(ch + 1) * P],
                             self.q9stg[:, i * P:(i + 1) * P],
                             reads=["q9stg"], writes=["q9_sp"], q="gpsimd",
                             tag="q9spill")
            self.ve(lambda e, ch=ch: e.tensor_tensor(
                ddt[:], self.Ddec[:, ch * 32:(ch + 1) * 32],
                self.dt_all[:, (ch + 1) * 32:(ch + 2) * 32], ALU.mult),
                reads=["Ddec", "dt_all"], writes=["p2_ddt"], tag="p2ddt")
            self.ve(lambda e, b=b: e.tensor_tensor(
                ud[:].rearrange("p (h z) -> p h z", h=32),
                u_ch[b][:].rearrange("p (h z) -> p h z", h=32),
                ddt[:].unsqueeze(2).broadcast_to((P, 32, 64)), ALU.mult),
                reads=[f"p2_u{b}", "p2_ddt"], writes=["p2_ud", f"p2u_used{b}"],
                tag="p2udec")
            g = self.nm("gst2")
            for q in range(4):
                self.pe(lambda e, ch=ch, q=q: e.matmul(
                    self.pb[q][:], self.B_tok[:, ch * P:(ch + 1) * P],
                    ud[:, q * 512:(q + 1) * 512], start=True, stop=True),
                    reads=["B_tok", "p2_ud"], writes=[f"pb{q}"], tag="mm_st2",
                    group=g)
            self.ve(lambda e, ch=ch: e.tensor_tensor(
                tA[:].rearrange("p (h z) -> p h z", h=32),
                self.S_run[:].rearrange("p (h z) -> p h z", h=32),
                self.eA_rep[:, ch * 32:(ch + 1) * 32].unsqueeze(2).broadcast_to(
                    (P, 32, 64)), ALU.mult),
                reads=["S_run", "eA_rep"], writes=["p2_tA"], tag="p2smul")
            for q in range(4):
                self.ve(lambda e, q=q: e.tensor_tensor(
                    self.S_run[:, q * 512:(q + 1) * 512],
                    tA[:, q * 512:(q + 1) * 512], self.pb[q][:], ALU.add),
                    reads=["p2_tA", f"pb{q}"], writes=["S_run"], tag="p2sadd")

    def _aa_outproj(self):
        NT, Tc = self.NT, self.Tc
        assert self.xres is not None
        oW = [self.sbP(f"op_w{i}", [P, 512], BF16) for i in range(2)]
        q9b = [self.sbP(f"op_q{i}", [P, 512], BF16) for i in range(16)]
        nw = 0
        for tg in range((NT + 3) // 4):
            tiles = [tg * 4 + i for i in range(4) if tg * 4 + i < NT]
            w = len(tiles) * P
            for kk in range(16):
                self.dma(q9b[kk][:, 0:w],
                         self.q9t_hbm[kk, :, tiles[0] * P:tiles[0] * P + w],
                         reads=[f"opq_used{kk}", "q9_sp"], writes=[f"op_q{kk}"],
                         tag="opql")
            for nn in range(2):
                for kk in range(16):
                    bw = nw % 2
                    nw += 1
                    self.dma(oW[bw][:], self.oWT_hbm[kk, :, nn * 512:(nn + 1) * 512],
                             reads=[f"opw_used{bw}", "awsp_op"],
                             writes=[f"op_w{bw}"], tag="opwl")
                    for i, t in enumerate(tiles):
                        self.pe(lambda e, bw=bw, kk=kk, i=i: e.matmul(
                            self.pb[i][:], q9b[kk][:, i * P:(i + 1) * P],
                            oW[bw][:], start=(kk == 0), stop=(kk == 15)),
                            reads=[f"op_w{bw}", f"op_q{kk}"],
                            writes=[f"pb{i}"] + ([f"opw_used{bw}"] if i == len(tiles) - 1 else []),
                            tag="mm_op", group=f"gop{tg}_{nn}_{i}")
                for i, t in enumerate(tiles):
                    self.ve(lambda e, i=i, t=t, nn=nn: e.tensor_tensor(
                        self.xres[t][:, nn * 512:(nn + 1) * 512],
                        self.xres[t][:, nn * 512:(nn + 1) * 512],
                        self.pb[i][:], ALU.add),
                        reads=[f"pb{i}", f"xres{t}"],
                        writes=[f"xres{t}"] + ([f"opq_used{kk2}" for kk2 in range(16)] if nn == 1 and i == len(tiles) - 1 else []),
                        tag="op_ev")

    def attn_b(self):
        nc, NT, Tc = self.nc, self.NT, self.Tc
        NT1 = NT + 1
        self.TW2 = NT1 * P
        self.qf_hbm = nc.dram_tensor("qf_hbm", [8, P, Tc], BF16)
        self.kf_hbm = nc.dram_tensor("kf_hbm", [8, P, Tc], BF16)
        self.kt_hbm = nc.dram_tensor("kt_hbm", [Tc, 1024], BF16)
        self.uw_hbm = nc.dram_tensor("uw_hbm", [Tc, 3072], BF16)
        self.suw_hbm = nc.dram_tensor("suw_hbm", [Tc, 3072], BF16)
        self.gg_hbm = nc.dram_tensor("gg_hbm", [Tc, 2048], BF16)
        self.yt_hbm = nc.dram_tensor("yt_hbm", [16, P, Tc], BF16)
        self.wout_hbm = nc.dram_tensor("wout_hbm", [16, P, 1024], BF16)
        self.ccb1_in = nc.dram_tensor("ccb1_in", [3, D], F32)
        self.ccb1_out = nc.dram_tensor("ccb1_out", [self.NC * 3, D], F32, **self._shr)
        self.ccb2_in = nc.dram_tensor("ccb2_in", [64, 3072], F32)
        self.ccb2_out = nc.dram_tensor("ccb2_out", [self.NC * 64, 3072], F32, **self._shr)

        self.begin_phase()
        self._ab_consts()
        self.ab_beta = self.sbP("ab_beta", [P, NT * 16], F32)
        self.ab_G = self.sbP("ab_G", [P, NT * 16], F32)
        self.ab_CSE = self.sbP("ab_CSE", [1, NT * 16], F32)
        self.ab_Dbdec = self.sbP("ab_Dbdec", [P, NT * 16], F32)
        self.ab_LamT = self.sbP("ab_LamT", [P, NT * 16], F32)
        self.ab_eG = self.sbP("ab_eG", [P, NT * 16], F32)
        self.ab_S = self.sbP("ab_S", [64, 2048], F32)
        self.rhs99b = self.sbP("rhs99b", [99, 2048], BF16)
        self.lhsT99b = self.sbP("lhsT99b", [99, P], BF16)
        self.s_csp = self.sbP("s_csp", [P, 16], F32)
        self.s_hi = self.sbP("s_hi", [P, 16], BF16)
        self.s_mid = self.sbP("s_mid", [P, 16], BF16)
        self.s_lo = self.sbP("s_lo", [P, 16], BF16)
        self.s_eAt = self.sbP("s_eAt", [P, 2048], BF16)
        self.s_eBt = self.sbP("s_eBt", [P, 2048], BF16)
        self._ab_kron16()

        self.begin_phase()
        self.x2lnT = self.sbP("x2lnT", [P, 8 * NT1 * P], BF16)
        self.begin_phase()
        self._ab_ln()
        self.end_phase()
        self.begin_phase()
        self._ab_proj()
        self.end_phase()
        self.begin_phase()
        self._ab_gg_wout()
        self.end_phase()
        self.end_phase()
        self.begin_phase()
        self._ab_solve()
        self.end_phase()
        self.begin_phase()
        self.begin_phase()
        self._ab_sin()
        self.end_phase()
        self._ab_pass2()
        self.end_phase()
        self.end_phase()

    def _ab_consts(self):
        half01 = np.zeros((P, 2), np.float32)
        half01[0:64, 0] = 1.0
        half01[64:128, 1] = 1.0
        self.c_half01_h = self.const("c_half01", half01)
        sel2 = np.zeros((2, P), np.float32)
        sel2[0, 0:64] = 1.0
        sel2[1, 64:128] = 1.0
        self.c_sel2_h = self.const("c_sel2", sel2)
        self.c_ntrius_h = self.const("c_ntrius",
                                     -np.triu(np.ones((P, P), np.float32), 1))
        self.c_ntril_h = self.const("c_ntril",
                                    -np.tril(np.ones((P, P), np.float32), -1))
        self.c_i64t_h = self.const("c_i64t",
                                   np.tile(np.eye(64, dtype=np.float32), (1, 16)))
        self.prevsel3_hbm = self.inp("prevsel3", [self.NC * 3, 3])

        h01f = self.sbP("half01f", [P, 2], F32)
        self.dma(h01f[:], self.c_half01_h[:], reads=[], writes=["half01f"])
        self.half01 = self.sbP("half01", [P, 2], BF16)
        self.ve(lambda e: e.tensor_copy(self.half01[:], h01f[:]),
                reads=["half01f"], writes=["half01"], tag="h01c")
        self.sel2f = self.sbP("sel2f", [2, P], F32)
        self.dma(self.sel2f[:], self.c_sel2_h[:], reads=[], writes=["sel2f"])
        self.ntrius = self.sbP("ntrius", [P, P], BF16)
        t1 = self.sbP("ntrius_f", [P, P], F32)
        self.dma(t1[:], self.c_ntrius_h[:], reads=[], writes=["ntrius_f"])
        self.ve(lambda e: e.tensor_copy(self.ntrius[:], t1[:]),
                reads=["ntrius_f"], writes=["ntrius"], tag="ntuc")
        self.ntril = self.sbP("ntril", [P, P], BF16)
        t2 = self.sbP("ntril_f", [P, P], F32)
        self.dma(t2[:], self.c_ntril_h[:], reads=[], writes=["ntril_f"])
        self.ve(lambda e: e.tensor_copy(self.ntril[:], t2[:]),
                reads=["ntril_f"], writes=["ntril"], tag="ntlc")
        self.i64t = self.sbP("i64t", [64, 1024], F32)
        self.dma(self.i64t[:], self.c_i64t_h[:], reads=[], writes=["i64t"])
        al = self.rowload(self.W["bw_Alog"], 16, "bal_row")
        nA = self.sbP("bnegA_row", [1, 16], F32)
        self.se(lambda e: e.activation(nA[:], al[:], AF.Exp, bias=0.0, scale=1.0),
                reads=[al.name], writes=["bnegA_row"], tag="bnA1")
        self.ve(lambda e: e.tensor_scalar(nA[:], nA[:], -1.0, None, ALU.mult),
                reads=["bnegA_row"], writes=["bnegA_row"], tag="bnA2")
        self.bnegA = self.bcast_row(nA[:], "bnegA_row", 16, "bnegA_b")
        dtb = self.rowload(self.W["bw_dtb"], 16, "bdtb_row")
        self.bdtb = self.bcast_row(dtb[:], dtb.name, 16, "bdtb_b")
        nr = self.rowload(self.W["bw_norm"], 128, "bnorm_row")
        self.bnorm = self.bcast_row(nr[:], nr.name, 128, "bnorm_b")
        self.bcw = self.sbP("bcw", [P, 32 * 4], F32)
        for i in range(8):
            self.dma(self.bcw[:, i * 4:(i + 1) * 4],
                     self.W["bw_convq"][i * P:(i + 1) * P, :], reads=[],
                     writes=["bcw"], tag="bcwl")
        for i in range(8):
            self.dma(self.bcw[:, 32 + i * 4:32 + (i + 1) * 4],
                     self.W["bw_convk"][i * P:(i + 1) * P, :], reads=[],
                     writes=["bcw"], tag="bcwl")
        for i in range(16):
            self.dma(self.bcw[:, 64 + i * 4:64 + (i + 1) * 4],
                     self.W["bw_convv"][i * P:(i + 1) * P, :], reads=[],
                     writes=["bcw"], tag="bcwl")

    def _ab_kron16(self):
        self.ve(lambda e: e.memset(self.rhs99b[0:96, :], 0.0),
                reads=[], writes=["rhs99b"], tag="rhsbz")
        self.ve(lambda e: e.memset(self.lhsT99b[0:96, :], 0.0),
                reads=[], writes=["lhsT99b"], tag="lhsbz")
        for rr in range(3):
            for h in range(16):
                self.dma(self.rhs99b[32 * rr + h:32 * rr + h + 1,
                                     h * P:(h + 1) * P],
                         self.scrb_hbm[3:4, 0:P],
                         reads=["scrb_nr"], writes=["rhs99b"], q="gpsimd",
                         tag="kron16")
        self.dma(self.lhsT99b[96:99, :], self.scrb_hbm[0:3, 0:P],
                 reads=["scrb3"], writes=["lhsT99b"], q="gpsimd")

    def _ab_ln(self):
        NT, NT1 = self.NT, self.NT + 1
        self.dma(self.ccb1_in[:, :], self.xres[NT - 1][125:128, :],
                 reads=[f"xres{NT-1}"], writes=["ccb1i"], q="gpsimd", tag="b_ccw")
        self.gp(lambda e: e.collective_compute(
            "AllGather", ALU.bypass, replica_groups=[list(range(self.NC))],
            ins=[self.ccb1_in.ap().opt()], outs=[self.ccb1_out.ap().opt()]),
            reads=["ccb1i"], writes=["ccb1o"], kind="cc", tag="b_ag")
        gath = self.sbP("b_gath", [self.NC * 3, D], F32)
        self.dma(gath[:], self.ccb1_out[:, :], reads=["ccb1o"],
                 writes=["b_gath"], q="gpsimd", tag="b_ccr")
        ps3 = self.sbP("b_ps3", [self.NC * 3, 3], F32)
        self.dma(ps3[:], self.prevsel3_hbm[:, :], reads=[], writes=["b_ps3"])
        xh2 = self.sbP("b_xh2", [P, D], F32)
        hstage = self.sbP("b_hstage", [3, D], F32)
        scr2 = self.nc.dram_tensor("b_scr2", [3, D], F32)
        self.ve(lambda e: e.memset(xh2[:], 0.0), reads=[], writes=["b_xh2"],
                tag="bxz")
        for hh in range(2):
            self.pe(lambda e, hh=hh: e.matmul(
                self.pb[hh][0:3, :], ps3[:], gath[:, hh * 512:(hh + 1) * 512],
                start=True, stop=True),
                reads=["b_gath", "b_ps3"], writes=[f"pb{hh}"], tag="bhsel")
            self.ve(lambda e, hh=hh: e.tensor_copy(
                hstage[0:3, hh * 512:(hh + 1) * 512], self.pb[hh][0:3, :]),
                reads=[f"pb{hh}"], writes=["b_hstage"], tag="bhev")
        self.dma(scr2[:, :], hstage[:], reads=["b_hstage"], writes=["b_scr2"],
                 q="gpsimd", tag="bhscw")
        self.dma(xh2[125:128, :], scr2[:, :], reads=["b_scr2", "b_xh2"],
                 writes=["b_xh2"], q="gpsimd", tag="bhscr")
        tiles = [(xh2, "b_xh2")] + [(self.xres[t], f"xres{t}") for t in range(NT)]
        self.ln_stats(tiles)
        xln = self.sbP("b_xln", [P, D], BF16)
        for t in range(NT1):
            xt, xkey = tiles[t]
            self.ln_apply(xt, xkey, t, xln[:], xln.name)
            for kk in range(8):
                self.transpose_to(xln[:, kk * P:(kk + 1) * P], xln.name,
                                  self.x2lnT[:, kk * self.TW2 + t * P:
                                             kk * self.TW2 + (t + 1) * P],
                                  f"x2lnT_t{t}")

    def _ab_proj(self):
        NT, NT1, Tc, TW2 = self.NT, self.NT + 1, self.Tc, self.TW2
        grps = [(g * 512, min(TW2, (g + 1) * 512))
                for g in range((TW2 + 511) // 512)]
        xkeys = [f"x2lnT_t{t}" for t in range(NT1)]
        wbuf = [self.sbP(f"bw_d{i}", [P, 1024], BF16) for i in range(2)]
        wld = [self.sbP(f"bw_ld{i}", [P, 1024], F32) for i in range(2)]
        stage = self.sbP("b_stage", [P, TW2], BF16)
        ct1 = self.sbP("b_ct1", [P, Tc], BF16)
        ct2 = self.sbP("b_ct2", [P, Tc], BF16)
        cf = self.sbP("b_cf", [P, Tc], BF16)
        sq = self.sbP("b_sq", [P, Tc], BF16)
        ssb2 = self.sbP("b_ss2", [2, Tc], F32)
        rst = self.sbP("b_rst", [2, Tc], F32)
        stg = self.sbP("b_stg", [P, 512], BF16)
        e6 = self.fconst(1e-6)
        lg8 = self.fconst(float(np.log(1.0 / 8.0)))

        def wsrc(dc):
            if dc < 8:
                return self.W["bw_q"], dc, "q"
            if dc < 16:
                return self.W["bw_k"], dc - 8, "k"
            return self.W["bw_v"], dc - 16, "v"

        for dc in range(32):
            hbm, dcl, which = wsrc(dc)
            b = dc % 2
            self.dma(wld[b][:].rearrange("r (kk c) -> r kk c", kk=8),
                     hbm[:, dcl * P:(dcl + 1) * P].rearrange(
                         "(kk r) c -> r kk c", r=P),
                     reads=[f"bwld_used{b}"], writes=[f"bw_ld{b}"], tag="bwl")
            self.ve(lambda e, b=b: e.tensor_copy(wbuf[b][:], wld[b][:]),
                    reads=[f"bw_ld{b}"], writes=[f"bw_d{b}", f"bwld_used{b}"],
                    tag="bwc")
            for gi, (lo, hi) in enumerate(grps):
                pbk = self.pb[gi % 2]
                pk = f"pb{gi%2}"
                g = self.nm("gbp")
                for kk in range(8):
                    self.pe(lambda e, b=b, kk=kk, lo=lo, hi=hi, pbk=pbk: e.matmul(
                        pbk[0:P, 0:hi - lo], wbuf[b][:, kk * P:(kk + 1) * P],
                        self.x2lnT[:, kk * TW2 + lo:kk * TW2 + hi],
                        start=(kk == 0), stop=(kk == 7)),
                        reads=[f"bw_d{b}"] + xkeys, writes=[pk], tag="mm_bp",
                        group=g)
                self.ve(lambda e, lo=lo, hi=hi, pbk=pbk: e.tensor_copy(
                    stage[:, lo:hi], pbk[0:P, 0:hi - lo]),
                    reads=[pk], writes=["b_stage"], tag="bpev")
            cwc = [self.bcw[:, dc * 4 + k:dc * 4 + k + 1] for k in range(4)]
            self.ve(lambda e, cwc=cwc: e.tensor_scalar(
                ct1[:], stage[:, 125:125 + Tc], cwc[0], None, ALU.mult),
                reads=["b_stage", "bcw"], writes=["b_ct1"], tag="bcv0")
            self.ve(lambda e, cwc=cwc: e.scalar_tensor_tensor(
                ct2[:], stage[:, 126:126 + Tc], cwc[1], ct1[:], ALU.mult, ALU.add),
                reads=["b_stage", "bcw", "b_ct1"], writes=["b_ct2"], tag="bcv1")
            self.ve(lambda e, cwc=cwc: e.scalar_tensor_tensor(
                ct1[:], stage[:, 127:127 + Tc], cwc[2], ct2[:], ALU.mult, ALU.add),
                reads=["b_stage", "bcw", "b_ct2"], writes=["b_ct1"], tag="bcv2")
            self.ve(lambda e, cwc=cwc: e.scalar_tensor_tensor(
                ct2[:], stage[:, 128:128 + Tc], cwc[3], ct1[:], ALU.mult, ALU.add),
                reads=["b_stage", "bcw", "b_ct1"], writes=["b_ct2"], tag="bcv3")
            self.se(lambda e: e.activation(cf[:], ct2[:], AF.Silu, bias=0.0,
                                           scale=1.0),
                    reads=["b_ct2"], writes=["b_cf"], tag="bcsilu")
            if which in ("q", "k"):
                self.se(lambda e: e.activation(sq[:], cf[:], AF.Square, bias=0.0,
                                               scale=1.0),
                        reads=["b_cf"], writes=["b_sq"], tag="bsq")
                for c0 in range(0, Tc, 512):
                    cw = min(512, Tc - c0)
                    self.pe(lambda e, c0=c0, cw=cw: e.matmul(
                        self.pb[2][0:2, 0:cw], self.half01[:], sq[:, c0:c0 + cw],
                        start=True, stop=True),
                        reads=["b_sq", "half01"], writes=["pb2"], tag="mm_ss")
                    self.ve(lambda e, c0=c0, cw=cw: e.tensor_copy(
                        ssb2[0:2, c0:c0 + cw], self.pb[2][0:2, 0:cw]),
                        reads=["pb2"], writes=["b_ss2"], tag="ssev")
                self.se(lambda e: e.activation(rst[:], ssb2[:], AF.Ln,
                                               bias=e6[0:2, 0:1], scale=1.0),
                        reads=["b_ss2", e6.name], writes=["b_rst"], tag="bl2ln")
                if which == "q":
                    self.se(lambda e: e.activation(rst[:], rst[:], AF.Exp,
                                                   bias=lg8[0:2, 0:1],
                                                   scale=-0.5),
                            reads=["b_rst", lg8.name], writes=["b_rst"],
                            tag="bl2exq")
                else:
                    self.se(lambda e: e.activation(rst[:], rst[:], AF.Exp,
                                                   bias=0.0, scale=-0.5),
                            reads=["b_rst"], writes=["b_rst"], tag="bl2exk")
                for c0 in range(0, Tc, 512):
                    cw = min(512, Tc - c0)
                    self.pe(lambda e, c0=c0, cw=cw: e.matmul(
                        self.pb[3][0:P, 0:cw], self.sel2f[:],
                        rst[0:2, c0:c0 + cw], start=True, stop=True),
                        reads=["b_rst", "sel2f"], writes=["pb3"], tag="mm_rsb")
                    self.ve(lambda e, c0=c0, cw=cw: e.tensor_tensor(
                        cf[:, c0:c0 + cw], cf[:, c0:c0 + cw],
                        self.pb[3][0:P, 0:cw], ALU.mult),
                        reads=["pb3", "b_cf"], writes=["b_cf"], tag="bqks")
                dst = self.qf_hbm if which == "q" else self.kf_hbm
                self.dma(dst[dcl, :, :], cf[:], reads=["b_cf"],
                         writes=["qf_sp" if which == "q" else "kf_sp"],
                         q="gpsimd", tag="bqk_sp")
                if which == "k":
                    for t0 in range(0, NT, 4):
                        n4 = min(4, NT - t0)
                        for i in range(n4):
                            tp = self.tpb4[i % 2][:]
                            self.pe(lambda e, tp=tp, t=t0 + i: e.transpose(
                                tp[:], cf[:, t * P:(t + 1) * P], self.id_bf[:]),
                                reads=["b_cf", "id_bf"], writes=[f"tpb4_{i % 2}"],
                                tag="bkT")
                            self.ve(lambda e, tp=tp, i=i: e.tensor_copy(
                                stg[:, i * P:(i + 1) * P], tp),
                                reads=[f"tpb4_{i % 2}"], writes=["b_stg"],
                                tag="bkTe")
                        for i in range(n4):
                            t_abs = t0 + i
                            self.dma(self.kt_hbm[t_abs * P:(t_abs + 1) * P,
                                                 dcl * P:(dcl + 1) * P],
                                     stg[:, i * P:(i + 1) * P],
                                     reads=["b_stg"], writes=["kt_sp"],
                                     q="gpsimd", tag="bkt_sp")
                            for hf in range(2):
                                h = dcl * 2 + hf
                                self.dma(
                                    self.uw_hbm[t_abs * P:(t_abs + 1) * P,
                                                h * 192 + 128:h * 192 + 192],
                                    stg[:, i * P + hf * 64:i * P + hf * 64 + 64],
                                    reads=["b_stg"], writes=["uw_sp"],
                                    q="gpsimd", tag="bku_sp")
            else:
                for t0 in range(0, NT, 4):
                    n4 = min(4, NT - t0)
                    for i in range(n4):
                        tp = self.tpb4[i % 2][:]
                        self.pe(lambda e, tp=tp, t=t0 + i: e.transpose(
                            tp[:], cf[:, t * P:(t + 1) * P], self.id_bf[:]),
                            reads=["b_cf", "id_bf"], writes=[f"tpb4_{i % 2}"],
                            tag="bvT")
                        self.ve(lambda e, tp=tp, i=i: e.tensor_copy(
                            stg[:, i * P:(i + 1) * P], tp),
                            reads=[f"tpb4_{i % 2}"], writes=["b_stg"], tag="bvTe")
                    for i in range(n4):
                        self.dma(self.uw_hbm[(t0 + i) * P:(t0 + i + 1) * P,
                                             dcl * 192:dcl * 192 + 128],
                                 stg[:, i * P:(i + 1) * P],
                                 reads=["b_stg"], writes=["uw_sp"], q="gpsimd",
                                 tag="bv_sp")
        self._ab_ab()

    def _ab_ab(self):
        NT, TW2 = self.NT, self.TW2
        wab = self.sbP("b_wab", [P, 8 * 32], BF16)
        wab_f = self.sbP("b_wabf", [P, 32], F32)
        for kk in range(8):
            self.dma(wab_f[:, 0:16], self.W["bw_a"][kk * P:(kk + 1) * P, :],
                     reads=["b_wabf"], writes=["b_wabf"], tag="babl")
            self.dma(wab_f[:, 16:32], self.W["bw_b"][kk * P:(kk + 1) * P, :],
                     reads=["b_wabf"], writes=["b_wabf"], tag="babl")
            self.ve(lambda e, kk=kk: e.tensor_copy(wab[:, kk * 32:(kk + 1) * 32],
                                                   wab_f[:]),
                    reads=["b_wabf"], writes=["b_wab"], tag="babc")
        abp = self.sbP("b_abp", [P, NT * 32], F32)
        for t in range(NT):
            g = self.nm("gab")
            for kk in range(8):
                self.pe(lambda e, kk=kk, t=t: e.matmul(
                    self.pb[2][0:P, 0:32],
                    self.x2lnT[:, kk * TW2 + (t + 1) * P:kk * TW2 + (t + 2) * P],
                    wab[:, kk * 32:(kk + 1) * 32], start=(kk == 0), stop=(kk == 7)),
                    reads=["b_wab", f"x2lnT_t{t+1}"], writes=["pb2"], tag="mm_ab",
                    group=g)
            self.ve(lambda e, t=t: e.tensor_copy(
                abp[:, t * 32:(t + 1) * 32], self.pb[2][0:P, 0:32]),
                reads=["pb2"], writes=["b_abp"], tag="abev")
        av = abp[:].rearrange("p (t two h) -> p t two h", two=2, h=16)[:, :, 0:1, :]
        bv = abp[:].rearrange("p (t two h) -> p t two h", two=2, h=16)[:, :, 1:2, :]
        bview = self.ab_beta[:].rearrange("p (t h) -> p t h", h=16).unsqueeze(2)
        self.se(lambda e: e.activation(bview, bv, AF.Sigmoid, bias=0.0, scale=1.0),
                reads=["b_abp"], writes=["ab_beta"], tag="bsig")
        self.ve(lambda e: e.tensor_scalar(self.ab_beta[:], self.ab_beta[:], 2.0,
                                          None, ALU.mult),
                reads=["ab_beta"], writes=["ab_beta"], tag="b2x")
        gt = self.sbP("b_gt", [P, NT * 16], F32)
        self.ve(lambda e: e.tensor_tensor(
            gt[:].rearrange("p (t h) -> p t h", h=16).unsqueeze(2), av,
            self.bdtb[:].unsqueeze(1).unsqueeze(2).broadcast_to((P, NT, 1, 16)),
            ALU.add),
            reads=["b_abp", "bdtb_b"], writes=["b_gt"], tag="bgb")
        ab_ = self.sbP("b_gabs", [P, NT * 16], F32)
        self.se(lambda e: e.activation(ab_[:], gt[:], AF.Abs, bias=0.0,
                                       scale=1.0),
                reads=["b_gt"], writes=["b_gabs"], tag="bgab")
        one = self.fconst(1.0)
        self.se(lambda e: e.activation(ab_[:], ab_[:], AF.Exp, bias=0.0, scale=-1.0),
                reads=["b_gabs"], writes=["b_gabs"], tag="bgex")
        self.se(lambda e: e.activation(ab_[:], ab_[:], AF.Ln, bias=one[:], scale=1.0),
                reads=["b_gabs", one.name], writes=["b_gabs"], tag="bgln")
        rl = self.sbP("b_grl", [P, NT * 16], F32)
        self.ve(lambda e: e.tensor_scalar(rl[:], gt[:], 0.0, None, ALU.max),
                reads=["b_gt"], writes=["b_grl"], tag="bgrl")
        self.ve(lambda e: e.tensor_tensor(gt[:], rl[:], ab_[:], ALU.add),
                reads=["b_grl", "b_gabs"], writes=["b_gt"], tag="bgsp")
        gv = self.sbP("b_g", [P, NT * 16], F32)
        self.ve(lambda e: e.tensor_tensor(
            gv[:].rearrange("p (t h) -> p t h", h=16),
            gt[:].rearrange("p (t h) -> p t h", h=16),
            self.bnegA[:].unsqueeze(1).broadcast_to((P, NT, 16)), ALU.mult),
            reads=["b_gt", "bnegA_b"], writes=["b_g"], tag="bgmk")
        for ch in range(NT):
            s3 = self._tps % 4
            self._tps += 1
            tp = self.tpf[:, 128 * s3:128 * s3 + 128]
            tk = "tpfB"
            self.pe(lambda e, tp=tp, ch=ch: e.matmul(
                tp[0:P, 0:16], self.tri_f32[:], gv[:, ch * 16:(ch + 1) * 16],
                start=True, stop=True),
                reads=["b_g", "tri_f32"], writes=[tk], tag="bcs_mm")
            self.ve(lambda e, tp=tp, ch=ch: e.tensor_copy(
                self.ab_G[:, ch * 16:(ch + 1) * 16], tp[0:P, 0:16]),
                reads=[tk], writes=["ab_G"], tag="bcs_ev")
            s4 = self._tps % 4
            self._tps += 1
            tp2 = self.tpf[:, 128 * s4:128 * s4 + 128]
            tk2 = "tpfB"
            self.pe(lambda e, tp2=tp2, ch=ch: e.matmul(
                tp2[0:1, 0:16], self.sel_last[:],
                self.ab_G[:, ch * 16:(ch + 1) * 16], start=True, stop=True),
                reads=["ab_G", "sel_last"], writes=[tk2], tag="bcse_mm")
            self.ve(lambda e, tp2=tp2, ch=ch: e.tensor_copy(
                self.ab_CSE[0:1, ch * 16:(ch + 1) * 16], tp2[0:1, 0:16]),
                reads=[tk2], writes=["ab_CSE"], tag="bcse_ev")
            s5 = self._tps % 4
            self._tps += 1
            tp3 = self.tpf[:, 128 * s5:128 * s5 + 128]
            tk3 = "tpfB"
            self.pe(lambda e, tp3=tp3, ch=ch: e.matmul(
                tp3[0:P, 0:16], self.ones_row[:],
                self.ab_CSE[0:1, ch * 16:(ch + 1) * 16], start=True, stop=True),
                reads=["ab_CSE", "ones_row"], writes=[tk3], tag="bdd_bc")
            self.ve(lambda e, tp3=tp3, ch=ch: e.tensor_tensor(
                self.ab_Dbdec[:, ch * 16:(ch + 1) * 16], tp3[0:P, 0:16],
                self.ab_G[:, ch * 16:(ch + 1) * 16], ALU.subtract),
                reads=[tk3, "ab_G"], writes=["ab_Dbdec"], tag="bdd_sub")
        self.se(lambda e: e.activation(self.ab_Dbdec[:], self.ab_Dbdec[:], AF.Exp,
                                       bias=0.0, scale=1.0),
                reads=["ab_Dbdec"], writes=["ab_Dbdec"], tag="bdd_exp")
        self.se(lambda e: e.activation(self.ab_LamT[:], self.ab_G[:], AF.Exp,
                                       bias=0.0, scale=1.0),
                reads=["ab_G"], writes=["ab_LamT"], tag="bLam")
        eC = self.sbP("b_eCSE", [1, NT * 16], F32)
        self.se(lambda e: e.activation(eC[:], self.ab_CSE[:], AF.Exp, bias=0.0,
                                       scale=1.0),
                reads=["ab_CSE"], writes=["b_eCSE"], tag="becse")
        nb = (NT * 16 + 511) // 512
        for bn in range(nb):
            lo, hi = bn * 512, min(NT * 16, (bn + 1) * 512)
            pbk = self.pb[bn % 2]
            self.pe(lambda e, lo=lo, hi=hi, pbk=pbk: e.matmul(
                pbk[0:P, 0:hi - lo], self.ones_row[:], eC[0:1, lo:hi],
                start=True, stop=True),
                reads=["b_eCSE", "ones_row"], writes=[f"pb{bn%2}"], tag="beG_bc")
            self.ve(lambda e, lo=lo, hi=hi, pbk=pbk: e.tensor_copy(
                self.ab_eG[:, lo:hi], pbk[0:P, 0:hi - lo]),
                reads=[f"pb{bn%2}"], writes=["ab_eG"], tag="beG_ev")

    def _ab_gg_wout(self):
        NT, TW2 = self.NT, self.TW2
        zw = [self.sbP(f"b_zw{i}", [P, 512], BF16) for i in range(2)]
        zf = self.sbP("b_zf", [P, 512], F32)
        zs = self.sbP("b_zsil", [P, 512], BF16)
        ngw = 0
        for tg in range((NT + 3) // 4):
            tiles = [tg * 4 + i for i in range(4) if tg * 4 + i < NT]
            for zg in range(4):
                g = self.nm("gz2")
                for kk in range(8):
                    b = ngw % 2
                    ngw += 1
                    self.dma(zf[:], self.W["bw_g"][kk * P:(kk + 1) * P,
                                                   zg * 512:(zg + 1) * 512],
                             reads=["b_zf"], writes=["b_zf"], tag="ggwl")
                    self.ve(lambda e, b=b: e.tensor_copy(zw[b][:], zf[:]),
                            reads=["b_zf"], writes=[f"b_zw{b}"], tag="ggwc")
                    for i, t in enumerate(tiles):
                        self.pe(lambda e, b=b, kk=kk, t=t, i=i: e.matmul(
                            self.pb[i][:],
                            self.x2lnT[:, kk * TW2 + (t + 1) * P:
                                       kk * TW2 + (t + 2) * P],
                            zw[b][:], start=(kk == 0), stop=(kk == 7)),
                            reads=[f"b_zw{b}", f"x2lnT_t{t+1}"],
                            writes=[f"pb{i}"], tag="mm_gg", group=g + str(i))
                for i, t in enumerate(tiles):
                    self.se(lambda e, i=i: e.activation(
                        zs[:], self.pb[i][:], AF.Silu, bias=0.0, scale=1.0),
                        reads=[f"pb{i}"], writes=["b_zsil"], tag="ggsilu")
                    self.dma(self.gg_hbm[t * P:(t + 1) * P,
                                         zg * 512:(zg + 1) * 512],
                             zs[:], reads=["b_zsil"], writes=["gg_sp"],
                             q="gpsimd", tag="ggspill")
        wo_f = self.sbP("b_wof", [P, 1024], F32)
        wo_b = self.sbP("b_wob", [P, 1024], BF16)
        for kk in range(16):
            self.dma(wo_f[:], self.W["bw_out"][kk * P:(kk + 1) * P, :],
                     reads=["b_wof"], writes=["b_wof"], tag="wol")
            self.ve(lambda e: e.tensor_copy(wo_b[:], wo_f[:]),
                    reads=["b_wof"], writes=["b_wob"], tag="woc")
            self.dma(self.wout_hbm[kk, :, :], wo_b[:], reads=["b_wob"],
                     writes=["wout_sp", "b_wof"], q="gpsimd", tag="wosp")

    def _ab_logD(self, ch, orientA=True, orientB=True):
        csp, hi, mid, lo = self.s_csp, self.s_hi, self.s_mid, self.s_lo
        self.ve(lambda e, ch=ch: e.tensor_copy(
            csp[:], self.ab_G[:, ch * 16:(ch + 1) * 16]),
            reads=["ab_G"], writes=["s_csp"], tag="bcsp")
        self.ve(lambda e: e.tensor_copy(hi[:], csp[:]),
                reads=["s_csp"], writes=["s_hi"], tag="bhi")
        self.ve(lambda e: e.tensor_tensor(csp[:], csp[:], hi[:], ALU.subtract),
                reads=["s_csp", "s_hi"], writes=["s_csp"], tag="br1")
        self.ve(lambda e: e.tensor_copy(mid[:], csp[:]),
                reads=["s_csp"], writes=["s_mid"], tag="bmid")
        self.ve(lambda e: e.tensor_tensor(csp[:], csp[:], mid[:], ALU.subtract),
                reads=["s_csp", "s_mid"], writes=["s_csp"], tag="br2")
        self.ve(lambda e: e.tensor_copy(lo[:], csp[:]),
                reads=["s_csp"], writes=["s_lo"], tag="blo")
        for rr, (t_bf, t_key) in enumerate(
                ((hi, "s_hi"), (mid, "s_mid"), (lo, "s_lo"))):
            tpb = self.tpb4[rr % 2][:]
            self.pe(lambda e, tpb=tpb, t_bf=t_bf: e.transpose(
                tpb[0:16, 0:P], t_bf[:], self.id_bf[:]),
                reads=[t_key, "id_bf"], writes=[f"tpb4_{rr % 2}"], tag="bcsT")
            self.ve(lambda e, tpb=tpb, rr=rr: e.tensor_copy(
                self.lhsT99b[32 * rr:32 * rr + 16, :], tpb[0:16, 0:P]),
                reads=[f"tpb4_{rr % 2}"], writes=["lhsT99b"], tag="bcsTe")
            self.dma(self.scrb_hbm[4 + rr:5 + rr, 0:2048].rearrange(
                "a (h l) -> (a h) l", h=16),
                self.lhsT99b[32 * rr:32 * rr + 16, :],
                reads=["lhsT99b"], writes=[f"bfl{rr}"], q="gpsimd", tag="bflw")
            self.dma(self.rhs99b[96 + rr:97 + rr, :],
                     self.scrb_hbm[4 + rr:5 + rr, 0:2048],
                     reads=[f"bfl{rr}"], writes=["rhs99b"], q="gpsimd",
                     tag="bflr")
        g = self.nm("gld")
        for q in range(4):
            self.pe(lambda e, q=q: e.matmul(
                self.pb[q][:], self.lhsT99b[:],
                self.rhs99b[:, q * 512:(q + 1) * 512], start=True, stop=True),
                reads=["lhsT99b", "rhs99b"], writes=[f"pb{q}"], tag="mm_ld",
                group=g)
        if orientA:
            for q in range(4):
                self.ve(lambda e, q=q: e.tensor_scalar(
                    self.s_eAt[:, q * 512:(q + 1) * 512], self.pb[q][:], 0.0,
                    None, ALU.min),
                    reads=[f"pb{q}"], writes=["s_eAt"], tag="ldAev")
            self.se(lambda e: e.activation(self.s_eAt[:], self.s_eAt[:], AF.Exp,
                                           bias=0.0, scale=1.0),
                    reads=["s_eAt"], writes=["s_eAt"], tag="ldAexp")
        if orientB:
            for q in range(4):
                self.ve(lambda e, q=q: e.tensor_scalar(
                    self.s_eBt[:, q * 512:(q + 1) * 512], self.pb[q][:], -1.0,
                    0.0, ALU.mult, ALU.min),
                    reads=[f"pb{q}"], writes=["s_eBt"], tag="ldBev")
            self.se(lambda e: e.activation(self.s_eBt[:], self.s_eBt[:], AF.Exp,
                                           bias=0.0, scale=1.0),
                    reads=["s_eBt"], writes=["s_eBt"], tag="ldBexp")

    def _ab_solve(self):
        NT, Tc = self.NT, self.Tc
        kfc = self.sbP("s_kfc", [64, 2048], BF16)
        ktc = self.sbP("s_ktc", [P, 1024], BF16)
        kbt = self.sbP("s_kbt", [P, 1024], BF16)
        kbf = self.sbP("s_kbf", [64, 2048], BF16)
        U1 = self.sbP("s_U1", [P, 2048], BF16)
        U2 = self.sbP("s_U2", [P, 1024], BF16)
        Bh = [self.sbP(f"s_Bh{i}", [P, 2048], BF16) for i in range(2)]
        Bu = [self.sbP(f"s_Bu{i}", [P, 2048], BF16) for i in range(2)]
        Kt = self.sbP("s_Kt", [P, 1024], BF16)
        bl = self.sbP("s_bl", [P, 16], F32)
        vt = [self.sbP(f"s_vt{i}", [P, 3072], BF16) for i in range(2)]
        self.s_pcT = self.sbP("s_pcT", [64, 1024], F32)
        self.s_pcU = self.sbP("s_pcU", [64, 1024], F32)
        self.s_lamI = self.sbP("s_lamI", [64, 1024], F32)
        self.s_pcTb = self.sbP("s_pcTb", [64, 1024], BF16)
        self.s_Pub = self.sbP("s_Pub", [64, 1024], BF16)
        self.s_Qrb = self.sbP("s_Qrb", [64, 2048], BF16)
        self.s_Pr = self.sbP("s_Pr", [64, 1024], F32)
        self.s_Pu = self.sbP("s_Pu", [64, 1024], F32)
        self.s_Qr = self.sbP("s_Qr", [64, 2048], F32)

        for ch in range(NT):
            b = ch % 2
            for h2 in range(2):
                self.dma(kfc[:].rearrange("p (d t) -> p d t", t=256)[
                    :, :, h2 * 128:(h2 + 1) * 128],
                    self.kf_hbm[:, h2 * 64:h2 * 64 + 64,
                                ch * P:(ch + 1) * P].rearrange("d p t -> p d t"),
                    reads=["kf_sp", "s_kfc"], writes=["s_kfc"], tag="skf")
            self.dma(ktc[:], self.kt_hbm[ch * P:(ch + 1) * P, :],
                     reads=["kt_sp", "s_ktc"], writes=["s_ktc"], tag="skt")
            self.dma(vt[b][:], self.uw_hbm[ch * P:(ch + 1) * P, :],
                     reads=["uw_sp", f"s_vt{b}"], writes=[f"s_vt{b}"], tag="svt")
            self.ve(lambda e, ch=ch: e.tensor_tensor(
                kbt[:].rearrange("p (h k) -> p h k", h=16),
                ktc[:].rearrange("p (h k) -> p h k", h=16),
                self.ab_beta[:, ch * 16:(ch + 1) * 16].unsqueeze(2).broadcast_to(
                    (P, 16, 64)), ALU.mult),
                reads=["s_ktc", "ab_beta"], writes=["s_kbt"], tag="skbt")
            self.ve(lambda e, ch=ch: e.tensor_tensor(
                bl[:], self.ab_beta[:, ch * 16:(ch + 1) * 16],
                self.ab_LamT[:, ch * 16:(ch + 1) * 16], ALU.mult),
                reads=["ab_beta", "ab_LamT"], writes=["s_bl"], tag="sbl")
            self.ve(lambda e: e.tensor_tensor(
                U2[:].rearrange("p (h k) -> p h k", h=16),
                ktc[:].rearrange("p (h k) -> p h k", h=16),
                bl[:].unsqueeze(2).broadcast_to((P, 16, 64)), ALU.mult),
                reads=["s_ktc", "s_bl"], writes=["s_U2"], tag="sU2")
            self.ve(lambda e, ch=ch, b=b: e.tensor_tensor(
                U1[:].rearrange("p (h k) -> p h k", h=16),
                vt[b][:].rearrange("p (h c) -> p h c", c=192)[:, :, 0:128],
                self.ab_beta[:, ch * 16:(ch + 1) * 16].unsqueeze(2).broadcast_to(
                    (P, 16, 128)), ALU.mult),
                reads=[f"s_vt{b}", "ab_beta"], writes=["s_U1"], tag="sU1")
            self.ve(lambda e, ch=ch: e.tensor_tensor(
                Kt[:].rearrange("p (h k) -> p h k", h=16),
                ktc[:].rearrange("p (h k) -> p h k", h=16),
                self.ab_Dbdec[:, ch * 16:(ch + 1) * 16].unsqueeze(2).broadcast_to(
                    (P, 16, 64)), ALU.mult),
                reads=["s_ktc", "ab_Dbdec"], writes=["s_Kt"], tag="sKt")
            for i in range(8):
                sl = i % 2
                tp = self.tpb4[sl][:]
                self.pe(lambda e, tp=tp, i=i: e.transpose(
                    tp[:], kbt[:, i * P:(i + 1) * P], self.id_bf[:]),
                    reads=["s_kbt", "id_bf"], writes=[f"tpb4_{sl}"], tag="skbT")
                self.ve(lambda e, tp=tp, i=i: e.tensor_copy(
                    kbf[0:64, (2 * i) * P:(2 * i + 1) * P], tp[0:64, :]),
                    reads=[f"tpb4_{sl}"], writes=["s_kbf"], tag="skbTe")
                self.ve(lambda e, tp=tp, i=i: e.tensor_copy(
                    kbf[0:64, (2 * i + 1) * P:(2 * i + 2) * P], tp[64:128, :]),
                    reads=[f"tpb4_{sl}"], writes=["s_kbf"], tag="skbTe2")
            self._ab_logD(ch)
            self._ab_kkt(kfc, kbf, Bh[0], Bu[0])
            for j in range(7):
                cj, nj = j % 2, (j + 1) % 2
                for hg in range(2):
                    g1 = self.nm("gbu")
                    for hh in range(8):
                        h = hg * 8 + hh
                        self.pe(lambda e, h=h, hh=hh, cj=cj: e.matmul(
                            self.pb[hh // 4][:, (hh % 4) * 128:(hh % 4) * 128 + 128],
                            Bh[cj][:, h * P:(h + 1) * P], U1[:, h * P:(h + 1) * P],
                            start=True, stop=True),
                            reads=[f"s_Bh{cj}", "s_U1"], writes=[f"pb{hh//4}"],
                            tag="mm_bu1", group=g1 + str(hh // 4))
                        self.pe(lambda e, h=h, hh=hh, cj=cj: e.matmul(
                            self.pb[2][:, hh * 64:hh * 64 + 64],
                            Bh[cj][:, h * P:(h + 1) * P],
                            U2[:, h * 64:(h + 1) * 64],
                            start=True, stop=True),
                            reads=[f"s_Bh{cj}", "s_U2"], writes=["pb2"],
                            tag="mm_bu2", group=g1 + "u")
                    for qq in range(2):
                        self.ve(lambda e, hg=hg, qq=qq: e.tensor_tensor(
                            U1[:, hg * 1024 + qq * 512:hg * 1024 + (qq + 1) * 512],
                            U1[:, hg * 1024 + qq * 512:hg * 1024 + (qq + 1) * 512],
                            self.pb[qq][:], ALU.add),
                            reads=[f"pb{qq}", "s_U1"], writes=["s_U1"],
                            tag="bu1ev")
                    self.ve(lambda e, hg=hg: e.tensor_tensor(
                        U2[:, hg * 512:(hg + 1) * 512],
                        U2[:, hg * 512:(hg + 1) * 512], self.pb[2][:], ALU.add),
                        reads=["pb2", "s_U2"], writes=["s_U2"], tag="bu2ev")
                if j == 6:
                    break
                for r0 in range(4):
                    g3 = self.nm("gsq")
                    for hh in range(4):
                        h = r0 * 4 + hh
                        self.pe(lambda e, h=h, hh=hh, cj=cj: e.matmul(
                            self.pb[hh][:, 0:128],
                            Bu[cj][:, h * P:(h + 1) * P],
                            Bh[cj][:, h * P:(h + 1) * P],
                            start=True, stop=True),
                            reads=[f"s_Bu{cj}", f"s_Bh{cj}"],
                            writes=[f"pb{hh}"], tag="mm_sqh", group=g3)
                        self.pe(lambda e, h=h, hh=hh, cj=cj: e.matmul(
                            self.pb[hh][:, 128:256],
                            Bh[cj][:, h * P:(h + 1) * P],
                            Bu[cj][:, h * P:(h + 1) * P],
                            start=True, stop=True),
                            reads=[f"s_Bu{cj}", f"s_Bh{cj}"],
                            writes=[f"pb{hh}"], tag="mm_squ", group=g3)
                    for hh in range(4):
                        h = r0 * 4 + hh
                        self.ve(lambda e, h=h, hh=hh, nj=nj: e.tensor_copy(
                            Bh[nj][:, h * P:(h + 1) * P], self.pb[hh][:, 0:128]),
                            reads=[f"pb{hh}"], writes=[f"s_Bh{nj}"], tag="sqhev")
                        self.ve(lambda e, h=h, hh=hh, nj=nj: e.tensor_copy(
                            Bu[nj][:, h * P:(h + 1) * P], self.pb[hh][:, 128:256]),
                            reads=[f"pb{hh}"], writes=[f"s_Bu{nj}"], tag="squev")
            self.ve(lambda e, b=b: e.tensor_copy(
                vt[b][:].rearrange("p (h c) -> p h c", c=192)[:, :, 0:128],
                U1[:].rearrange("p (h c) -> p h c", c=128)),
                reads=["s_U1", f"s_vt{b}"], writes=[f"s_vt{b}"], tag="suw1")
            self.ve(lambda e, b=b: e.tensor_copy(
                vt[b][:].rearrange("p (h c) -> p h c", c=192)[:, :, 128:192],
                U2[:].rearrange("p (h c) -> p h c", c=64)),
                reads=["s_U2", f"s_vt{b}"], writes=[f"s_vt{b}"], tag="suw2")
            self.dma(self.suw_hbm[ch * P:(ch + 1) * P, :], vt[b][:],
                     reads=[f"s_vt{b}"], writes=["suw_sp"], q="gpsimd",
                     tag="suw_sp")
            self.ve(lambda e, ch=ch: e.tensor_tensor(
                self.s_lamI[:].rearrange("p (h k) -> p h k", h=16),
                self.i64t[:].rearrange("p (h k) -> p h k", h=16),
                self.ab_eG[0:64, ch * 16:(ch + 1) * 16].unsqueeze(2).broadcast_to(
                    (64, 16, 64)), ALU.mult),
                reads=["i64t", "ab_eG"], writes=["s_lamI"], tag="lamI")
            for grp in range(2):
                gph = self.nm("gph")
                for hh in range(8):
                    h = grp * 8 + hh
                    self.pe(lambda e, h=h, hh=hh: e.matmul(
                        self.pb[hh // 2][0:64, (hh % 2) * 256:(hh % 2) * 256 + 64],
                        U2[:, h * 64:(h + 1) * 64], Kt[:, h * 64:(h + 1) * 64],
                        start=True, stop=True),
                        reads=["s_U2", "s_Kt"], writes=[f"pb{hh//2}"],
                        tag="mm_ph", group=gph)
                    self.pe(lambda e, h=h, hh=hh: e.matmul(
                        self.pb[hh // 2][0:64, (hh % 2) * 256 + 64:
                                         (hh % 2) * 256 + 128],
                        Kt[:, h * 64:(h + 1) * 64], U2[:, h * 64:(h + 1) * 64],
                        start=True, stop=True),
                        reads=["s_U2", "s_Kt"], writes=[f"pb{hh//2}"],
                        tag="mm_pu", group=gph)
                for hh in range(8):
                    h = grp * 8 + hh
                    base = (hh % 2) * 256
                    self.ve(lambda e, h=h, hh=hh, base=base:
                            e.scalar_tensor_tensor(
                                self.s_pcT[0:64, h * 64:(h + 1) * 64],
                                self.pb[hh // 2][0:64, base:base + 64], -1.0,
                                self.s_lamI[0:64, h * 64:(h + 1) * 64],
                                ALU.mult, ALU.add),
                            reads=[f"pb{hh//2}", "s_lamI"], writes=["s_pcT"],
                            tag="phev")
                    self.ve(lambda e, h=h, hh=hh, base=base:
                            e.scalar_tensor_tensor(
                                self.s_pcU[0:64, h * 64:(h + 1) * 64],
                                self.pb[hh // 2][0:64, base + 64:base + 128],
                                -1.0, self.s_lamI[0:64, h * 64:(h + 1) * 64],
                                ALU.mult, ALU.add),
                            reads=[f"pb{hh//2}", "s_lamI"], writes=["s_pcU"],
                            tag="puev")
            self._ab_compose(ch, Kt, U1, ch == 0)
        self.dma(self.ccb2_in[:, 0:1024], self.s_Pr[:], reads=["s_Pr"],
                 writes=["ccb2i"], q="gpsimd", tag="e3w1")
        self.dma(self.ccb2_in[:, 1024:3072], self.s_Qr[:], reads=["s_Qr"],
                 writes=["ccb2i2"], q="gpsimd", tag="e3w2")
        self.gp(lambda e: e.collective_compute(
            "AllGather", ALU.bypass, replica_groups=[list(range(self.NC))],
            ins=[self.ccb2_in.ap().opt()], outs=[self.ccb2_out.ap().opt()]),
            reads=["ccb2i", "ccb2i2"], writes=["ccb2o"], kind="cc", tag="e3ag")

    def _ab_kkt(self, kfc, kbf, B0h, B0u):
        for q in range(4):
            g = self.nm("gk1")
            for hh in range(4):
                h = q * 4 + hh
                self.pe(lambda e, h=h, hh=hh, q=q: e.matmul(
                    self.pb[q][:, hh * 128:(hh + 1) * 128],
                    kfc[0:64, h * P:(h + 1) * P], kbf[0:64, h * P:(h + 1) * P],
                    start=True, stop=True),
                    reads=["s_kfc", "s_kbf"], writes=[f"pb{q}"], tag="mm_kkA",
                    group=g)
            for hh in range(4):
                h = q * 4 + hh
                self.ve(lambda e, h=h, hh=hh, q=q: e.tensor_tensor(
                    B0h[:, h * 128:(h + 1) * 128],
                    self.pb[q][:, hh * 128:(hh + 1) * 128], self.ntrius[:],
                    ALU.mult),
                    reads=[f"pb{q}", "ntrius"], writes=["s_Bh0"], tag="kkAe1")
        self.ve(lambda e, B0h=B0h: e.tensor_tensor(B0h[:], B0h[:], self.s_eAt[:],
                                                   ALU.mult),
                reads=["s_Bh0", "s_eAt"], writes=["s_Bh0"], tag="kkAe2")
        for q in range(4):
            g = self.nm("gk2")
            for hh in range(4):
                h = q * 4 + hh
                self.pe(lambda e, h=h, hh=hh, q=q: e.matmul(
                    self.pb[q][:, hh * 128:(hh + 1) * 128],
                    kbf[0:64, h * P:(h + 1) * P], kfc[0:64, h * P:(h + 1) * P],
                    start=True, stop=True),
                    reads=["s_kfc", "s_kbf"], writes=[f"pb{q}"], tag="mm_kkB",
                    group=g)
            for hh in range(4):
                h = q * 4 + hh
                self.ve(lambda e, h=h, hh=hh, q=q: e.tensor_tensor(
                    B0u[:, h * 128:(h + 1) * 128],
                    self.pb[q][:, hh * 128:(hh + 1) * 128], self.ntril[:],
                    ALU.mult),
                    reads=[f"pb{q}", "ntril"], writes=["s_Bu0"], tag="kkBe1")
        self.ve(lambda e, B0u=B0u: e.tensor_tensor(B0u[:], B0u[:], self.s_eBt[:],
                                                   ALU.mult),
                reads=["s_Bu0", "s_eBt"], writes=["s_Bu0"], tag="kkBe2")

    def _ab_compose(self, ch, Kt, U1, first):
        for grp in range(2):
            g = self.nm("gqc")
            for hh in range(8):
                h = grp * 8 + hh
                if not first:
                    self.pe(lambda e, h=h, hh=hh: e.matmul(
                        self.pb[hh // 2][0:64,
                                         (hh % 2) * 256:(hh % 2) * 256 + 128],
                        self.s_pcTb[0:64, h * 64:(h + 1) * 64],
                        self.s_Qrb[0:64, h * 128:(h + 1) * 128],
                        start=True, stop=False),
                        reads=["s_pcTb", "s_Qrb"], writes=[f"pb{hh//2}"],
                        tag="mm_pq", group=g)
                self.pe(lambda e, h=h, hh=hh, first=first: e.matmul(
                    self.pb[hh // 2][0:64, (hh % 2) * 256:(hh % 2) * 256 + 128],
                    Kt[:, h * 64:(h + 1) * 64], U1[:, h * P:(h + 1) * P],
                    start=first, stop=True),
                    reads=["s_Kt", "s_U1"], writes=[f"pb{hh//2}"], tag="mm_qc",
                    group=g)
            for hh in range(8):
                h = grp * 8 + hh
                self.ve(lambda e, h=h, hh=hh: e.tensor_copy(
                    self.s_Qr[0:64, h * 128:(h + 1) * 128],
                    self.pb[hh // 2][0:64, (hh % 2) * 256:(hh % 2) * 256 + 128]),
                    reads=[f"pb{hh//2}"], writes=["s_Qr"], tag="qcev")
        if first:
            self.ve(lambda e: e.tensor_copy(self.s_Pr[:], self.s_pcT[:]),
                    reads=["s_pcT"], writes=["s_Pr"], tag="prcp")
            self.ve(lambda e: e.tensor_copy(self.s_Pu[:], self.s_pcU[:]),
                    reads=["s_pcU"], writes=["s_Pu"], tag="pucp")
        else:
            for grp in range(2):
                g = self.nm("gpr")
                for hh in range(8):
                    h = grp * 8 + hh
                    self.pe(lambda e, h=h, hh=hh: e.matmul(
                        self.pb[hh // 2][0:64,
                                         (hh % 2) * 256:(hh % 2) * 256 + 64],
                        self.s_Pub[0:64, h * 64:(h + 1) * 64],
                        self.s_pcTb[0:64, h * 64:(h + 1) * 64],
                        start=True, stop=True),
                        reads=["s_Pub", "s_pcTb"], writes=[f"pb{hh//2}"],
                        tag="mm_prT", group=g)
                    self.pe(lambda e, h=h, hh=hh: e.matmul(
                        self.pb[hh // 2][0:64, (hh % 2) * 256 + 64:
                                         (hh % 2) * 256 + 128],
                        self.s_pcTb[0:64, h * 64:(h + 1) * 64],
                        self.s_Pub[0:64, h * 64:(h + 1) * 64],
                        start=True, stop=True),
                        reads=["s_Pub", "s_pcTb"], writes=[f"pb{hh//2}"],
                        tag="mm_prU", group=g)
                for hh in range(8):
                    h = grp * 8 + hh
                    base = (hh % 2) * 256
                    self.ve(lambda e, h=h, hh=hh, base=base: e.tensor_copy(
                        self.s_Pr[0:64, h * 64:(h + 1) * 64],
                        self.pb[hh // 2][0:64, base:base + 64]),
                        reads=[f"pb{hh//2}"], writes=["s_Pr"], tag="prev")
                    self.ve(lambda e, h=h, hh=hh, base=base: e.tensor_copy(
                        self.s_Pu[0:64, h * 64:(h + 1) * 64],
                        self.pb[hh // 2][0:64, base + 64:base + 128]),
                        reads=[f"pb{hh//2}"], writes=["s_Pu"], tag="puev2")
        self.ve(lambda e: e.tensor_copy(self.s_pcTb[:], self.s_pcT[:]),
                reads=["s_pcT"], writes=["s_pcTb"], tag="pctc")
        self.ve(lambda e: e.tensor_copy(self.s_Pub[:], self.s_Pu[:]),
                reads=["s_Pu"], writes=["s_Pub"], tag="pubc")
        self.ve(lambda e: e.tensor_copy(self.s_Qrb[:], self.s_Qr[:]),
                reads=["s_Qr"], writes=["s_Qrb"], tag="qrbc")

    def _ab_sin(self):
        NC = self.NC
        gth = self.sbP("bs_gth", [64, 3072], F32)
        pmr = self.sbP("bs_pmr", [1, NC], F32)
        self.dma(pmr[:], self.prevmask_hbm.rearrange("a b -> b a"), reads=[],
                 writes=["bs_pmr"], tag="bpmr")
        pmb = self.bcast_row(pmr[:], "bs_pmr", NC, "bs_pmb")
        omb = self.sbP("bs_omb", [P, NC], F32)
        self.ve(lambda e: e.tensor_scalar(omb[:], pmb[:], -1.0, 1.0, ALU.mult,
                                          ALU.add),
                reads=["bs_pmb"], writes=["bs_omb"], tag="bomb")
        Pt = self.sbP("bs_Pt", [64, 1024], BF16)
        Qt = self.sbP("bs_Qt", [64, 2048], F32)
        Qtb = self.sbP("bs_Qtb", [64, 2048], BF16)
        Sb = self.sbP("bs_Sb", [64, 2048], BF16)
        imI = self.sbP("bs_imI", [64, 1024], F32)
        Ptf = self.sbP("bs_Ptf", [64, 1024], F32)
        self.ve(lambda e: e.memset(self.ab_S[:], 0.0), reads=[], writes=["ab_S"],
                tag="bsz")
        for j in range(NC):
            self.dma(gth[:], self.ccb2_out[j * 64:(j + 1) * 64, :],
                     reads=["ccb2o", "bs_gth"], writes=["bs_gth"], tag="be3r")
            self.ve(lambda e, j=j: e.tensor_scalar(
                Ptf[:], gth[:, 0:1024], pmb[0:64, j:j + 1], None, ALU.mult),
                reads=["bs_gth", "bs_pmb"], writes=["bs_Ptf"], tag="bptm")
            self.ve(lambda e, j=j: e.tensor_scalar(
                imI[:], self.i64t[:], omb[0:64, j:j + 1], None, ALU.mult),
                reads=["i64t", "bs_omb"], writes=["bs_imI"], tag="bimi")
            self.ve(lambda e: e.tensor_tensor(Ptf[:], Ptf[:], imI[:], ALU.add),
                    reads=["bs_Ptf", "bs_imI"], writes=["bs_Ptf"], tag="bpta")
            self.ve(lambda e: e.tensor_copy(Pt[:], Ptf[:]),
                    reads=["bs_Ptf"], writes=["bs_Pt"], tag="bptc")
            self.ve(lambda e, j=j: e.tensor_scalar(
                Qt[:], gth[:, 1024:3072], pmb[0:64, j:j + 1], None, ALU.mult),
                reads=["bs_gth", "bs_pmb"], writes=["bs_Qt"], tag="bqtm")
            self.ve(lambda e: e.tensor_copy(Sb[:], self.ab_S[:]),
                    reads=["ab_S"], writes=["bs_Sb"], tag="bsbc")
            g = self.nm("gsin")
            for hh in range(16):
                self.pe(lambda e, hh=hh: e.matmul(
                    self.pb[hh // 4][0:64, (hh % 4) * 128:(hh % 4) * 128 + 128],
                    Pt[:, hh * 64:(hh + 1) * 64], Sb[:, hh * 128:(hh + 1) * 128],
                    start=True, stop=True),
                    reads=["bs_Pt", "bs_Sb"], writes=[f"pb{hh//4}"],
                    tag="mm_sin", group=g)
            for q in range(4):
                self.ve(lambda e, q=q: e.tensor_tensor(
                    self.ab_S[0:64, q * 512:(q + 1) * 512],
                    Qt[0:64, q * 512:(q + 1) * 512], self.pb[q][0:64, :],
                    ALU.add),
                    reads=[f"pb{q}", "bs_Qt", "ab_S"], writes=["ab_S"],
                    tag="bsup")

    def _ab_pass2(self):
        NT, Tc = self.NT, self.Tc
        suw = [self.sbP(f"p3_suw{i}", [P, 3072], BF16) for i in range(2)]
        gg = [self.sbP(f"p3_gg{i}", [P, 2048], BF16) for i in range(2)]
        qfc = self.sbP("p3_qfc", [64, 2048], BF16)
        kfc = self.sbP("p3_kfc", [64, 2048], BF16)
        ktc = self.sbP("p3_ktc", [P, 1024], BF16)
        Kt = self.sbP("p3_Kt", [P, 1024], BF16)
        Wh = self.sbP("p3_Wh", [64, 2048], BF16)
        Sb = self.sbP("p3_Sb", [P, 2048], BF16)
        Up = self.sbP("p3_Up", [P, 2048], BF16)
        Ps = self.sbP("p3_Ps", [P, 2048], BF16)
        O1 = self.sbP("p3_O1", [P, 2048], F32)
        ob = self.sbP("p3_ob", [P, 2048], F32)
        scr = self.sbP("p3_scr", [P, 2048], BF16)
        ssb = self.sbP("p3_ss", [P, 16], F32)
        tA = self.sbP("p3_tA", [64, 2048], F32)
        yb = self.sbP("p3_yb", [P, 2048], BF16)
        t2b = self.sbP("p3_t2b", [P, 2048], BF16)
        ystg = self.sbP("p3_ystg", [P, 512], BF16)
        e5 = self.fconst(LN_EPS)
        s128 = self.fconst(1.0 / 128.0)

        for ch in range(NT):
            b = ch % 2
            self.dma(suw[b][:], self.suw_hbm[ch * P:(ch + 1) * P, :],
                     reads=["suw_sp", f"p3_suw{b}"], writes=[f"p3_suw{b}"],
                     tag="p3suw")
            self.dma(gg[b][:], self.gg_hbm[ch * P:(ch + 1) * P, :],
                     reads=["gg_sp", f"p3_gg{b}"], writes=[f"p3_gg{b}"],
                     tag="p3gg")
            for h2 in range(2):
                self.dma(qfc[:].rearrange("p (d t) -> p d t", t=256)[
                    :, :, h2 * 128:(h2 + 1) * 128],
                    self.qf_hbm[:, h2 * 64:h2 * 64 + 64,
                                ch * P:(ch + 1) * P].rearrange("d p t -> p d t"),
                    reads=["qf_sp", "p3_qfc"], writes=["p3_qfc"], tag="p3qf")
                self.dma(kfc[:].rearrange("p (d t) -> p d t", t=256)[
                    :, :, h2 * 128:(h2 + 1) * 128],
                    self.kf_hbm[:, h2 * 64:h2 * 64 + 64,
                                ch * P:(ch + 1) * P].rearrange("d p t -> p d t"),
                    reads=["kf_sp", "p3_kfc"], writes=["p3_kfc"], tag="p3kf")
            self.dma(ktc[:], self.kt_hbm[ch * P:(ch + 1) * P, :],
                     reads=["kt_sp", "p3_ktc"], writes=["p3_ktc"], tag="p3kt")
            self.ve(lambda e, ch=ch: e.tensor_tensor(
                Kt[:].rearrange("p (h k) -> p h k", h=16),
                ktc[:].rearrange("p (h k) -> p h k", h=16),
                self.ab_Dbdec[:, ch * 16:(ch + 1) * 16].unsqueeze(2).broadcast_to(
                    (P, 16, 64)), ALU.mult),
                reads=["p3_ktc", "ab_Dbdec"], writes=["p3_Kt"], tag="p3Kt")
            self.ve(lambda e: e.tensor_copy(Sb[0:64, :], self.ab_S[:]),
                    reads=["ab_S"], writes=["p3_Sb"], tag="p3sbc")
            for i in range(16):
                sl = i % 2
                tp = self.tpb4[sl][:]
                self.pe(lambda e, tp=tp, i=i, b=b: e.transpose(
                    tp[0:64, 0:P],
                    suw[b][:, i * 192 + 128:i * 192 + 192],
                    self.id_bf[:]),
                    reads=[f"p3_suw{b}", "id_bf"], writes=[f"tpb4_{sl}"],
                    tag="p3WT")
                self.ve(lambda e, tp=tp, i=i, sl=sl: e.tensor_scalar(
                    Wh[0:64, i * 128:(i + 1) * 128], tp[0:64, 0:P], -1.0, None,
                    ALU.mult),
                    reads=[f"tpb4_{sl}"], writes=["p3_Wh"], tag="p3WTe")
            for grp in range(2):
                g = self.nm("gufix")
                for hh in range(8):
                    h = grp * 8 + hh
                    self.pe(lambda e, h=h, hh=hh: e.matmul(
                        self.pb[hh // 2][:, (hh % 2) * 256:(hh % 2) * 256 + 128],
                        Wh[0:64, h * 128:(h + 1) * 128],
                        Sb[0:64, h * 128:(h + 1) * 128],
                        start=True, stop=False),
                        reads=["p3_Wh", "p3_Sb"], writes=[f"pb{hh//2}"],
                        tag="mm_ws", group=g)
                    self.pe(lambda e, h=h, hh=hh, b=b: e.matmul(
                        self.pb[hh // 2][:, (hh % 2) * 256:(hh % 2) * 256 + 128],
                        self.id_bf[:],
                        suw[b][:, h * 192:h * 192 + 128],
                        start=False, stop=True),
                        reads=[f"p3_suw{b}", "id_bf"], writes=[f"pb{hh//2}"],
                        tag="mm_uv", group=g)
                for hh in range(8):
                    h = grp * 8 + hh
                    self.ve(lambda e, h=h, hh=hh: e.tensor_copy(
                        Up[:, h * 128:(h + 1) * 128],
                        self.pb[hh // 2][:, (hh % 2) * 256:(hh % 2) * 256 + 128]),
                        reads=[f"pb{hh//2}"], writes=["p3_Up"], tag="ufixev")
            self._ab_logD(ch, orientA=True, orientB=False)
            for q in range(4):
                g = self.nm("gps")
                for hh in range(4):
                    h = q * 4 + hh
                    self.pe(lambda e, h=h, hh=hh, q=q: e.matmul(
                        self.pb[q][:, hh * 128:(hh + 1) * 128],
                        kfc[0:64, h * P:(h + 1) * P],
                        qfc[0:64, h * P:(h + 1) * P],
                        start=True, stop=True),
                        reads=["p3_kfc", "p3_qfc"], writes=[f"pb{q}"],
                        tag="mm_ps", group=g)
                for hh in range(4):
                    h = q * 4 + hh
                    self.ve(lambda e, h=h, hh=hh, q=q: e.tensor_tensor(
                        Ps[:, h * 128:(h + 1) * 128],
                        self.pb[q][:, hh * 128:(hh + 1) * 128], self.triu_bf[:],
                        ALU.mult),
                        reads=[f"pb{q}", "triu_bf"], writes=["p3_Ps"], tag="psev")
            self.ve(lambda e: e.tensor_tensor(Ps[:], Ps[:], self.s_eAt[:],
                                              ALU.mult),
                    reads=["p3_Ps", "s_eAt"], writes=["p3_Ps"], tag="psev2")
            g = self.nm("gqs")
            for hh in range(16):
                h = hh
                self.pe(lambda e, h=h, hh=hh: e.matmul(
                    self.pb[hh // 4][:, (hh % 4) * 128:(hh % 4) * 128 + 128],
                    qfc[0:64, h * P:(h + 1) * P],
                    Sb[0:64, h * 128:(h + 1) * 128],
                    start=True, stop=True),
                    reads=["p3_qfc", "p3_Sb"], writes=[f"pb{hh//4}"],
                    tag="mm_qs", group=g)
            for q in range(4):
                self.ve(lambda e, q=q, ch=ch: e.tensor_tensor(
                    O1[:, q * 512:(q + 1) * 512].rearrange(
                        "p (h v) -> p h v", v=128),
                    self.pb[q][:].rearrange("p (h v) -> p h v", v=128),
                    self.ab_LamT[:, ch * 16 + q * 4:ch * 16 + (q + 1) * 4
                                 ].unsqueeze(2).broadcast_to((P, 4, 128)),
                    ALU.mult),
                    reads=[f"pb{q}", "ab_LamT"], writes=["p3_O1"], tag="o1ev")
            g = self.nm("gpu2")
            for hh in range(16):
                h = hh
                self.pe(lambda e, h=h, hh=hh: e.matmul(
                    self.pb[hh // 4][:, (hh % 4) * 128:(hh % 4) * 128 + 128],
                    Ps[:, h * 128:(h + 1) * 128], Up[:, h * 128:(h + 1) * 128],
                    start=True, stop=True),
                    reads=["p3_Ps", "p3_Up"], writes=[f"pb{hh//4}"],
                    tag="mm_psu", group=g)
            for q in range(4):
                self.ve(lambda e, q=q: e.tensor_tensor(
                    ob[:, q * 512:(q + 1) * 512], O1[:, q * 512:(q + 1) * 512],
                    self.pb[q][:], ALU.add),
                    reads=[f"pb{q}", "p3_O1"], writes=["p3_ob"], tag="oev")
            self.ve(lambda e, ch=ch: e.tensor_tensor(
                tA[:].rearrange("p (h v) -> p h v", h=16),
                self.ab_S[0:64, :].rearrange("p (h v) -> p h v", h=16),
                self.ab_eG[0:64, ch * 16:(ch + 1) * 16].unsqueeze(2).broadcast_to(
                    (64, 16, 128)), ALU.mult),
                reads=["ab_S", "ab_eG"], writes=["p3_tA"], tag="p3smul")
            g = self.nm("gsu")
            for hh in range(16):
                h = hh
                self.pe(lambda e, h=h, hh=hh: e.matmul(
                    self.pb[hh // 4][0:64, (hh % 4) * 128:(hh % 4) * 128 + 128],
                    Kt[:, h * 64:(h + 1) * 64], Up[:, h * 128:(h + 1) * 128],
                    start=True, stop=True),
                    reads=["p3_Kt", "p3_Up"], writes=[f"pb{hh//4}"],
                    tag="mm_su", group=g)
            for q in range(4):
                self.ve(lambda e, q=q: e.tensor_tensor(
                    self.ab_S[0:64, q * 512:(q + 1) * 512],
                    tA[0:64, q * 512:(q + 1) * 512], self.pb[q][0:64, :],
                    ALU.add),
                    reads=[f"pb{q}", "p3_tA", "ab_S"], writes=["ab_S"],
                    tag="p3sadd")
            self.se(lambda e: e.activation(scr[:], ob[:], AF.Square, bias=0.0,
                                           scale=1.0),
                    reads=["p3_ob"], writes=["p3_scr"], tag="p3sq")
            self.ve(lambda e: e.tensor_reduce(
                ssb[:].rearrange("p (h o) -> p h o", o=1),
                scr[:].rearrange("p (h v) -> p h v", v=128), AX.X, ALU.add),
                reads=["p3_scr"], writes=["p3_ss"], tag="p3red")
            self.se(lambda e: e.activation(ssb[:], ssb[:], AF.Ln, bias=e5[:],
                                           scale=s128[:]),
                    reads=["p3_ss", e5.name, s128.name], writes=["p3_ss"],
                    tag="p3ln")
            self.se(lambda e: e.activation(ssb[:], ssb[:], AF.Exp, bias=0.0,
                                           scale=-0.5),
                    reads=["p3_ss"], writes=["p3_ss"], tag="p3ex")
            self.ve(lambda e: e.tensor_tensor(
                ob[:].rearrange("p (h v) -> p h v", v=128),
                ob[:].rearrange("p (h v) -> p h v", v=128),
                ssb[:].unsqueeze(2).broadcast_to((P, 16, 128)), ALU.mult),
                reads=["p3_ob", "p3_ss"], writes=["p3_ob"], tag="p3rms")
            self.ve(lambda e, b=b: e.tensor_tensor(
                t2b[:].rearrange("p (h v) -> p h v", v=128),
                gg[b][:].rearrange("p (h v) -> p h v", v=128),
                self.bnorm[:].unsqueeze(1).broadcast_to((P, 16, 128)), ALU.mult),
                reads=[f"p3_gg{b}", "bnorm_b"], writes=["p3_t2b"], tag="p3gn")
            self.ve(lambda e: e.tensor_tensor(yb[:], ob[:], t2b[:], ALU.mult),
                    reads=["p3_ob", "p3_t2b"], writes=["p3_yb"], tag="p3y")
            for k0 in range(0, 16, 4):
                for i in range(4):
                    tpb = self.tpb4[i % 2][:]
                    self.pe(lambda e, tpb=tpb, kk=k0 + i: e.transpose(
                        tpb[:], yb[:, kk * P:(kk + 1) * P], self.id_bf[:]),
                        reads=["p3_yb", "id_bf"], writes=[f"tpb4_{i % 2}"],
                        tag="p3yT")
                    self.ve(lambda e, tpb=tpb, i=i: e.tensor_copy(
                        ystg[:, i * P:(i + 1) * P], tpb),
                        reads=[f"tpb4_{i % 2}"], writes=["p3_ystg"], tag="p3yTe")
                for i in range(4):
                    self.dma(self.yt_hbm[k0 + i, :, ch * P:(ch + 1) * P],
                             ystg[:, i * P:(i + 1) * P],
                             reads=["p3_ystg"], writes=["yt_sp"], q="gpsimd",
                             tag="p3ysp")
        oW = [self.sbP(f"p3_ow{i}", [P, 512], BF16) for i in range(2)]
        q9b = [self.sbP(f"p3_yt{i}", [P, 512], BF16) for i in range(16)]
        nw = 0
        for tg in range((NT + 3) // 4):
            tiles = [tg * 4 + i for i in range(4) if tg * 4 + i < NT]
            w = len(tiles) * P
            for kk in range(16):
                self.dma(q9b[kk][:, 0:w],
                         self.yt_hbm[kk, :, tiles[0] * P:tiles[0] * P + w],
                         reads=[f"p3ytu{kk}", "yt_sp"], writes=[f"p3_yt{kk}"],
                         tag="p3ytl")
            for nn in range(2):
                for kk in range(16):
                    bw = nw % 2
                    nw += 1
                    self.dma(oW[bw][:], self.wout_hbm[kk, :,
                                                      nn * 512:(nn + 1) * 512],
                             reads=[f"p3owu{bw}", "wout_sp"],
                             writes=[f"p3_ow{bw}"], tag="p3owl")
                    for i, t in enumerate(tiles):
                        self.pe(lambda e, bw=bw, kk=kk, i=i: e.matmul(
                            self.pb[i][:], q9b[kk][:, i * P:(i + 1) * P],
                            oW[bw][:], start=(kk == 0), stop=(kk == 15)),
                            reads=[f"p3_ow{bw}", f"p3_yt{kk}"],
                            writes=[f"pb{i}"] + ([f"p3owu{bw}"]
                                                 if i == len(tiles) - 1 else []),
                            tag="mm_wo", group=f"gwo{tg}_{nn}_{i}")
                for i, t in enumerate(tiles):
                    self.ve(lambda e, i=i, t=t, nn=nn: e.tensor_tensor(
                        self.xres[t][:, nn * 512:(nn + 1) * 512],
                        self.xres[t][:, nn * 512:(nn + 1) * 512],
                        self.pb[i][:], ALU.add),
                        reads=[f"pb{i}", f"xres{t}"],
                        writes=[f"xres{t}"] + ([f"p3ytu{kk2}" for kk2 in
                                                range(16)]
                                               if nn == 1 and i == len(tiles) - 1
                                               else []),
                        tag="wo_ev")



_BUILD_CACHE = {}


def get_builder(Tc, n_cores, flags=(True, True, True, True)):
    key = (Tc, n_cores, flags)
    if key not in _BUILD_CACHE:
        b = Builder(Tc, n_cores, *flags)
        nc = b.build()
        _BUILD_CACHE[key] = (b, nc)
    return _BUILD_CACHE[key]


WMAP = {
    "aw_norm": "L0_attn_a_norm_w", "aw_inproj": "L0_attn_a_in_proj_w",
    "aw_convw": "L0_attn_a_conv_w", "aw_convb": "L0_attn_a_conv_b",
    "aw_Alog": "L0_attn_a_A_log", "aw_Dp": "L0_attn_a_Dp",
    "aw_dtb": "L0_attn_a_dt_bias", "aw_onorm": "L0_attn_a_out_norm_w",
    "aw_outproj": "L0_attn_a_out_proj_w",
    "bw_q": "L0_attn_b_w_q", "bw_k": "L0_attn_b_w_k", "bw_v": "L0_attn_b_w_v",
    "bw_a": "L0_attn_b_w_a", "bw_b": "L0_attn_b_w_b", "bw_g": "L0_attn_b_w_g",
    "bw_out": "L0_attn_b_w_out", "bw_Alog": "L0_attn_b_A_log",
    "bw_dtb": "L0_attn_b_dt_bias", "bw_convq": "L0_attn_b_conv_q",
    "bw_convk": "L0_attn_b_conv_k", "bw_convv": "L0_attn_b_conv_v",
    "bw_norm": "L0_attn_b_norm_w",
    "fw_r": "L0_ffn_W_r", "fw_k": "L0_ffn_W_k", "fw_v": "L0_ffn_W_v",
    "fw_mur": "L0_ffn_mu_r", "fw_muk": "L0_ffn_mu_k",
}


def make_in_maps(b, x, inputs):
    T, Dm = x.shape
    n = b.NC
    Tc = b.Tc
    in_maps = []
    for c in range(n):
        m = {}
        m["x_in"] = np.ascontiguousarray(x[c * Tc:(c + 1) * Tc])
        m["x_halo"] = (np.zeros((3, Dm), np.float32) if c == 0
                       else np.ascontiguousarray(x[c * Tc - 3:c * Tc]))
        pm = np.zeros((n, 1), np.float32)
        pm[:c, 0] = 1.0
        ps = np.zeros((n, 1), np.float32)
        if c > 0:
            ps[c - 1, 0] = 1.0
        m["prevmask"] = pm
        m["prevsel"] = ps
        p3 = np.zeros((n * 3, 3), np.float32)
        if c > 0:
            for r in range(3):
                p3[(c - 1) * 3 + r, r] = 1.0
        m["prevsel3"] = p3
        for k, v in WMAP.items():
            m[k] = np.ascontiguousarray(inputs[v], np.float32)
        for k, v in b.hostc.items():
            m[k] = v
        valid = set(b.inputs) | set(b.hostc)
        m = {k: v for k, v in m.items() if k in valid}
        in_maps.append(m)
    return in_maps


def kernel(**inputs):
    x = np.ascontiguousarray(inputs["L0_attn_a_hidden_states"], np.float32)
    B, T, Dm = x.shape
    n = 8
    b, nc = get_builder(T // n, n)
    in_maps = make_in_maps(b, x.reshape(T, Dm), inputs)
    res = run_bass_kernel_spmd(nc, in_maps, core_ids=list(range(n)))
    out = np.concatenate([res.results[c]["x_out"] for c in range(n)], axis=0)
    return out.reshape(B, T, Dm)
